# version 01
# baseline (speedup 1.0000x reference)
import contextlib
import numpy as np
import ml_dtypes
import concourse.bass as bass
import concourse.mybir as mybir
from concourse.bass_utils import run_bass_kernel_spmd

F32 = mybir.dt.float32
BF16 = mybir.dt.bfloat16
AF = mybir.ActivationFunctionType
ALU = mybir.AluOpType
AX = mybir.AxisListType

T = 2048
D = 1024
NT = T // 128
ND = D // 128
H = 16
FF = 4096
NE = 8
LCH = 128
SCN = 128
NSC = T // SCN
CPS = SCN // LCH
GN_EPS = 1e-5 * 64.0

ENGINES = ["tensor", "vector", "scalar", "gpsimd", "sync"]

DEBUG_TAPS = ()
UPTO = 99
SIM_SAFE = False
CHAIN_ENGINES = ("vector", "scalar", "gpsimd")
QKV_SUB = 99
QKV_NW = 2
QKV_NDO = 8


def _bf(x):
    return np.ascontiguousarray(np.asarray(x).astype(ml_dtypes.bfloat16))



class Sem:
    def __init__(self, handle, name):
        self.h = handle
        self.name = name
        self.n = 0

    def inc(self, k=1):
        self.n += k
        return self.n


class Ph:
    def __init__(self, p, name):
        self.p = p
        self.name = name
        self.items = {e: [] for e in ENGINES}
        self._laststream = {}

    def _sem(self, eng, dma):
        return self.p.sem(("d_" if dma else "e_") + eng)

    def op(self, eng, fn, after=(), token=False, force=False):
        sem = self._sem(eng, False)
        tok = None
        if token or SIM_SAFE or eng in CHAIN_ENGINES:
            tok = (sem, sem.n + 1, eng, False)
            sem.inc(1)
        self.items[eng].append(
            (fn, tuple(a for a in after if a), tok, False, force))
        return tok

    def tok_last(self, eng):
        items = self.items[eng]
        fn, after, tok0, isdma, force = items[-1]
        if tok0 is not None:
            return tok0
        sem = self._sem(eng, False)
        tok = (sem, sem.n + 1, eng, False)
        sem.inc(1)
        items[-1] = (fn, after, tok, isdma, force)
        return tok

    def dma(self, eng, out, in_, after=(), transpose=False, stream="m"):
        sem = self.p.sem("d_" + eng + "_" + stream)
        if SIM_SAFE:
            after = tuple(after) + (self._laststream.get(sem.name),)
        tok = (sem, sem.n + 16, eng, True)
        sem.inc(16)
        self._laststream[sem.name] = tok

        def fn(e, out=out, in_=in_, tr=transpose):
            if tr:
                return e.dma_start(out, in_, transpose=True)
            return e.dma_start(out, in_)

        self.items[eng].append(
            (fn, tuple(a for a in after if a), tok, True, False))
        return tok

    def emit(self):
        nc = self.p.nc
        active = [e for e in ENGINES if self.items[e]]
        with nc.Block(self.name) as blk:
            for eng in active:
                items = self.items[eng]

                def body(e, eng=eng, items=items):
                    water = {}
                    last_dma = {}
                    prev_tok = None
                    chain = SIM_SAFE or eng in CHAIN_ENGINES
                    for fn, after, tok, isdma, force in items:
                        if chain and prev_tok is not None:
                            e.wait_ge(prev_tok[0].h, prev_tok[1])
                        need = {}
                        for (s, v, peng, pdma) in after:
                            if peng == eng and not pdma and not force:
                                continue
                            if water.get(s.name, -1) >= v:
                                continue
                            if need.get(s.name, (None, -1))[1] < v:
                                need[s.name] = (s, v)
                        for s, v in need.values():
                            e.wait_ge(s.h, v)
                            water[s.name] = v
                        inst = fn(e)
                        if tok is not None:
                            inst.then_inc(tok[0].h, 16 if isdma else 1)
                            if isdma:
                                last_dma[tok[0].name] = tok
                            else:
                                prev_tok = tok
                    for s, v, _, _ in last_dma.values():
                        if water.get(s.name, -1) < v:
                            e.wait_ge(s.h, v)
                            water[s.name] = v

                getattr(blk, eng)(body)


class P:
    def __init__(self, nc, stack):
        self.nc = nc
        self.stack = stack
        self.sems = {}
        self.dram = {}
        self.taps = {}

    def sem(self, name):
        if name not in self.sems:
            self.sems[name] = Sem(
                self.stack.enter_context(self.nc.semaphore(name)), name)
        return self.sems[name]

    def sb(self, name, shape, dtype, stack=None):
        return (stack or self.stack).enter_context(
            self.nc.sbuf_tensor(name, shape, dtype))

    def ps(self, name, shape, stack=None, dtype=F32):
        return (stack or self.stack).enter_context(
            self.nc.psum_tensor(name, shape, dtype))

    def tap(self, name, shape, dtype=F32):
        if name in DEBUG_TAPS and name not in self.taps:
            self.taps[name] = self.nc.dram_tensor(
                "tap_" + name, list(shape), dtype, kind="ExternalOutput").ap()
        return self.taps.get(name)



def make_consts():
    c = {}
    c["ident_f32"] = np.eye(128, dtype=np.float32)
    c["ident_bf16"] = _bf(np.eye(128))
    c["ones_1x128"] = np.ones((1, 128), dtype=np.float32)
    c["onescol_bf"] = _bf(np.ones((128, 1)))
    for off in range(4):
        pk = off * 128 + np.arange(128)[:, None]
        fq = np.arange(512)[None, :]
        c[f"cmask{off}"] = _bf((pk <= fq).astype(np.float32))
    tt = np.arange(LCH)
    c["tri_incl"] = (tt[:, None] <= tt[None, :]).astype(np.float32)
    c["mask_su"] = (tt[:, None] < tt[None, :]).astype(np.float32)
    c["mask_iu"] = (tt[:, None] <= tt[None, :]).astype(np.float32)
    c["mask_sl"] = (tt[:, None] > tt[None, :]).astype(np.float32)
    blk = np.zeros((ND, 128, H), dtype=np.float32)
    for dc in range(ND):
        for j in range(2):
            blk[dc, j * 64:(j + 1) * 64, 2 * dc + j] = 1.0
    c["headblk_bf"] = _bf(blk.transpose(1, 0, 2).reshape(128, ND * H))
    c["fvals"] = np.tile(np.array(FCONSTS, dtype=np.float32), (128, 1))
    return c


FCONSTS = [1e-5, GN_EPS, 1.0, -1.0, 0.5, -0.5, 1e12, -1e30, float(D), 2.0,
           -2.0, 1.0 / 32.0, -1.0 / 32.0, 0.0, 3.0, -3e38]


def prep_inputs(inputs):
    g = {}
    x = np.asarray(inputs["L0_attn_x"], dtype=np.float32)
    percore = [np.ascontiguousarray(x[b]) for b in range(8)]

    def A(name, arr):
        g[name] = np.ascontiguousarray(np.asarray(arr, dtype=np.float32))

    for w in ["Wq", "Wk", "Wv"]:
        W = np.asarray(inputs[f"L0_ffn_{w}"], dtype=np.float32)
        A(f"l0_{w.lower()}", W.transpose(1, 0, 2).reshape(D, D))
    A("l0_gate", np.asarray(inputs["L0_ffn_gate_W"], dtype=np.float32)
      .reshape(ND, 128, NE).transpose(1, 0, 2).reshape(128, ND * NE))
    A("l0_w1", inputs["L0_ffn_W1"])
    A("l0_w2", inputs["L0_ffn_W2"])
    b1 = np.asarray(inputs["L0_ffn_b1"], dtype=np.float32)
    A("l0_b1", b1.reshape(NE, 32, 128).transpose(2, 0, 1).reshape(128, NE * 32))
    A("l0_b2", inputs["L0_ffn_b2"])
    for v in ["ln1_g", "ln1_b", "ln2_g", "ln2_b"]:
        A(f"l0_{v}", np.asarray(inputs[f"L0_ffn_{v}"]).reshape(1, D))

    def chvec(name, arr):
        v = np.asarray(arr, dtype=np.float32).reshape(D)
        g[name] = np.ascontiguousarray(v.reshape(ND, 128).T)

    for v in ["time_maa_x", "time_maa_r", "time_maa_w", "time_maa_k",
              "time_maa_v", "time_maa_a", "time_maa_g", "time_decay",
              "time_aaaaa"]:
        chvec("l1_" + v, inputs["L1_attn_" + v])
    chvec("l1_faaaa", np.asarray(inputs["L1_attn_time_faaaa"]).reshape(D))
    A("l1_maa_w1", inputs["L1_attn_time_maa_w1"])
    A("l1_maa_w2", np.asarray(inputs["L1_attn_time_maa_w2"], dtype=np.float32)
      .reshape(3, 2, 32, D).transpose(1, 2, 0, 3).reshape(64, 3 * D))
    A("l1_decay_w1", inputs["L1_attn_time_decay_w1"])
    A("l1_decay_w2", inputs["L1_attn_time_decay_w2"])
    A("l1_aaa_w1", inputs["L1_attn_time_aaa_w1"])
    A("l1_aaa_w2", inputs["L1_attn_time_aaa_w2"])
    A("l1_kkk_w1", inputs["L1_attn_time_kkk_w1"])
    A("l1_kkk_w2", inputs["L1_attn_time_kkk_w2"])
    A("l1_gate_w1", inputs["L1_attn_gate_w1"])
    A("l1_gate_w2", inputs["L1_attn_gate_w2"])
    for w in ["W_r", "W_k", "W_v", "W_o"]:
        A("l1_" + w.lower(), inputs["L1_attn_" + w])
    A("l1_ln_w", np.asarray(inputs["L1_attn_ln_w"]).reshape(1, D))
    A("l1_ln_b", np.asarray(inputs["L1_attn_ln_b"]).reshape(1, D))
    A("l1_gate", np.asarray(inputs["L1_ffn_w_router"], dtype=np.float32)
      .reshape(ND, 128, NE).transpose(1, 0, 2).reshape(128, ND * NE))
    A("l1_w1", inputs["L1_ffn_w1"])
    A("l1_w2", inputs["L1_ffn_w2"])
    b1 = np.asarray(inputs["L1_ffn_b1"], dtype=np.float32)
    A("l1_b1", b1.reshape(NE, 32, 128).transpose(2, 0, 1).reshape(128, NE * 32))
    A("l1_b2", inputs["L1_ffn_b2"])
    return g, percore



def build_program(upto=None):
    if upto is None:
        upto = UPTO
    nc = bass.Bass("TRN2", target_bir_lowering=False)
    stack = contextlib.ExitStack()
    p = P(nc, stack)
    dram = p.dram

    def din(name, shape, dtype=F32):
        dram[name] = nc.dram_tensor(name, shape, dtype, kind="ExternalInput").ap()

    din("x", [T, D])
    for w in ["l0_wq", "l0_wk", "l0_wv"]:
        din(w, [D, D])
    din("l0_gate", [128, ND * NE])
    din("l0_w1", [NE, D, FF])
    din("l0_w2", [NE, FF, D])
    din("l0_b1", [128, NE * 32])
    din("l0_b2", [NE, D])
    for v in ["ln1_g", "ln1_b", "ln2_g", "ln2_b"]:
        din("l0_" + v, [1, D])
    for v in ["time_maa_x", "time_maa_r", "time_maa_w", "time_maa_k",
              "time_maa_v", "time_maa_a", "time_maa_g", "time_decay",
              "time_aaaaa", "faaaa"]:
        din("l1_" + v, [128, ND])
    din("l1_maa_w1", [D, 192])
    din("l1_maa_w2", [64, 3 * D])
    din("l1_decay_w1", [D, 64])
    din("l1_decay_w2", [64, D])
    din("l1_aaa_w1", [D, 64])
    din("l1_aaa_w2", [64, D])
    din("l1_kkk_w1", [D, 64])
    din("l1_kkk_w2", [64, D])
    din("l1_gate_w1", [D, 128])
    din("l1_gate_w2", [128, D])
    for w in ["l1_w_r", "l1_w_k", "l1_w_v", "l1_w_o"]:
        din(w, [D, D])
    din("l1_ln_w", [1, D])
    din("l1_ln_b", [1, D])
    din("l1_gate", [128, ND * NE])
    din("l1_w1", [NE, D, FF])
    din("l1_w2", [NE, FF, D])
    din("l1_b1", [128, NE * 32])
    din("l1_b2", [NE, D])
    cn = make_consts()
    for k, v in cn.items():
        din("c_" + k, list(v.shape), BF16 if v.dtype == ml_dtypes.bfloat16 else F32)

    dram["out"] = nc.dram_tensor("out", [T, D], F32, kind="ExternalOutput").ap()

    def dscr(name, shape, dtype=F32):
        dram[name] = nc.dram_tensor(name, shape, dtype).ap()

    dscr("xn0_f", [T, D])
    dscr("xn0_b", [T, D], BF16)
    dscr("y0_f", [T, D])
    dscr("y0_b", [T, D], BF16)
    dscr("base0", [T, D])
    dscr("x1_f", [T, D])
    dscr("xn1_b", [T, D], BF16)
    dscr("xn1_ch", [D, T + 4])
    dscr("att1", [T, D])

    g = {}
    g["stats"] = p.sb("stats", [128, 8], F32)
    g["idf"] = p.sb("idf", [128, 128], F32)
    g["idb"] = p.sb("idb", [128, 128], BF16)
    g["ones1"] = p.sb("ones1", [1, 128], F32)
    g["onescol_bf"] = p.sb("onescol_bf", [128, 1], BF16)
    g["tri_incl"] = p.sb("tri_incl", [128, 128], F32)
    g["mask_su"] = p.sb("mask_su", [128, 128], F32)
    g["mask_iu"] = p.sb("mask_iu", [128, 128], F32)
    g["mask_sl"] = p.sb("mask_sl", [128, 128], F32)
    g["fvals"] = p.sb("fvals", [128, len(FCONSTS)], F32)
    for i, v in enumerate(FCONSTS):
        nc.const_aps.register(F32, v, g["fvals"][:, i:i + 1])

    ph = Ph(p, "consts")
    for nm, dst in [("c_ident_f32", g["idf"]), ("c_ident_bf16", g["idb"]),
                    ("c_ones_1x128", g["ones1"]),
                    ("c_onescol_bf", g["onescol_bf"]),
                    ("c_tri_incl", g["tri_incl"]), ("c_mask_su", g["mask_su"]),
                    ("c_mask_iu", g["mask_iu"]), ("c_mask_sl", g["mask_sl"]),
                    ("c_fvals", g["fvals"])]:
        ph.dma("sync", dst[:], dram[nm])
    ph.emit()

    if not globals().get('SKIP_LN0', False):
        phase_ln0(p, g)
    with contextlib.ExitStack() as qkv_stack:
        if upto >= 2:
            phase_qkv(p, g, qkv_stack)
        if upto >= 3:
            phase_mha(p, g)
    if upto >= 4:
        phase_attn0(p, g)
    if upto >= 5:
        phase_moe(p, g, layer=0)
    if upto >= 6:
        phase_rwkv(p, g)
    if upto >= 7:
        phase_moe(p, g, layer=1)

    if upto < 7:
        srcs = {1: "xn0_f", 2: "xn0_f", 3: "y0_f", 4: "base0", 5: "x1_f",
                6: "att1"}
        _copy_out(p, srcs.get(upto, "x1_f"))

    stack.close()
    return nc


def _copy_out(p, src):
    ph = Ph(p, "fin")
    with contextlib.ExitStack() as s_:
        buf = [p.sb(f"fin{i}", [128, D], F32, s_) for i in range(2)]
        toks = [None, None]
        for i in range(NT):
            t = ph.dma("sync", buf[i % 2][:],
                       p.dram[src][i * 128:(i + 1) * 128, :],
                       after=(toks[i % 2],), stream=f"f{i % 2}")
            toks[i % 2] = ph.dma("sync", p.dram["out"][i * 128:(i + 1) * 128, :],
                                 buf[i % 2][:], after=(t,), stream=f"f{i % 2}")
        ph.emit()



def bcast_row(ph, p, g, src_1xD, dst_128xD, ps2, after=(), guard=(None, None)):
    toks = []
    for fo in range(D // 512):
        tk = ph.op("tensor", lambda e, fo=fo: e.matmul(
            ps2[fo % 2][:, 0:512], g["ones1"][:],
            src_1xD[0:1, fo * 512:(fo + 1) * 512],
            start=True, stop=True), after=tuple(after) + (guard[fo % 2],),
            token=True)
        toks.append(ph.op("vector", lambda e, fo=fo: e.tensor_copy(
            dst_128xD[:, fo * 512:(fo + 1) * 512], ps2[fo % 2][:, 0:512]),
            after=(tk,), token=True))
    return toks


def ln_stats(ph, mv, x_ap, rstd, eps, extra_after=()):
    tb1 = ph.op("vector", lambda e: e.bn_stats(mv[:, 0:6], x_ap[:, 0:512]),
                after=extra_after, token=True)
    tb2 = ph.op("vector", lambda e: e.bn_stats(mv[:, 6:12], x_ap[:, 512:1024]),
                token=True)
    ts_ = ph.op("vector", lambda e: e.bn_aggr(mv[:, 12:14], mv[:, 0:12]),
                after=(tb1, tb2), force=True, token=True)
    ph.op("scalar", lambda e: e.activation(rstd[:], mv[:, 13:14], AF.Ln,
                                           bias=eps), after=(ts_,))
    tr = ph.op("scalar", lambda e: e.activation(rstd[:], rstd[:], AF.Exp,
                                                scale=-0.5), token=True)
    return ts_, tr



def phase_ln0(p, g):
    dram = p.dram
    ph = Ph(p, "ln0")
    with contextlib.ExitStack() as s_:
        xa = [p.sb(f"a_x{i}", [128, D], F32, s_) for i in range(2)]
        xn = [p.sb(f"a_xn{i}", [128, D], F32, s_) for i in range(2)]
        xnb = [p.sb(f"a_xnb{i}", [128, D], BF16, s_) for i in range(2)]
        mv = [p.sb(f"a_mv{i}", [128, 16], F32, s_) for i in range(2)]
        rs = [p.sb(f"a_rs{i}", [128, 1], F32, s_) for i in range(2)]
        n2st = p.sb("a_n2", [128, NT], F32, s_)
        gpad = p.sb("a_gpad", [128, 128], F32, s_)
        cmax = p.sb("a_cmax", [1, 1], F32, s_)
        pt = p.ps("a_pt", [128, 128], s_)
        pb = p.ps("a_pb", [128, 1], s_)

        tap_xn = p.tap("xn0", [T, D])
        ph.op("vector", lambda e: e.memset(gpad[:], -3e38))
        done = [None, None]
        vtok = [None, None]
        for i in range(NT):
            b = i % 2
            tl = ph.dma("sync", xa[b][:], dram["x"][i * 128:(i + 1) * 128, :],
                        after=(vtok[b],), stream=f"x{b}")
            tsa, tr = ln_stats(ph, mv[b], xa[b], rs[b], 1e-5,
                               extra_after=(tl, done[b]))
            ph.op("vector", lambda e, b=b: e.tensor_scalar(
                xn[b][:], xa[b][:], mv[b][:, 12:13], rs[b][:],
                op0=ALU.subtract, op1=ALU.mult), after=(tr, tsa), force=True)
            ph.op("vector", lambda e, b=b: e.tensor_copy(xnb[b][:], xn[b][:]))
            ph.op("vector", lambda e, b=b: e.tensor_tensor(
                mv[b][:, 14:15], rs[b][:], rs[b][:], op=ALU.mult))
            ph.op("vector", lambda e, b=b: e.tensor_tensor(
                mv[b][:, 15:16], mv[b][:, 13:14], mv[b][:, 14:15], op=ALU.mult))
            vtok[b] = ph.op("vector", lambda e, b=b, i=i: e.tensor_scalar(
                n2st[:, i:i + 1], mv[b][:, 15:16], float(D), None,
                op0=ALU.mult), token=True)
            ph.dma("gpsimd", dram["xn0_f"][i * 128:(i + 1) * 128, :],
                   xn[b][:], after=(vtok[b],), stream=f"s{b}")
            done[b] = ph.dma("gpsimd", dram["xn0_b"][i * 128:(i + 1) * 128, :],
                             xnb[b][:], after=(vtok[b],), stream=f"s{b}")
            if tap_xn is not None:
                done[b] = ph.dma("gpsimd", tap_xn[i * 128:(i + 1) * 128, :],
                                 xn[b][:], after=(vtok[b],), stream=f"s{b}")
        tm1 = ph.op("vector", lambda e: e.tensor_reduce(
            gpad[:, 0:1], n2st[:], axis=AX.X, op=ALU.max),
            after=tuple(vtok), token=True)
        tt = ph.op("tensor", lambda e: e.transpose(pt[:], gpad[:], g["idf"][:]),
                   after=(tm1,), token=True)
        tv = ph.op("vector", lambda e: e.tensor_reduce(
            cmax[:], pt[0:1, :], axis=AX.X, op=ALU.max), after=(tt,),
            token=True)
        tb = ph.op("tensor", lambda e: e.matmul(
            pb[:], g["ones1"][:], cmax[:], start=True, stop=True),
            after=(tv,), token=True)
        ph.op("scalar", lambda e: e.mul(g["stats"][:, 0:1], pb[:], -1.0),
              after=(tb,))
        ph.emit()



def phase_qkv(p, g, qkv_stack):
    dram = p.dram
    g["qT"] = p.sb("qT", [128, ND * T], BF16, qkv_stack)
    g["kT"] = p.sb("kT", [128, ND * T], BF16, qkv_stack)
    g["vtok"] = p.sb("vtok", [128, NT * D], BF16, qkv_stack)

    ph = Ph(p, "qkv")
    with contextlib.ExitStack() as s_:
        xnch = p.sb("b_xnch", [128, ND * T], BF16, s_)
        wbf = [p.sb(f"b_w{i}", [128, ND * D], BF16, s_) for i in range(2)]
        hb = p.sb("b_hb", [128, ND * H], BF16, s_)
        qsq = p.sb("b_qsq", [128, T], BF16, s_)
        nrm = p.sb("b_nrm", [128, 2 * NT * H], F32, s_)
        gpad = p.sb("b_gpad", [128, 128], F32, s_)
        cm = p.sb("b_cm", [1, 4], F32, s_)
        ps = [p.ps(f"b_ps{i}", [128, 512], s_) for i in range(2)]
        pn = [p.ps(f"b_pn{i}", [128, H], s_) for i in range(2)]
        pt = p.ps("b_pt", [128, 128], s_)
        pb = p.ps("b_pb", [128, 1], s_)

        xt = []
        for dc in range(ND):
            xt.append(ph.dma("sync", xnch[:, dc * T:(dc + 1) * T],
                             dram["xn0_b"][:, dc * 128:(dc + 1) * 128],
                             transpose=True, stream="xnch"))
        hbt = ph.dma("sync", hb[:], dram["c_headblk_bf"], stream="xnch")

        if QKV_SUB < 1:
            ph.emit()
            return
        wtok = {}
        consumed = [None, None]
        for wi, wname in enumerate(["l0_wq", "l0_wk"]):
            for dc in range(ND):
                wtok[(wi, dc)] = ph.dma(
                    "gpsimd", wbf[wi % 2][:, dc * D:(dc + 1) * D],
                    dram[wname][dc * 128:(dc + 1) * 128, :],
                    stream=f"w{wi % 2}")
        if QKV_SUB < 2:
            ph.emit()
            return

        evtok = [None, None]
        for wi in range(QKV_NW):
            dst = g["qT"] if wi == 0 else g["kT"]
            lastmm = None
            for do in range(QKV_NDO):
                for tq in range(4):
                    k = do * 4 + tq
                    pp = ps[k % 2]
                    aft = tuple(wtok[(wi, dc)] for dc in range(ND)) + \
                        (evtok[k % 2],) + tuple(xt)
                    for dc in range(ND):
                        ph.op("tensor", lambda e, pp=pp, wi=wi, dc=dc, do=do,
                              tq=tq: e.matmul(
                            pp[:],
                            wbf[wi % 2][:, dc * D + do * 128:
                                        dc * D + (do + 1) * 128],
                            xnch[:, dc * T + tq * 512:dc * T + (tq + 1) * 512],
                            start=(dc == 0), stop=(dc == ND - 1)),
                            after=aft if dc == 0 else ())
                    lastmm = ph.tok_last("tensor")
                    evtok[k % 2] = ph.op(
                        "scalar", lambda e, dst=dst, do=do, tq=tq, pp=pp:
                        e.copy(dst[:, do * T + tq * 512:do * T + (tq + 1) * 512],
                               pp[:]), after=(lastmm,), token=True)
            consumed[wi % 2] = lastmm
        if QKV_SUB < 3:
            ph.emit()
            return
        for dc in range(ND):
            wtok[(2, dc)] = ph.dma(
                "gpsimd", wbf[0][:, dc * D:(dc + 1) * D],
                dram["l0_wv"][dc * 128:(dc + 1) * 128, :],
                after=(consumed[0],), stream="w0")
        for ti in range(NT):
            for fo in range(2):
                k = ti * 2 + fo
                pp = ps[k % 2]
                aft = tuple(wtok[(2, dc)] for dc in range(ND)) + (evtok[k % 2],)
                for dc in range(ND):
                    ph.op("tensor", lambda e, pp=pp, dc=dc, ti=ti, fo=fo:
                          e.matmul(
                        pp[:],
                        xnch[:, dc * T + ti * 128:dc * T + (ti + 1) * 128],
                        wbf[0][:, dc * D + fo * 512:dc * D + (fo + 1) * 512],
                        start=(dc == 0), stop=(dc == ND - 1)),
                        after=aft if dc == 0 else ())
                lastmm = ph.tok_last("tensor")
                evtok[k % 2] = ph.op(
                    "scalar", lambda e, ti=ti, fo=fo, pp=pp: e.copy(
                        g["vtok"][:, ti * D + fo * 512:ti * D + (fo + 1) * 512],
                        pp[:]), after=(lastmm,), token=True)

        if QKV_SUB < 4:
            ph.emit()
            return
        mz = ph.op("vector", lambda e: e.memset(nrm[:], 0.0), token=True)
        ph.op("vector", lambda e: e.memset(gpad[:], -3e38))
        pndone = [mz, mz]
        sqdone = None
        addtoks = []
        for wi in range(2):
            src = g["qT"] if wi == 0 else g["kT"]
            for dc in range(ND):
                sq = ph.op("scalar", lambda e, src=src, dc=dc: e.activation(
                    qsq[:], src[:, dc * T:(dc + 1) * T], AF.Square),
                    after=(evtok[0], evtok[1], sqdone), token=True)
                for ti in range(NT):
                    b = ti % 2
                    tmm = ph.op("tensor", lambda e, b=b, ti=ti, dc=dc:
                                e.matmul(
                        pn[b][:], qsq[:, ti * 128:(ti + 1) * 128],
                        hb[:, dc * H:(dc + 1) * H], start=True, stop=True),
                        after=(sq, pndone[b], hbt), token=True)
                    col = (wi * NT + ti) * H
                    pndone[b] = ph.op(
                        "vector", lambda e, col=col, b=b: e.tensor_tensor(
                            nrm[:, col:col + H], nrm[:, col:col + H],
                            pn[b][:], op=ALU.add), after=(tmm, mz), token=True)
                    addtoks.append(pndone[b])
                sqdone = tmm
        t1 = ph.op("vector", lambda e: e.tensor_reduce(
            gpad[:, 0:1], nrm[:, 0:NT * H], axis=AX.X, op=ALU.max),
            after=tuple(addtoks[-NT:]), token=True)
        t2 = ph.op("vector", lambda e: e.tensor_reduce(
            gpad[:, 64:65], nrm[:, NT * H:], axis=AX.X, op=ALU.max),
            token=True)
        tt = ph.op("tensor", lambda e: e.transpose(pt[:], gpad[:], g["idf"][:]),
                   after=(t1, t2), token=True)
        tr1 = ph.op("vector", lambda e: e.tensor_reduce(
            cm[:, 0:1], pt[0:1, :], axis=AX.X, op=ALU.max), after=(tt,),
            token=True)
        tr2 = ph.op("vector", lambda e: e.tensor_reduce(
            cm[:, 1:2], pt[64:65, :], axis=AX.X, op=ALU.max), token=True)
        tv = ph.op("vector", lambda e: e.tensor_tensor(
            cm[:, 2:3], cm[:, 0:1], cm[:, 1:2], op=ALU.mult),
            after=(tr1, tr2), force=True, token=True)
        tsq = ph.op("scalar", lambda e: e.activation(cm[:, 3:4], cm[:, 2:3],
                                                     AF.Sqrt),
                    after=(tv,), token=True)
        tb = ph.op("tensor", lambda e: e.matmul(
            pb[:], g["ones1"][:], cm[:, 3:4], start=True, stop=True),
            after=(tsq,), token=True)
        ph.op("scalar", lambda e: e.mul(g["stats"][:, 1:2], pb[:],
                                        -1.0 / 32.0), after=(tb,))
        ph.emit()



def phase_mha(p, g):
    dram = p.dram
    ph = Ph(p, "mha")
    with contextlib.ExitStack() as s_:
        ET = p.sb("c_et", [128, NT * 512], BF16, s_)
        stage = [p.sb(f"c_st{i}", [128, 512], F32, s_) for i in range(2)]
        mha = p.sb("c_mha", [128, 4 * D], F32, s_)
        rz = p.sb("c_rz", [128, 64], F32, s_)
        xnb_ = [p.sb(f"c_xn{i}", [128, D], F32, s_) for i in range(2)]
        y0 = [p.sb(f"c_y0{i}", [128, D], F32, s_) for i in range(2)]
        y0b = [p.sb(f"c_y0b{i}", [128, D], BF16, s_) for i in range(2)]
        hat = p.sb("c_hat", [128, D], F32, s_)
        mv = p.sb("c_mv", [128, 16], F32, s_)
        rs = p.sb("c_rs", [128, 1], F32, s_)
        g1 = p.sb("c_g1", [128, D], F32, s_)
        b1 = p.sb("c_b1", [128, D], F32, s_)
        lg1 = p.sb("c_lg1", [1, D], F32, s_)
        lb1 = p.sb("c_lb1", [1, D], F32, s_)
        cmask = p.sb("c_cmask", [128, 4 * 512], BF16, s_)
        psS = [p.ps(f"c_psS{i}", [128, 512], s_) for i in range(2)]
        psO = [p.ps(f"c_psO{i}", [128, 512], s_) for i in range(2)]
        psT = [p.ps(f"c_psT{i}", [128, 128], s_) for i in range(2)]

        tap_y0 = p.tap("y0", [T, D])
        tap_mha = p.tap("mha", [T, D])
        tap_et = p.tap("et0", [128, 4 * 512])
        tap_st = p.tap("st0", [128, 512])
        tap_mr = p.tap("mhraw", [T, D])
        tap_g1 = p.tap("g1rep", [128, D])
        tap_b1 = p.tap("b1rep", [128, D])

        t1 = ph.dma("sync", lg1[:], dram["l0_ln1_g"], stream="p0")
        t2 = ph.dma("sync", lb1[:], dram["l0_ln1_b"], stream="p1")
        cmt = []
        for off in range(4):
            cmt.append(ph.dma("sync", cmask[:, off * 512:(off + 1) * 512],
                              dram[f"c_cmask{off}"], stream="p2"))
        bt1 = bcast_row(ph, p, g, lg1, g1, psS, after=(t1,))
        gtoks = bcast_row(ph, p, g, lb1, b1, psS, after=(t2,),
                          guard=(bt1[0], bt1[1]))
        if tap_g1 is not None:
            ph.dma("gpsimd", tap_g1, g1[:], after=tuple(bt1), stream="tap")
            ph.dma("gpsimd", tap_b1, b1[:], after=tuple(gtoks), stream="tap")
        gtoks = gtoks + cmt

        prev_qb = None
        prev_qb2 = ()
        avdone = [None, None]
        trdone = [None, None]
        xtoks = [None, None]
        ystore = [None, None]
        for qb in range(4):
            ktmax = 4 * qb + 4
            prev_h_av = None
            exp_h = {}
            norm_ready = []
            for h in range(H):
                dt_ = h // 2
                off = (h % 2) * 64
                masktoks = []
                for kt in range(ktmax):
                    b = kt % 2
                    tmm = ph.op("tensor", lambda e, b=b, kt=kt, dt_=dt_,
                                off=off, qb=qb: e.matmul(
                        psS[b][:],
                        g["kT"][off:off + 64,
                                dt_ * T + kt * 128:dt_ * T + (kt + 1) * 128],
                        g["qT"][off:off + 64,
                                dt_ * T + qb * 512:dt_ * T + (qb + 1) * 512],
                        start=True, stop=True),
                        after=(exp_h.get(kt - 2), prev_qb) +
                        (prev_qb2 if h == 0 and kt == 0 else ()) +
                        (tuple(gtoks) if qb == 0 and h == 0 else ()),
                        token=True)
                    exp_h[kt] = ph.op(
                        "scalar", lambda e, b=b, kt=kt: e.activation(
                            ET[:, kt * 512:(kt + 1) * 512], psS[b][:], AF.Exp,
                            scale=1.0 / 32.0, bias=g["stats"][:, 1:2]),
                        after=(tmm,), token=True)
                    if kt >= 4 * qb:
                        mo = kt - 4 * qb
                        masktoks.append(ph.op(
                            "vector", lambda e, kt=kt, mo=mo: e.tensor_tensor(
                                ET[:, kt * 512:(kt + 1) * 512],
                                ET[:, kt * 512:(kt + 1) * 512],
                                cmask[:, mo * 512:(mo + 1) * 512],
                                op=ALU.mult),
                            after=(exp_h[kt],), token=True))
                ob = h % 2
                aft0 = tuple(exp_h[kt] for kt in range(ktmax)) + \
                    tuple(masktoks) + (avdone[ob],)
                for kt in range(ktmax):
                    ph.op("tensor", lambda e, ob=ob, kt=kt, h=h, ktmax=ktmax:
                          e.matmul(
                        psO[ob][0:64, :],
                        g["vtok"][:, kt * D + h * 64:kt * D + (h + 1) * 64],
                        ET[:, kt * 512:(kt + 1) * 512],
                        start=(kt == 0), stop=(kt == ktmax - 1)),
                        after=aft0 if kt == 0 else ())
                for kt in range(ktmax):
                    ph.op("tensor", lambda e, ob=ob, kt=kt, ktmax=ktmax:
                          e.matmul(
                        psO[ob][64:65, :], g["onescol_bf"][:],
                        ET[:, kt * 512:(kt + 1) * 512],
                        start=(kt == 0), stop=(kt == ktmax - 1)))
                prev_h_av = ph.tok_last("tensor")
                if tap_et is not None and qb == 0 and h == 0:
                    ph.dma("gpsimd", tap_et, ET[:, 0:4 * 512],
                           after=tuple(masktoks) + (prev_h_av,), stream="tap")
                ev = ph.op("vector", lambda e, ob=ob: e.tensor_copy(
                    stage[ob][0:65, :], psO[ob][0:65, :]),
                    after=(prev_h_av,), token=True)
                if tap_st is not None and qb == 0 and h == 0:
                    ph.dma("gpsimd", tap_st, stage[ob][:, :], after=(ev,),
                           stream="tap")
                avdone[ob] = ev
                for j in range(4):
                    tb_ = j % 2
                    ttr = ph.op("tensor", lambda e, tb_=tb_, ob=ob, j=j:
                                e.transpose(
                        psT[tb_][:, 0:65],
                        stage[ob][0:65, j * 128:(j + 1) * 128],
                        g["idf"][0:65, 0:65]),
                        after=(ev,) + (trdone[tb_] or ()), token=True)
                    sc = ph.op("scalar", lambda e, tb_=tb_, j=j, h=h: e.copy(
                        mha[:, j * D + h * 64:j * D + (h + 1) * 64],
                        psT[tb_][:, 0:64]), after=(ttr,), token=True)
                    vr = ph.op("vector", lambda e, tb_=tb_, j=j, h=h:
                               e.reciprocal(
                        rz[:, j * 16 + h:j * 16 + h + 1],
                        psT[tb_][:, 64:65]), after=(ttr,), token=True)
                    trdone[tb_] = (sc, vr)
                    norm_ready += [sc, vr]
            ytoks = []
            for j in range(4):
                i = qb * 4 + j
                b = j % 2
                ntk = []
                for h in range(H):
                    ntk.append(ph.op("vector", lambda e, j=j, h=h:
                                     e.tensor_scalar(
                        mha[:, j * D + h * 64:j * D + (h + 1) * 64],
                        mha[:, j * D + h * 64:j * D + (h + 1) * 64],
                        rz[:, j * 16 + h:j * 16 + h + 1], None, op0=ALU.mult),
                        after=tuple(norm_ready), token=True))
                tl = ph.dma("sync", xnb_[b][:],
                            dram["xn0_f"][i * 128:(i + 1) * 128, :],
                            after=(xtoks[b],), stream=f"xn{b}")
                if tap_mr is not None:
                    ytoks.append(ph.dma(
                        "gpsimd", tap_mr[i * 128:(i + 1) * 128, :],
                        mha[:, j * D:(j + 1) * D], after=tuple(ntk),
                        stream="tap"))
                tsa, tr = ln_stats(ph, mv, mha[:, j * D:(j + 1) * D], rs,
                                   1e-5, extra_after=tuple(ntk))
                ph.op("vector", lambda e, j=j: e.tensor_scalar(
                    hat[:], mha[:, j * D:(j + 1) * D], mv[:, 12:13], rs[:],
                    op0=ALU.subtract, op1=ALU.mult), after=(tr, tsa),
                    force=True)
                if tap_mha is not None:
                    tmp = ph.op("vector", lambda e: e.nop(), token=True)
                    ph.dma("gpsimd", tap_mha[i * 128:(i + 1) * 128, :], hat[:],
                           after=(tmp,), stream="tap")
                ph.op("vector", lambda e: e.tensor_tensor(
                    hat[:], hat[:], g1[:], op=ALU.mult))
                ph.op("vector", lambda e: e.tensor_tensor(
                    hat[:], hat[:], b1[:], op=ALU.add))
                xtoks[b] = ph.op("vector", lambda e, b=b: e.tensor_tensor(
                    y0[b][:], hat[:], xnb_[b][:], op=ALU.add),
                    after=(tl, ystore[b]), token=True)
                yt = ph.op("vector", lambda e, b=b: e.tensor_copy(
                    y0b[b][:], y0[b][:]), token=True)
                ph.dma("gpsimd", dram["y0_f"][i * 128:(i + 1) * 128, :],
                       y0[b][:], after=(yt,), stream=f"y{b}")
                t3 = ph.dma("gpsimd", dram["y0_b"][i * 128:(i + 1) * 128, :],
                            y0b[b][:], after=(yt,), stream=f"y{b}")
                if tap_y0 is not None:
                    t3 = ph.dma("gpsimd", tap_y0[i * 128:(i + 1) * 128, :],
                                y0[b][:], after=(yt,), stream=f"y{b}")
                ystore[b] = t3
                ytoks.append(t3)
            prev_qb = ytoks[-1]
            prev_qb2 = tuple(ytoks[-3:])
        ph.emit()



def phase_attn0(p, g):
    dram = p.dram
    ph = Ph(p, "attn0")
    with contextlib.ExitStack() as s_:
        xnch = p.sb("d_xnch", [128, ND * T], BF16, s_)
        xntok = p.sb("d_xntok", [128, NT * D], BF16, s_)
        ET = p.sb("d_et", [128, NT * 512], BF16, s_)
        att = [p.sb(f"d_att{i}", [128, D], F32, s_) for i in range(2)]
        xb = [p.sb(f"d_xb{i}", [128, D], F32, s_) for i in range(2)]
        yb = [p.sb(f"d_yb{i}", [128, D], F32, s_) for i in range(2)]
        rz = p.sb("d_rz", [128, 2], F32, s_)
        psS = [p.ps(f"d_psS{i}", [128, 512], s_) for i in range(2)]
        psU = [p.ps(f"d_psU{i}", [128, 1024], s_) for i in range(2)]
        psZ = [p.ps(f"d_psZ{i}", [128, 1], s_) for i in range(2)]

        tap_att = p.tap("attn0", [T, D])

        xt = []
        for dc in range(ND):
            xt.append(ph.dma("sync", xnch[:, dc * T:(dc + 1) * T],
                             dram["xn0_b"][:, dc * 128:(dc + 1) * 128],
                             transpose=True, stream="xt"))
        for ti in range(NT):
            xt.append(ph.dma("sync", xntok[:, ti * D:(ti + 1) * D],
                             dram["xn0_b"][ti * 128:(ti + 1) * 128, :],
                             stream="xt"))
        XT = tuple(xt)

        evdone = [None, None]
        for qb in range(4):
            exp_t = {}
            for kt in range(NT):
                b = kt % 2
                for dc in range(ND):
                    ph.op("tensor", lambda e, b=b, kt=kt, qb=qb, dc=dc:
                          e.matmul(
                        psS[b][:],
                        xnch[:, dc * T + kt * 128:dc * T + (kt + 1) * 128],
                        xnch[:, dc * T + qb * 512:dc * T + (qb + 1) * 512],
                        start=(dc == 0), stop=(dc == ND - 1)),
                        after=(XT + (exp_t.get(kt - 2),)) if dc == 0 else ())
                tmm = ph.tok_last("tensor")
                exp_t[kt] = ph.op("scalar", lambda e, b=b, kt=kt:
                                  e.activation(
                    ET[:, kt * 512:(kt + 1) * 512], psS[b][:], AF.Exp,
                    bias=g["stats"][:, 0:1]), after=(tmm,), token=True)
            aft0 = tuple(exp_t[kt] for kt in range(NT))
            for j in range(4):
                ob = j % 2
                i = qb * 4 + j
                for kt in range(NT):
                    for fo in range(2):
                        ph.op("tensor", lambda e, ob=ob, kt=kt, j=j, fo=fo:
                              e.matmul(
                            psU[ob][:, fo * 512:(fo + 1) * 512],
                            ET[:, kt * 512 + j * 128:kt * 512 + (j + 1) * 128],
                            xntok[:, kt * D + fo * 512:kt * D + (fo + 1) * 512],
                            start=(kt == 0), stop=(kt == NT - 1)),
                            after=(aft0 + (evdone[ob],))
                            if (kt == 0 and fo == 0) else ())
                    ph.op("tensor", lambda e, ob=ob, kt=kt, j=j: e.matmul(
                        psZ[ob][:],
                        ET[:, kt * 512 + j * 128:kt * 512 + (j + 1) * 128],
                        g["onescol_bf"][:],
                        start=(kt == 0), stop=(kt == NT - 1)))
                tav = ph.tok_last("tensor")
                tl1 = ph.dma("sync", xb[ob][:],
                             dram["x"][i * 128:(i + 1) * 128, :],
                             after=(evdone[ob],), stream=f"a{ob}")
                tl2 = ph.dma("sync", yb[ob][:],
                             dram["y0_f"][i * 128:(i + 1) * 128, :],
                             after=(evdone[ob],), stream=f"c{ob}")
                ph.op("vector", lambda e, ob=ob: e.reciprocal(
                    rz[:, ob:ob + 1], psZ[ob][:]), after=(tav,))
                tv = ph.op("vector", lambda e, ob=ob: e.tensor_scalar(
                    att[ob][:], psU[ob][:], rz[:, ob:ob + 1], None,
                    op0=ALU.mult), token=True)
                taptok = None
                if tap_att is not None:
                    taptok = ph.dma("gpsimd", tap_att[i * 128:(i + 1) * 128, :],
                                    att[ob][:], after=(tv,), stream="tap")
                ph.op("vector", lambda e, ob=ob: e.tensor_tensor(
                    att[ob][:], att[ob][:], xb[ob][:], op=ALU.add),
                    after=(tl1, taptok))
                tv2 = ph.op("vector", lambda e, ob=ob: e.tensor_tensor(
                    att[ob][:], att[ob][:], yb[ob][:], op=ALU.add),
                    after=(tl2,), token=True)
                evdone[ob] = ph.dma(
                    "gpsimd", dram["base0"][i * 128:(i + 1) * 128, :],
                    att[ob][:], after=(tv2,), stream=f"b{ob}")
        ph.emit()



def phase_moe(p, g, layer):
    dram = p.dram
    L = f"l{layer}"
    ph = Ph(p, f"moe{layer}")
    with contextlib.ExitStack() as s_:
        es_ = s_.enter_context(contextlib.ExitStack())
        acc = p.sb(f"m{layer}_acc", [128, NT * D], F32, s_)
        comb = p.sb(f"m{layer}_comb", [128, NT * NE], F32, s_)
        aux = ([p.sb(f"m{layer}_aux{i}", [128, D], F32, s_) for i in range(2)]
               if layer == 1 else None)
        psA = [p.ps(f"m{layer}_psA{i}", [128, 512], s_) for i in range(2)]
        psO = [p.ps(f"m{layer}_psO{i}", [128, 1024], s_) for i in range(2)]
        psL = p.ps(f"m{layer}_psL", [128, NE], s_)
        psC = p.ps(f"m{layer}_psC", [8, 128], s_)
        xchb = p.sb(f"m{layer}_xchb", [128, ND * T], BF16, es_)
        w1b = p.sb(f"m{layer}_w1", [128, ND * 2048], BF16, es_)
        w2b = p.sb(f"m{layer}_w2", [128, 16 * D], BF16, es_)
        hT = p.sb(f"m{layer}_hT", [128, 16 * 512], BF16, es_)
        combT = p.sb(f"m{layer}_combT", [8, T], F32, es_)
        gsb = p.sb(f"m{layer}_gate", [128, ND * NE], F32, es_)
        b1sb = p.sb(f"m{layer}_b1", [128, NE * 32], F32, es_)
        b2sb = p.sb(f"m{layer}_b2", [8, D], F32, es_)
        ytile = (p.sb(f"m{layer}_yt", [128, D], F32, es_)
                 if layer == 0 else None)
        stg = p.sb(f"m{layer}_stg", [128, ND * 128], F32, es_)
        lgt = p.sb(f"m{layer}_lgt", [128, 16], F32, es_)

        src_b = "y0_b" if layer == 0 else "xn1_b"
        xt = []
        for dc in range(ND):
            xt.append(ph.dma("sync", xchb[:, dc * T:(dc + 1) * T],
                             dram[src_b][:, dc * 128:(dc + 1) * 128],
                             transpose=True, stream="xc"))
        gt_tok = ph.dma("sync", gsb[:], dram[L + "_gate"], stream="g1")
        b1_tok = ph.dma("sync", b1sb[:], dram[L + "_b1"], stream="g2")
        b2_tok = ph.dma("sync", b2sb[:], dram[L + "_b2"], stream="g3")
        tap_cmb = p.tap(f"combine{layer}", [T, NE])
        tap_lg = p.tap(f"logits{layer}", [T, 16])

        ctoks = []
        tprev = [None, None]
        lgdone = None
        lgtap = None
        for ti in range(NT):
            if layer == 0:
                tl = ph.dma("sync", ytile[:],
                            dram["y0_f"][ti * 128:(ti + 1) * 128, :],
                            after=(lgdone,), stream="yt")
                for dc in range(ND):
                    b = dc % 2
                    ttr = ph.op("tensor", lambda e, b=b, dc=dc: e.transpose(
                        psA[b][:, 0:128], ytile[:, dc * 128:(dc + 1) * 128],
                        g["idf"][:]), after=(tl, tprev[b]), token=True)
                    tprev[b] = ph.op("vector", lambda e, b=b, dc=dc:
                                     e.tensor_copy(
                        stg[:, dc * 128:(dc + 1) * 128], psA[b][:, 0:128]),
                        after=(ttr,), token=True)
                aft = tuple(tprev) + (gt_tok,)
                for dc in range(ND):
                    ph.op("tensor", lambda e, dc=dc: e.matmul(
                        psL[:], stg[:, dc * 128:(dc + 1) * 128],
                        gsb[:, dc * NE:(dc + 1) * NE],
                        start=(dc == 0), stop=(dc == ND - 1)),
                        after=aft if dc == 0 else ())
            else:
                stok = []
                for dc in range(ND):
                    stok.append(ph.dma(
                        "sync", stg[:, dc * 128:(dc + 1) * 128],
                        dram["xn1_ch"][dc * 128:(dc + 1) * 128,
                                       4 + ti * 128:4 + (ti + 1) * 128],
                        after=(lgdone,), stream="yt"))
                for dc in range(ND):
                    ph.op("tensor", lambda e, dc=dc: e.matmul(
                        psL[:], stg[:, dc * 128:(dc + 1) * 128],
                        gsb[:, dc * NE:(dc + 1) * NE],
                        start=(dc == 0), stop=(dc == ND - 1)),
                        after=tuple(stok) + (gt_tok,) if dc == 0 else ())
            tmm = ph.tok_last("tensor")
            cc = comb[:, ti * NE:(ti + 1) * NE]
            tv = ph.op("vector", lambda e: e.tensor_copy(lgt[:, 0:8], psL[:]),
                       after=(tmm, lgtap), token=True)
            lgdone = tv
            trm = ph.op("vector", lambda e: e.tensor_reduce(
                lgt[:, 8:9], lgt[:, 0:8], axis=AX.X, op=ALU.max), token=True)
            tmsk = ph.op("vector", lambda e, cc=cc: e.tensor_scalar(
                cc, lgt[:, 0:8], lgt[:, 8:9], None, op0=ALU.is_ge),
                after=(trm,), force=True, token=True)
            if layer == 0:
                ph.op("vector", lambda e, cc=cc: e.scalar_tensor_tensor(
                    lgt[:, 0:8], cc, -1e30, lgt[:, 0:8], op0=ALU.mult,
                    op1=ALU.add))
                trm2 = ph.op("vector", lambda e: e.tensor_reduce(
                    lgt[:, 9:10], lgt[:, 0:8], axis=AX.X, op=ALU.max),
                    token=True)
                tvv = ph.op("vector", lambda e: e.tensor_scalar(
                    lgt[:, 10:11], lgt[:, 9:10], lgt[:, 8:9], None,
                    op0=ALU.subtract), after=(trm2,), force=True, token=True)
                te = ph.op("scalar", lambda e: e.activation(
                    lgt[:, 11:12], lgt[:, 10:11], AF.Exp), after=(tvv,),
                    token=True)
                ph.op("vector", lambda e: e.tensor_scalar(
                    lgt[:, 12:13], lgt[:, 11:12], 1.0, None, op0=ALU.add),
                    after=(te,))
                ph.op("vector", lambda e: e.reciprocal(lgt[:, 13:14],
                                                       lgt[:, 12:13]))
                ph.op("vector", lambda e: e.tensor_tensor(
                    lgt[:, 14:15], lgt[:, 11:12], lgt[:, 13:14], op=ALU.mult))
                ph.op("vector", lambda e, cc=cc: e.tensor_scalar(
                    cc, cc, lgt[:, 13:14], None, op0=ALU.mult))
                ph.op("vector", lambda e: e.tensor_scalar(
                    lgt[:, 0:8], lgt[:, 0:8], lgt[:, 9:10], None,
                    op0=ALU.is_ge))
                tv2 = ph.op("vector", lambda e, cc=cc: e.scalar_tensor_tensor(
                    cc, lgt[:, 0:8], lgt[:, 14:15], cc, op0=ALU.mult,
                    op1=ALU.add), token=True)
            else:
                tn = ph.op("scalar", lambda e: e.mul(
                    lgt[:, 9:10], lgt[:, 8:9], -1.0), after=(trm,), token=True)
                te = ph.op("scalar", lambda e: e.activation(
                    lgt[:, 0:8], lgt[:, 0:8], AF.Exp, bias=lgt[:, 9:10],
                    accum_out=lgt[:, 11:12]), after=(tn, tmsk), token=True)
                ph.op("vector", lambda e: e.reciprocal(
                    lgt[:, 12:13], lgt[:, 11:12]), after=(te,))
                tv2 = ph.op("vector", lambda e, cc=cc: e.tensor_scalar(
                    cc, cc, lgt[:, 12:13], None, op0=ALU.mult), token=True)
            if tap_cmb is not None:
                ph.dma("gpsimd", tap_cmb[ti * 128:(ti + 1) * 128, :], cc,
                       after=(tv2,), stream="tap")
            if tap_lg is not None:
                lgtap = ph.dma("gpsimd", tap_lg[ti * 128:(ti + 1) * 128, :],
                               lgt[:], after=(tv2,), stream="tap")
            ttr = ph.op("tensor", lambda e, cc=cc: e.transpose(
                psC[:], cc, g["idf"][:]),
                after=(tv2, ctoks[-1] if ctoks else None), token=True)
            ctoks.append(ph.op("vector", lambda e, ti=ti: e.tensor_copy(
                combT[:, ti * 128:(ti + 1) * 128], psC[:]),
                after=(ttr,), token=True))

        initoks = []
        psO_guard = [None, None]
        auxg = [None, None]
        for ti in range(NT):
            b = ti % 2
            for fo in range(2):
                ph.op("tensor", lambda e, b=b, ti=ti, fo=fo: e.matmul(
                    psO[b][:, fo * 512:(fo + 1) * 512],
                    combT[:, ti * 128:(ti + 1) * 128],
                    b2sb[:, fo * 512:(fo + 1) * 512], start=True, stop=True),
                    after=(ctoks[ti], psO_guard[b], b2_tok)
                    if fo == 0 else ())
            tmm = ph.tok_last("tensor")
            if layer == 0:
                tv = ph.op("vector", lambda e, b=b, ti=ti: e.tensor_copy(
                    acc[:, ti * D:(ti + 1) * D], psO[b][:]),
                    after=(tmm,), token=True)
            else:
                ta = ph.dma("sync", aux[0][:],
                            dram["att1"][ti * 128:(ti + 1) * 128, :],
                            after=(auxg[0],), stream="ax0")
                tb = ph.dma("sync", aux[1][:],
                            dram["x1_f"][ti * 128:(ti + 1) * 128, :],
                            after=(auxg[1],), stream="ax1")
                auxg[0] = ph.op("vector", lambda e, b=b, ti=ti:
                                e.tensor_tensor(
                    acc[:, ti * D:(ti + 1) * D], psO[b][:], aux[0][:],
                    op=ALU.add), after=(tmm, ta), token=True)
                tv = ph.op("vector", lambda e, ti=ti: e.tensor_tensor(
                    acc[:, ti * D:(ti + 1) * D], acc[:, ti * D:(ti + 1) * D],
                    aux[1][:], op=ALU.add), after=(tb,), token=True)
                auxg[1] = tv
            psO_guard[b] = tv
            initoks.append(tv)

        acctoks = list(initoks)
        accdone = [initoks[-2], initoks[-1]]
        w1tok = {}
        w2tok = {}
        mm1done = None
        mm2done = None
        relu_prev = {}
        if layer == 0:
            relu_prev[-2] = tprev[0]
            relu_prev[-1] = tprev[1]
        for e_ in range(NE):
            for fh in range(2):
                for dc in range(ND):
                    w1tok[dc] = ph.dma(
                        "gpsimd", w1b[:, dc * 2048:(dc + 1) * 2048],
                        dram[L + "_w1"][e_, dc * 128:(dc + 1) * 128,
                                        fh * 2048:(fh + 1) * 2048],
                        after=(mm1done,), stream="w1")
                for ft in range(16):
                    w2tok[ft] = ph.dma(
                        "gpsimd", w2b[:, ft * D:(ft + 1) * D],
                        dram[L + "_w2"][e_, fh * 2048 + ft * 128:
                                        fh * 2048 + (ft + 1) * 128, :],
                        after=(mm2done,), stream="w2")
                for tc in range(4):
                    for ft in range(16):
                        b = ft % 2
                        aft = tuple(w1tok[dc] for dc in range(ND)) + \
                            (relu_prev.get(ft - 2), mm1done) + tuple(xt)
                        for dc in range(ND):
                            ph.op("tensor", lambda e, b=b, dc=dc, ft=ft,
                                  tc=tc: e.matmul(
                                psA[b][:],
                                w1b[:, dc * 2048 + ft * 128:
                                    dc * 2048 + (ft + 1) * 128],
                                xchb[:, dc * T + tc * 512:
                                     dc * T + (tc + 1) * 512],
                                start=(dc == 0), stop=(dc == ND - 1)),
                                after=aft if dc == 0 else ())
                        tmm = ph.tok_last("tensor")
                        col = e_ * 32 + fh * 16 + ft
                        relu_prev[ft] = ph.op(
                            "scalar", lambda e, b=b, ft=ft, col=col:
                            e.activation(
                                hT[:, ft * 512:(ft + 1) * 512], psA[b][:],
                                AF.Relu, bias=b1sb[:, col:col + 1]),
                            after=(tmm, b1_tok), token=True)
                    mm1done = tmm
                    for tt in range(4):
                        ti = tc * 4 + tt
                        ob = tt % 2
                        aft = tuple(relu_prev[ft] for ft in range(16)) + \
                            tuple(w2tok[ft] for ft in range(16)) + \
                            (accdone[ob],)
                        for ft in range(16):
                            for fo in range(2):
                                ph.op("tensor", lambda e, ob=ob, ft=ft, fo=fo,
                                      tt=tt: e.matmul(
                                    psO[ob][:, fo * 512:(fo + 1) * 512],
                                    hT[:, ft * 512 + tt * 128:
                                        ft * 512 + (tt + 1) * 128],
                                    w2b[:, ft * D + fo * 512:
                                        ft * D + (fo + 1) * 512],
                                    start=(ft == 0), stop=(ft == 15)),
                                    after=aft if (ft == 0 and fo == 0) else ())
                        tmm2 = ph.tok_last("tensor")
                        accdone[ob] = ph.op(
                            "vector", lambda e, ob=ob, ti=ti, e2=e_:
                            e.scalar_tensor_tensor(
                                acc[:, ti * D:(ti + 1) * D], psO[ob][:],
                                comb[:, ti * NE + e2:ti * NE + e2 + 1],
                                acc[:, ti * D:(ti + 1) * D],
                                op0=ALU.mult, op1=ALU.add),
                            after=(tmm2, acctoks[ti]), token=True)
                        acctoks[ti] = accdone[ob]
                    mm2done = tmm2

        es_.close()
        if layer == 0:
            tail_l0(p, g, ph, s_, acc, acctoks, psA)
        else:
            for ti in range(NT):
                ph.dma("sync", dram["out"][ti * 128:(ti + 1) * 128, :],
                       acc[:, ti * D:(ti + 1) * D],
                       after=(acctoks[ti], acctoks[-1]), stream="out")
        ph.emit()


def tail_l0(p, g, ph, s_, acc, acctoks, psA):
    dram = p.dram
    g2 = p.sb("t0_g2", [128, D], F32, s_)
    b2r = p.sb("t0_b2r", [128, D], F32, s_)
    lg2 = p.sb("t0_lg2", [1, D], F32, s_)
    lb2 = p.sb("t0_lb2", [1, D], F32, s_)
    mv = p.sb("t0_mv", [128, 16], F32, s_)
    rs = p.sb("t0_rs", [128, 1], F32, s_)
    hat = p.sb("t0_hat", [128, D], F32, s_)
    bb = [p.sb(f"t0_bb{i}", [128, D], F32, s_) for i in range(2)]
    x1 = [p.sb(f"t0_x1{i}", [128, D], F32, s_) for i in range(2)]
    xn1 = p.sb("t0_xn1", [128, D], F32, s_)
    xn1b = [p.sb(f"t0_xn1b{i}", [128, D], BF16, s_) for i in range(2)]
    stg2 = [p.sb(f"t0_stg{i}", [128, ND * 128], F32, s_) for i in range(2)]
    zc = p.sb("t0_zc", [128, 4], F32, s_)

    tap_x1 = p.tap("x1", [T, D])
    tap_xn1 = p.tap("xn1", [T, D])

    TG = acctoks[-1]
    t1 = ph.dma("sync", lg2[:], dram["l0_ln2_g"], after=(TG,), stream="p0")
    t2 = ph.dma("sync", lb2[:], dram["l0_ln2_b"], stream="p1")
    bt1 = bcast_row(ph, p, g, lg2, g2, psA, after=(t1, TG))
    bt = tuple(bt1)
    bt2 = bcast_row(ph, p, g, lb2, b2r, psA, after=(t2,),
                    guard=(bt1[0], bt1[1]))
    bt += tuple(bt2)
    zt = ph.op("vector", lambda e: e.memset(zc[:], 0.0), after=(TG,),
               token=True)
    for dc in range(ND):
        ph.dma("gpsimd", dram["xn1_ch"][dc * 128:(dc + 1) * 128, 0:4], zc[:],
               after=(zt,), stream="zc")

    bbg = [None, None]
    x1g = [None, None]
    stgg = [None, None]
    nbg = [None, None]
    x1st = [None, None]
    xn1tap = None
    trg = [bt2[-2], bt2[-1]]
    for ti in range(NT):
        b = ti % 2
        atile = acc[:, ti * D:(ti + 1) * D]
        tl = ph.dma("sync", bb[b][:],
                    dram["base0"][ti * 128:(ti + 1) * 128, :],
                    after=(bbg[b], TG), stream=f"bb{b}")
        tsa, tr = ln_stats(ph, mv, atile, rs, 1e-5,
                           extra_after=(acctoks[ti], TG) + bt)
        ph.op("vector", lambda e, atile=atile: e.tensor_scalar(
            hat[:], atile, mv[:, 12:13], rs[:], op0=ALU.subtract,
            op1=ALU.mult), after=(tr, tsa), force=True)
        ph.op("vector", lambda e: e.tensor_tensor(hat[:], hat[:], g2[:],
                                                  op=ALU.mult))
        ph.op("vector", lambda e: e.tensor_tensor(hat[:], hat[:], b2r[:],
                                                  op=ALU.add))
        tx = ph.op("vector", lambda e, b=b: e.tensor_tensor(
            x1[b][:], hat[:], bb[b][:], op=ALU.add),
            after=(tl, x1g[b], x1st[b]), token=True)
        bbg[b] = tx
        x1st[b] = ph.dma("gpsimd", dram["x1_f"][ti * 128:(ti + 1) * 128, :],
                         x1[b][:], after=(tx,), stream=f"xs{b}")
        if tap_x1 is not None:
            x1st[b] = ph.dma("gpsimd", tap_x1[ti * 128:(ti + 1) * 128, :],
                             x1[b][:], after=(tx,), stream=f"xs{b}")
        tsb, tr2 = ln_stats(ph, mv, x1[b], rs, 1e-5, extra_after=(tx,))
        tn = ph.op("vector", lambda e, b=b: e.tensor_scalar(
            xn1[:], x1[b][:], mv[:, 12:13], rs[:], op0=ALU.subtract,
            op1=ALU.mult), after=(tr2, tsb, xn1tap), force=True, token=True)
        x1g[b] = tn
        tc = ph.op("vector", lambda e, b=b: e.tensor_copy(xn1b[b][:], xn1[:]),
                   after=(nbg[b],), token=True)
        nbg[b] = ph.dma("gpsimd", dram["xn1_b"][ti * 128:(ti + 1) * 128, :],
                        xn1b[b][:], after=(tc,), stream=f"nb{b}")
        if tap_xn1 is not None:
            xn1tap = ph.dma("gpsimd", tap_xn1[ti * 128:(ti + 1) * 128, :],
                            xn1[:], after=(tn, stgg[b]), stream="tap")
        trtoks = []
        for dc in range(ND):
            tb_ = dc % 2
            ttr = ph.op("tensor", lambda e, tb_=tb_, dc=dc: e.transpose(
                psA[tb_][:, 0:128], xn1[:, dc * 128:(dc + 1) * 128],
                g["idf"][:]), after=(tn, trg[tb_], stgg[b]), token=True)
            trg[tb_] = ph.op("vector", lambda e, tb_=tb_, dc=dc, b=b:
                             e.tensor_copy(
                stg2[b][:, dc * 128:(dc + 1) * 128], psA[tb_][:, 0:128]),
                after=(ttr,), token=True)
            trtoks.append(trg[tb_])
        last = None
        for dc in range(ND):
            last = ph.dma(
                "gpsimd",
                dram["xn1_ch"][dc * 128:(dc + 1) * 128,
                               4 + ti * 128:4 + (ti + 1) * 128],
                stg2[b][:, dc * 128:(dc + 1) * 128],
                after=(trtoks[dc],), stream=f"ch{b}")
        stgg[b] = last


def phase_rwkv(p, g):
    dram = p.dram
    ph = Ph(p, "rwkv")
    with contextlib.ExitStack() as s_:
        wmaa1 = p.sb("r_wmaa1", [128, ND * 192], BF16, s_)
        wmaa2 = p.sb("r_wmaa2", [64, 3 * D], BF16, s_)
        wd1 = p.sb("r_wd1", [128, ND * 64], BF16, s_)
        wd2 = p.sb("r_wd2", [64, D], BF16, s_)
        wa1 = p.sb("r_wa1", [128, ND * 64], BF16, s_)
        wa2 = p.sb("r_wa2", [64, D], BF16, s_)
        wk1 = p.sb("r_wk1", [128, ND * 64], BF16, s_)
        wk2 = p.sb("r_wk2", [64, D], BF16, s_)
        wg1 = p.sb("r_wg1", [128, ND * 128], BF16, s_)
        wg2 = p.sb("r_wg2", [128, D], BF16, s_)
        wr = p.sb("r_wr", [128, ND * D], BF16, s_)
        wkk = p.sb("r_wk", [128, ND * D], BF16, s_)
        wv = p.sb("r_wv", [128, ND * D], BF16, s_)
        wo = p.sb("r_wo", [128, ND * D], BF16, s_)
        vecs = {}
        for v in ["time_maa_x", "time_maa_r", "time_maa_w", "time_maa_k",
                  "time_maa_v", "time_maa_a", "time_maa_g", "time_decay",
                  "time_aaaaa", "faaaa"]:
            vecs[v] = p.sb("r_" + v, [128, ND], F32, s_)
        lnw = p.sb("r_lnw", [128, D], F32, s_)
        lnb = p.sb("r_lnb", [128, D], F32, s_)
        l1v = p.sb("r_l1v", [1, D], F32, s_)
        l2v = p.sb("r_l2v", [1, D], F32, s_)
        hb = p.sb("r_hb", [128, ND * H], BF16, s_)
        SS = p.sb("r_SS", [128, 512], F32, s_)
        SSb = p.sb("r_SSb", [128, 512], BF16, s_)

        def scbuf(n, w=SCN, dt=F32):
            return p.sb("r_" + n, [128, ND * w], dt, s_)

        xch = scbuf("xch", SCN + 1)
        xx = scbuf("xx")
        rT = scbuf("rT")
        wraw = scbuf("wraw")
        ks = scbuf("ks")
        kkn = scbuf("kkn")
        ag = scbuf("ag")
        gT = scbuf("gT")
        vtok = p.sb("r_vtok", [128, CPS * D], F32, s_)
        kkn_tok = p.sb("r_kkntok", [128, CPS * D], F32, s_)
        y_sc = p.sb("r_ysc", [128, CPS * D], F32, s_)
        xs = p.sb("r_xs", [128, ND * SCN], BF16, s_)
        mixbuf = p.sb("r_mixbuf", [64, 3 * SCN], BF16, s_)
        tnh = p.sb("r_tnh", [64, SCN], BF16, s_)
        tnh128 = p.sb("r_tnh128", [128, SCN], BF16, s_)
        tmpa = [p.sb(f"r_tmpa{i}", [128, SCN], F32, s_) for i in range(3)]

        p_ch = p.sb("r_pch", [128, D], F32, s_)
        p_tok = p.sb("r_ptok", [128, D], F32, s_)
        P_tok = p.sb("r_Ptok", [128, D], F32, s_)
        E0 = p.sb("r_E0", [128, D], F32, s_)
        E1 = p.sb("r_E1", [128, D], F32, s_)
        E2 = p.sb("r_E2", [128, D], F32, s_)
        E2t = p.sb("r_E2t", [128, D], F32, s_)
        gl = p.sb("r_gl", [128, ND], F32, s_)
        Ab = p.sb("r_Ab", [128, D], BF16, s_)
        Rb = p.sb("r_Rb", [128, D], BF16, s_)
        Kcb = p.sb("r_Kcb", [128, D], BF16, s_)
        Bcb = p.sb("r_Bcb", [128, D], BF16, s_)
        Ktb = p.sb("r_Ktb", [128, D], BF16, s_)
        Btb = p.sb("r_Btb", [128, D], BF16, s_)
        Vb = p.sb("r_Vb", [128, D], BF16, s_)
        Gt = p.sb("r_Gt", [128, 128], BF16, s_)
        Ut = [p.sb(f"r_Ut{i}", [128, 128], BF16, s_) for i in range(7)]
        Mt = [p.sb(f"r_Mt{i}", [128, 128], BF16, s_) for i in range(7)]
        Wt = p.sb("r_Wt", [128, 128], BF16, s_)
        W2t = p.sb("r_W2t", [128, 128], BF16, s_)
        Xb = p.sb("r_Xb", [128, 64], BF16, s_)
        bscal = p.sb("r_bscal", [128, H], F32, s_)
        ygb = p.sb("r_ygb", [128, D], BF16, s_)
        mvg = p.sb("r_mvg", [128, 16], F32, s_)
        rsg = p.sb("r_rsg", [128, 1], F32, s_)

        psA = [p.ps(f"r_psA{i}", [128, 512], s_) for i in range(2)]
        psB = [p.ps(f"r_psB{i}", [128, 128], s_) for i in range(2)]
        pX = p.ps("r_pX", [128, 64], s_)
        ps96 = [p.ps(f"r_ps96{i}", [96, SCN], s_) for i in range(2)]

        psg = {"A0": None, "A1": None, "B0": None, "B1": None, "X": None,
               "960": None, "961": None}

        tap_att1 = p.tap("att1", [T, D])
        tap_y = p.tap("rwkv_y", [T, D])

        wt = []
        for dc in range(ND):
            d0 = dc * 128
            wt.append(ph.dma("gpsimd", wmaa1[:, dc * 192:(dc + 1) * 192],
                             dram["l1_maa_w1"][d0:d0 + 128, :], stream="wl"))
            wt.append(ph.dma("gpsimd", wd1[:, dc * 64:(dc + 1) * 64],
                             dram["l1_decay_w1"][d0:d0 + 128, :], stream="wl"))
            wt.append(ph.dma("gpsimd", wa1[:, dc * 64:(dc + 1) * 64],
                             dram["l1_aaa_w1"][d0:d0 + 128, :], stream="wl"))
            wt.append(ph.dma("gpsimd", wk1[:, dc * 64:(dc + 1) * 64],
                             dram["l1_kkk_w1"][d0:d0 + 128, :], stream="wl"))
            wt.append(ph.dma("gpsimd", wg1[:, dc * 128:(dc + 1) * 128],
                             dram["l1_gate_w1"][d0:d0 + 128, :], stream="wl"))
            for wnm, wdst in [("l1_w_r", wr), ("l1_w_k", wkk),
                              ("l1_w_v", wv), ("l1_w_o", wo)]:
                wt.append(ph.dma("gpsimd", wdst[:, dc * D:(dc + 1) * D],
                                 dram[wnm][d0:d0 + 128, :], stream="wl"))
        for wnm, wdst in [("l1_maa_w2", wmaa2), ("l1_decay_w2", wd2),
                          ("l1_aaa_w2", wa2), ("l1_kkk_w2", wk2),
                          ("l1_gate_w2", wg2)]:
            wt.append(ph.dma("gpsimd", wdst[:], dram[wnm], stream="wl"))
        wt.append(ph.dma("sync", hb[:], dram["c_headblk_bf"], stream="wl2"))
        for v in vecs:
            wt.append(ph.dma("sync", vecs[v][:], dram["l1_" + v],
                             stream="wl2"))
        tl1 = ph.dma("sync", l1v[:], dram["l1_ln_w"], stream="lv1")
        tl2 = ph.dma("sync", l2v[:], dram["l1_ln_b"], stream="lv2")
        bc1 = bcast_row(ph, p, g, l1v, lnw, psA, after=(tl1,))
        bc2 = bcast_row(ph, p, g, l2v, lnb, psA, after=(tl2,),
                        guard=(bc1[0], bc1[1]))
        wt += bc1 + bc2
        psg["A0"] = bc2[-2]
        psg["A1"] = bc2[-1]
        sst = ph.op("vector", lambda e: e.memset(SS[:], 0.0), token=True)
        wt.append(sst)
        WT = tuple(wt)

        def chs(buf, dc, c0=0, w=SCN, ext=0):
            base = dc * (SCN + ext)
            return buf[:, base + c0:base + c0 + w]

        def dview(buf):
            return buf[:].rearrange("p (a b) -> p a b", b=128)

        def MM(bank, fn, after=(), token=True):
            return ph.op("tensor", fn, after=tuple(after) + (psg[bank],),
                         token=token)

        SNAMES = ["r", "w", "k", "v", "a", "g"]
        tmpg = {0: None, 1: None, 2: None}

        prev_sc = None
        CG = ()
        ssb_prev = sst
        for sc in range(NSC):
            ci = sc
            xtl = []
            for dc in range(ND):
                xtl.append(ph.dma(
                    "sync", chs(xch, dc, 0, SCN + 1, 1),
                    dram["xn1_ch"][dc * 128:(dc + 1) * 128,
                                   sc * SCN + 3:sc * SCN + SCN + 4],
                    after=(prev_sc,), stream="xc"))
            xxt = []
            for dc in range(ND):
                xxt.append(ph.op("vector", lambda e, dc=dc: e.tensor_tensor(
                    chs(xx, dc), chs(xch, dc, 0, SCN, 1),
                    chs(xch, dc, 1, SCN, 1), op=ALU.subtract),
                    after=tuple(xtl) + WT + CG, token=True))
            mmxt = []
            for dc in range(ND):
                mmxt.append(ph.op("vector", lambda e, dc=dc:
                                  e.scalar_tensor_tensor(
                    chs(xs, dc), chs(xx, dc),
                    vecs["time_maa_x"][:, dc:dc + 1], chs(xch, dc, 1, SCN, 1),
                    op0=ALU.mult, op1=ALU.add), after=(xxt[dc],), token=True))
            mixt = {}
            for g2_ in range(2):
                bank = "96" + str(g2_)
                for dc in range(ND):
                    if dc == 0:
                        MM(bank, lambda e, g2_=g2_: e.matmul(
                            ps96[g2_][:], wmaa1[:, g2_ * 96:g2_ * 96 + 96],
                            chs(xs, 0), start=True, stop=False),
                            after=tuple(mmxt), token=False)
                    else:
                        ph.op("tensor", lambda e, g2_=g2_, dc=dc: e.matmul(
                            ps96[g2_][:],
                            wmaa1[:, dc * 192 + g2_ * 96:
                                  dc * 192 + (g2_ + 1) * 96],
                            chs(xs, dc), start=False, stop=(dc == ND - 1)))
                tmm = ph.tok_last("tensor")
                for s3 in range(3):
                    s6 = g2_ * 3 + s3
                    par = (s6 % 2) * 32
                    blk_ = (s6 // 2) * SCN
                    mixt[s6] = ph.op("scalar", lambda e, g2_=g2_, s3=s3,
                                     par=par, blk_=blk_: e.activation(
                        mixbuf[par:par + 32, blk_:blk_ + SCN],
                        ps96[g2_][s3 * 32:(s3 + 1) * 32, :], AF.Tanh),
                        after=(tmm,), token=True)
                psg[bank] = mixt[g2_ * 3 + 2]

            xs_guard = None
            pd = {}
            for s6 in range(6):
                vname = "time_maa_" + SNAMES[s6]
                par = (s6 % 2) * 32
                blk_ = (s6 // 2) * SCN
                mslice = mixbuf[par:par + 32, blk_:blk_ + SCN]
                w2slice = lambda dc, par=par, s6=s6: wmaa2[
                    par:par + 32,
                    (s6 // 2) * D + dc * 128:(s6 // 2) * D + (dc + 1) * 128]
                xst = []
                for dc in range(ND):
                    b = dc % 2
                    bank = "A" + str(b)
                    tmm = MM(bank, lambda e, dc=dc, b=b, mslice=mslice,
                             w2s=w2slice(dc): e.matmul(
                        psA[b][:, 0:SCN], w2s, mslice, start=True, stop=True),
                        after=(mixt[s6], xs_guard))
                    tv = ph.op("vector", lambda e, b=b, dc=dc, vname=vname:
                               e.tensor_scalar(
                        tmpa[b][:], psA[b][:, 0:SCN],
                        vecs[vname][:, dc:dc + 1], None, op0=ALU.add),
                        after=(tmm, tmpg[b]), token=True)
                    psg[bank] = tv
                    ph.op("vector", lambda e, b=b, dc=dc: e.tensor_tensor(
                        tmpa[b][:], tmpa[b][:], chs(xx, dc), op=ALU.mult),
                        after=(tv,))
                    xst.append(ph.op("vector", lambda e, b=b, dc=dc:
                                     e.tensor_tensor(
                        chs(xs, dc), tmpa[b][:], chs(xch, dc, 1, SCN, 1),
                        op=ALU.add), token=True))
                    tmpg[b] = xst[-1]
                XST = tuple(xst)

                if s6 in (0, 2):
                    W_ = wr if s6 == 0 else wkk
                    dst = rT if s6 == 0 else ks
                    ev = [None, None]
                    for do in range(ND):
                        b = do % 2
                        bank = "A" + str(b)
                        for dc in range(ND):
                            if dc == 0:
                                MM(bank, lambda e, b=b, do=do, W_=W_:
                                   e.matmul(
                                    psA[b][:, 0:SCN],
                                    W_[:, do * 128:(do + 1) * 128],
                                    chs(xs, 0), start=True, stop=False),
                                    after=XST, token=False)
                            else:
                                ph.op("tensor", lambda e, b=b, dc=dc, do=do,
                                      W_=W_: e.matmul(
                                    psA[b][:, 0:SCN],
                                    W_[:, dc * D + do * 128:
                                       dc * D + (do + 1) * 128],
                                    chs(xs, dc), start=False,
                                    stop=(dc == ND - 1)))
                        tmm = ph.tok_last("tensor")
                        ev[b] = ph.op("vector", lambda e, b=b, do=do, dst=dst:
                                      e.tensor_copy(chs(dst, do),
                                                    psA[b][:, 0:SCN]),
                                      after=(tmm,), token=True)
                        psg[bank] = ev[b]
                    if s6 == 0:
                        pd[(0, 0)], pd[(0, 1)] = ev[0], ev[1]
                    else:
                        for dc in range(ND):
                            if dc == 0:
                                MM("A0", lambda e: e.matmul(
                                    psA[0][0:64, 0:SCN], wk1[:, 0:64],
                                    chs(xs, 0), start=True, stop=False),
                                    after=XST, token=False)
                            else:
                                ph.op("tensor", lambda e, dc=dc: e.matmul(
                                    psA[0][0:64, 0:SCN],
                                    wk1[:, dc * 64:(dc + 1) * 64],
                                    chs(xs, dc), start=False,
                                    stop=(dc == ND - 1)))
                        tmm = ph.tok_last("tensor")
                        tt1 = ph.op("scalar", lambda e: e.activation(
                            tnh[:], psA[0][0:64, 0:SCN], AF.Tanh),
                            after=(tmm,), token=True)
                        psg["A0"] = tt1
                        kkt = []
                        kst = []
                        for do in range(ND):
                            b = do % 2
                            bank = "B" + str(b)
                            tmm = MM(bank, lambda e, do=do, b=b: e.matmul(
                                psB[b][:, 0:SCN],
                                wk2[:, do * 128:(do + 1) * 128], tnh[:],
                                start=True, stop=True), after=(tt1,))
                            kkt.append(ph.op(
                                "vector", lambda e, do=do, b=b:
                                e.tensor_tensor(
                                    chs(kkn, do), psB[b][:, 0:SCN],
                                    chs(ks, do), op=ALU.add),
                                after=(tmm, ev[do % 2]), token=True))
                            psg[bank] = kkt[-1]
                            t1_ = ph.op("vector", lambda e, do=do, b=b:
                                        e.tensor_scalar(
                                tmpa[b][:], chs(wraw, do), 0.5, 0.0,
                                op0=ALU.mult, op1=ALU.min),
                                after=(pd[(1, "wr", do)], tmpg[b]),
                                token=True)
                            t2_ = ph.op("scalar", lambda e, b=b: e.activation(
                                tmpa[b][:], tmpa[b][:], AF.Exp),
                                after=(t1_,), token=True)
                            kst.append(ph.op("vector", lambda e, do=do, b=b:
                                             e.tensor_tensor(
                                chs(ks, do), chs(ks, do), tmpa[b][:],
                                op=ALU.mult),
                                after=(t2_, kkt[-1]), token=True))
                            tmpg[b] = kst[-1]
                        pd[(2, 0)] = kkt[-1]
                        pd[(2, 1)] = kst[-1]
                        pd[(2, "kk")] = tuple(kkt)
                        pd[(2, "ks")] = tuple(kst)
                elif s6 == 1:
                    for dc in range(ND):
                        if dc == 0:
                            MM("A0", lambda e: e.matmul(
                                psA[0][0:64, 0:SCN], wd1[:, 0:64],
                                chs(xs, 0), start=True, stop=False),
                                after=XST, token=False)
                        else:
                            ph.op("tensor", lambda e, dc=dc: e.matmul(
                                psA[0][0:64, 0:SCN],
                                wd1[:, dc * 64:(dc + 1) * 64],
                                chs(xs, dc), start=False,
                                stop=(dc == ND - 1)))
                    tmm = ph.tok_last("tensor")
                    tt1 = ph.op("scalar", lambda e: e.activation(
                        tnh[:], psA[0][0:64, 0:SCN], AF.Tanh), after=(tmm,),
                        token=True)
                    psg["A0"] = tt1
                    wrt = []
                    for do in range(ND):
                        b = do % 2
                        bank = "B" + str(b)
                        tmm = MM(bank, lambda e, do=do, b=b: e.matmul(
                            psB[b][:, 0:SCN],
                            wd2[:, do * 128:(do + 1) * 128], tnh[:],
                            start=True, stop=True), after=(tt1,))
                        tz = ph.op("vector", lambda e, b=b, do=do:
                                   e.tensor_scalar(
                            tmpa[b][:], psB[b][:, 0:SCN],
                            vecs["time_decay"][:, do:do + 1], None,
                            op0=ALU.add), after=(tmm, tmpg[b]), token=True)
                        psg[bank] = tz
                        ta1 = ph.op("scalar", lambda e, b=b: e.activation(
                            tmpa[2][:], tmpa[b][:], AF.Abs),
                            after=(tmpg[2], tz), token=True)
                        ph.op("scalar", lambda e: e.activation(
                            tmpa[2][:], tmpa[2][:], AF.Exp, scale=-1.0))
                        ta = ph.op("scalar", lambda e: e.activation(
                            tmpa[2][:], tmpa[2][:], AF.Ln, bias=1.0),
                            token=True)
                        ph.op("vector", lambda e, b=b: e.tensor_scalar(
                            tmpa[b][:], tmpa[b][:], -1.0, 0.0,
                            op0=ALU.mult, op1=ALU.max), after=(tz, ta1))
                        tadd = ph.op("vector", lambda e, b=b:
                                     e.tensor_tensor(
                            tmpa[b][:], tmpa[b][:], tmpa[2][:], op=ALU.add),
                            after=(ta,), token=True)
                        tmpg[2] = tadd
                        wrt.append(ph.op("vector", lambda e, b=b, do=do:
                                         e.tensor_scalar(
                            chs(wraw, do), tmpa[b][:], -1.0, -0.5,
                            op0=ALU.mult, op1=ALU.add), token=True))
                        pd[(1, "wr", do)] = wrt[-1]
                        tmpg[b] = wrt[-1]
                    pd[(1, 0)] = wrt[-2]
                    pd[(1, 1)] = wrt[-1]
                elif s6 == 3:
                    vt = []
                    for fo in range(2):
                        b = fo % 2
                        bank = "A" + str(b)
                        for dc in range(ND):
                            if dc == 0:
                                MM(bank, lambda e, b=b, fo=fo: e.matmul(
                                    psA[b][:], chs(xs, 0),
                                    wv[:, fo * 512:(fo + 1) * 512],
                                    start=True, stop=False),
                                    after=XST, token=False)
                            else:
                                ph.op("tensor", lambda e, b=b, dc=dc, fo=fo:
                                      e.matmul(
                                    psA[b][:], chs(xs, dc),
                                    wv[:, dc * D + fo * 512:
                                       dc * D + (fo + 1) * 512],
                                    start=False, stop=(dc == ND - 1)))
                        tmm = ph.tok_last("tensor")
                        vt.append(ph.op(
                            "vector", lambda e, fo=fo, b=b: e.tensor_copy(
                                vtok[:, fo * 512:(fo + 1) * 512], psA[b][:]),
                            after=(tmm,), token=True))
                        psg[bank] = vt[-1]
                    pd[(3, 0)] = vt[-2]
                    pd[(3, 1)] = vt[-1]
                elif s6 == 4:
                    for dc in range(ND):
                        if dc == 0:
                            MM("A0", lambda e: e.matmul(
                                psA[0][0:64, 0:SCN], wa1[:, 0:64],
                                chs(xs, 0), start=True, stop=False),
                                after=XST, token=False)
                        else:
                            ph.op("tensor", lambda e, dc=dc: e.matmul(
                                psA[0][0:64, 0:SCN],
                                wa1[:, dc * 64:(dc + 1) * 64],
                                chs(xs, dc), start=False,
                                stop=(dc == ND - 1)))
                    tmm = ph.tok_last("tensor")
                    tt1 = ph.op("scalar", lambda e: e.activation(
                        tnh[:], psA[0][0:64, 0:SCN], AF.Copy), after=(tmm,),
                        token=True)
                    psg["A0"] = tt1
                    agt = []
                    for do in range(ND):
                        b = do % 2
                        bank = "B" + str(b)
                        tmm = MM(bank, lambda e, do=do, b=b: e.matmul(
                            psB[b][:, 0:SCN],
                            wa2[:, do * 128:(do + 1) * 128], tnh[:],
                            start=True, stop=True), after=(tt1,))
                        ts_ = ph.op("scalar", lambda e, b=b, do=do:
                                    e.activation(
                            chs(ag, do), psB[b][:, 0:SCN], AF.Sigmoid,
                            bias=vecs["time_aaaaa"][:, do:do + 1]),
                            after=(tmm,), token=True)
                        psg[bank] = ts_
                        agt.append(ph.op("scalar", lambda e, do=do: e.mul(
                            chs(ag, do), chs(ag, do), 2.0), token=True))
                    pd[(4, 0)] = agt[-2]
                    pd[(4, 1)] = agt[-1]
                else:
                    for dc in range(ND):
                        if dc == 0:
                            MM("A0", lambda e: e.matmul(
                                psA[0][:, 0:SCN], wg1[:, 0:128],
                                chs(xs, 0), start=True, stop=False),
                                after=XST, token=False)
                        else:
                            ph.op("tensor", lambda e, dc=dc: e.matmul(
                                psA[0][:, 0:SCN],
                                wg1[:, dc * 128:(dc + 1) * 128],
                                chs(xs, dc), start=False,
                                stop=(dc == ND - 1)))
                    tmm = ph.tok_last("tensor")
                    tt1 = ph.op("scalar", lambda e: e.activation(
                        tnh128[:], psA[0][:, 0:SCN], AF.Tanh), after=(tmm,),
                        token=True)
                    psg["A0"] = tt1
                    ggt = []
                    for do in range(ND):
                        b = do % 2
                        bank = "B" + str(b)
                        tmm = MM(bank, lambda e, do=do, b=b: e.matmul(
                            psB[b][:, 0:SCN],
                            wg2[:, do * 128:(do + 1) * 128], tnh128[:],
                            start=True, stop=True), after=(tt1,))
                        ggt.append(ph.op("vector", lambda e, b=b, do=do:
                                         e.tensor_copy(
                            chs(gT, do), psB[b][:, 0:SCN]), after=(tmm,),
                            token=True))
                        psg[bank] = ggt[-1]
                    pd[(5, 0)] = ggt[-2]
                    pd[(5, 1)] = ggt[-1]
                xs_guard = pd[(s6, 1)]

            kknt = []
            for co in range(CPS):
                fw = []
                for dc in range(ND):
                    b = dc % 2
                    bank = "B" + str(b)
                    ttr = MM(bank, lambda e, dc=dc, co=co, b=b: e.transpose(
                        psB[b][:], chs(kkn, dc)[:, co * 128:(co + 1) * 128],
                        g["idf"][:]), after=pd[(2, "kk")] + pd[(2, "ks")])
                    tcp = ph.op("vector", lambda e, b=b, dc=dc, co=co:
                                e.tensor_copy(
                        kkn_tok[:, co * D + dc * 128:co * D + (dc + 1) * 128],
                        psB[b][:]), after=(ttr,) + CG, token=True)
                    psg[bank] = tcp
                    fw.append(tcp)
                tq = ph.op("scalar", lambda e, co=co: e.activation(
                    ygb[:], kkn_tok[:, co * D:(co + 1) * D], AF.Square,
                    accum_out=rsg[:]), after=tuple(fw) + CG, token=True)
                tq = ph.op("scalar", lambda e: e.activation(
                    rsg[:], rsg[:], AF.Ln), after=(tq,), force=True)
                tq = ph.op("scalar", lambda e: e.activation(
                    rsg[:], rsg[:], AF.Exp, scale=-0.5), token=True)
                tq2 = ph.op("vector", lambda e: e.tensor_scalar(
                    rsg[:], rsg[:], 1e12, None, op0=ALU.min), after=(tq,))
                tq2 = ph.op("vector", lambda e, co=co: e.tensor_scalar(
                    kkn_tok[:, co * D:(co + 1) * D],
                    kkn_tok[:, co * D:(co + 1) * D], rsg[:], None,
                    op0=ALU.mult), token=True)
                for dc in range(ND):
                    b = dc % 2
                    bank = "B" + str(b)
                    ttr = MM(bank, lambda e, b=b, dc=dc, co=co: e.transpose(
                        psB[b][:],
                        kkn_tok[:, co * D + dc * 128:co * D + (dc + 1) * 128],
                        g["idf"][:]), after=(tq2,))
                    tcp = ph.op("vector", lambda e, b=b, dc=dc, co=co:
                                e.tensor_copy(
                        chs(kkn, dc)[:, co * 128:(co + 1) * 128], psB[b][:]),
                        after=(ttr,), token=True)
                    psg[bank] = tcp
                    kknt.append(tcp)
            KKNT = tuple(kknt)

            for co in range(CPS):
                ssb_t = ph.op("scalar", lambda e: e.copy(SSb[:], SS[:]),
                              after=(ssb_prev,) + CG, token=True)
                pcht = []
                for dc in range(ND):
                    pcht.append(ph.op("scalar", lambda e, dc=dc, co=co:
                                      e.activation(
                        p_ch[:, dc * 128:(dc + 1) * 128],
                        chs(wraw, dc)[:, co * 128:(co + 1) * 128],
                        AF.Exp), after=(pd[(1, 0)], pd[(1, 1)]) + CG,
                        token=True))
                ptt = []
                trg2 = [None, None]
                for dc in range(ND):
                    b = dc % 2
                    bank = "B" + str(b)
                    ttr = MM(bank, lambda e, b=b, dc=dc: e.transpose(
                        psB[b][:], p_ch[:, dc * 128:(dc + 1) * 128],
                        g["idf"][:]), after=(pcht[dc],))
                    tcp = ph.op("vector", lambda e, b=b, dc=dc:
                                e.tensor_copy(
                        p_tok[:, dc * 128:(dc + 1) * 128], psB[b][:]),
                        after=(ttr,) + CG, token=True)
                    psg[bank] = tcp
                    ptt.append(tcp)
                Ptoks = []
                for hf in range(2):
                    bank = "A" + str(hf)
                    tmm = MM(bank, lambda e, hf=hf: e.matmul(
                        psA[hf][:], g["tri_incl"][:],
                        p_tok[:, hf * 512:(hf + 1) * 512],
                        start=True, stop=True), after=tuple(ptt))
                    tcp = ph.op("vector", lambda e, hf=hf: e.tensor_copy(
                        P_tok[:, hf * 512:(hf + 1) * 512], psA[hf][:]),
                        after=(tmm,) + CG, token=True)
                    psg[bank] = tcp
                    Ptoks.append(tcp)
                e2t_t = ph.op("scalar", lambda e: e.activation(
                    E2t[:], P_tok[:], AF.Exp), after=tuple(Ptoks) + CG,
                    token=True)
                pchT = []
                for dc in range(ND):
                    b = dc % 2
                    bank = "B" + str(b)
                    sl = slice(dc * 128, (dc + 1) * 128)
                    tmm = MM(bank, lambda e, b=b, dc=dc: e.matmul(
                        psB[b][:], p_tok[:, dc * 128:(dc + 1) * 128],
                        g["tri_incl"][:], start=True, stop=True),
                        after=tuple(ptt) + tuple(Ptoks) + (e2t_t,))
                    ph.op("scalar", lambda e, b=b, sl=sl: e.activation(
                        E0[:, sl], psB[b][:], AF.Exp, scale=-1.0),
                        after=(tmm,) + CG)
                    ph.op("scalar", lambda e, b=b, sl=sl: e.activation(
                        E2[:, sl], psB[b][:], AF.Exp), token=False)
                    t3 = ph.op("scalar", lambda e, b=b, sl=sl: e.activation(
                        E1[:, sl], psB[b][:], AF.Copy, accum_out=None),
                        token=True)
                    tcp = ph.op("vector", lambda e, b=b, sl=sl:
                                e.tensor_tensor(
                        E1[:, sl], E1[:, sl], p_ch[:, sl], op=ALU.subtract),
                        after=(t3, pcht[dc]) + CG, token=True)
                    psg[bank] = tcp
                    pchT.append(tcp)
                t_e0 = ph.op("scalar", lambda e: e.nop(),
                             after=tuple(pchT), token=True)
                t_e2 = t_e0
                t_e1 = ph.op("scalar", lambda e: e.activation(
                    E1[:], E1[:], AF.Exp, scale=-1.0),
                    after=tuple(pchT) + CG, token=True)
                glt = []
                for dc in range(ND):
                    glt.append(ph.op("vector", lambda e, dc=dc:
                                     e.tensor_copy(
                        gl[:, dc:dc + 1],
                        E0[:, dc * 128 + 127:dc * 128 + 128]),
                        after=(t_e0,), token=True))
                t_ab = ph.op("vector", lambda e: e.scalar_tensor_tensor(
                    dview(Ab), dview(kkn), -1.0, dview(E1),
                    op0=ALU.mult, op1=ALU.mult), after=KKNT + (t_e1,) + CG,
                    token=True)
                t_rb = ph.op("vector", lambda e: e.tensor_tensor(
                    dview(Rb), dview(rT), dview(E0), op=ALU.mult),
                    after=(t_e0, pd[(0, 0)], pd[(0, 1)]) + CG, token=True)
                t_kcb = ph.op("vector", lambda e: e.tensor_tensor(
                    dview(Kcb), dview(ks), dview(E2), op=ALU.mult),
                    after=(t_e2,) + pd[(2, "ks")] + CG, token=True)
                t_tk = ph.op("vector", lambda e: e.tensor_tensor(
                    dview(p_tok), dview(kkn), dview(ag), op=ALU.mult),
                    after=KKNT + (t_e1, pd[(4, 0)], pd[(4, 1)]), token=True)
                t_bcb = ph.op("vector", lambda e: e.tensor_tensor(
                    Bcb[:], p_tok[:], E2[:], op=ALU.mult),
                    after=(t_tk, t_e2) + CG, token=True)
                t_vb = ph.op("scalar", lambda e, co=co: e.copy(
                    Vb[:], vtok[:, co * D:(co + 1) * D]),
                    after=(pd[(3, 0)], pd[(3, 1)]) + CG, token=True)
                ktbt = []
                for dc in range(ND):
                    b = dc % 2
                    bank = "B" + str(b)
                    ttr = MM(bank, lambda e, b=b, dc=dc, co=co: e.transpose(
                        psB[b][:], chs(ks, dc)[:, co * 128:(co + 1) * 128],
                        g["idf"][:]), after=pd[(2, "ks")])
                    tcp = ph.op("vector", lambda e, b=b, dc=dc:
                                e.tensor_tensor(
                        Ktb[:, dc * 128:(dc + 1) * 128], psB[b][:],
                        E2t[:, dc * 128:(dc + 1) * 128], op=ALU.mult),
                        after=(ttr, e2t_t) + CG, token=True)
                    psg[bank] = tcp
                    ktbt.append(tcp)
                btbt = []
                trg2 = [None, None]
                for dc in range(ND):
                    b = dc % 2
                    bank = "B" + str(b)
                    ttr = MM(bank, lambda e, b=b, dc=dc, co=co: e.transpose(
                        psB[b][:], chs(ag, dc)[:, co * 128:(co + 1) * 128],
                        g["idf"][:]), after=(pd[(4, 0)], pd[(4, 1)]))
                    tv_ = ph.op("vector", lambda e, b=b, dc=dc:
                                e.tensor_tensor(
                        E1[:, dc * 128:(dc + 1) * 128], psB[b][:],
                        E2t[:, dc * 128:(dc + 1) * 128], op=ALU.mult),
                        after=(ttr, e2t_t, t_ab, t_bcb), token=True)
                    psg[bank] = tv_
                    btbt.append(ph.op("vector", lambda e, dc=dc, co=co:
                                      e.tensor_tensor(
                        Btb[:, dc * 128:(dc + 1) * 128],
                        E1[:, dc * 128:(dc + 1) * 128],
                        kkn_tok[:, co * D + dc * 128:co * D + (dc + 1) * 128],
                        op=ALU.mult), after=(tv_,) + CG, token=True))
                ARR = (t_ab, t_rb, t_kcb, t_bcb, t_vb, ssb_t) + tuple(ktbt) \
                    + tuple(btbt) + tuple(glt)

                ssdone = []
                for h in range(H):
                    off = (h % 2) * 64
                    dct = h // 2
                    aKc = Kcb[off:off + 64, dct * 128:(dct + 1) * 128]
                    aAb = Ab[off:off + 64, dct * 128:(dct + 1) * 128]
                    aBc = Bcb[off:off + 64, dct * 128:(dct + 1) * 128]
                    aRb = Rb[off:off + 64, dct * 128:(dct + 1) * 128]
                    aSSb = SSb[off:off + 64, dct * 64:(dct + 1) * 64]
                    aSS = SS[off:off + 64, dct * 64:(dct + 1) * 64]
                    aVb = Vb[:, h * 64:(h + 1) * 64]
                    aKt = Ktb[:, h * 64:(h + 1) * 64]
                    aBt = Btb[:, h * 64:(h + 1) * 64]
                    agl = gl[off:off + 64, dct:dct + 1]

                    tg = MM("B0", lambda e, aKc=aKc, aAb=aAb: e.matmul(
                        psB[0][:], aKc, aAb, start=True, stop=True),
                        after=ARR + tuple(ssdone[-2:]))
                    tge = ph.op("vector", lambda e: e.tensor_tensor(
                        Gt[:], psB[0][:], g["mask_su"][:], op=ALU.mult),
                        after=(tg,), token=True)
                    psg["B0"] = tge
                    tu = MM("B1", lambda e, aBc=aBc, aAb=aAb: e.matmul(
                        psB[1][:], aBc, aAb, start=True, stop=True),
                        after=ARR)
                    tue = ph.op("vector", lambda e: e.tensor_tensor(
                        Ut[0][:], psB[1][:], g["mask_su"][:], op=ALU.mult),
                        after=(tu,), token=True)
                    psg["B1"] = tue
                    tm_ = MM("B0", lambda e, aAb=aAb, aBc=aBc: e.matmul(
                        psB[0][:], aAb, aBc, start=True, stop=True))
                    tme = ph.op("vector", lambda e: e.tensor_tensor(
                        Mt[0][:], psB[0][:], g["mask_sl"][:], op=ALU.mult),
                        after=(tm_,), token=True)
                    psg["B0"] = tme
                    pe = {0: (tue, tme)}
                    for kp in range(1, 7):
                        tu_ = MM("B0", lambda e, kp=kp: e.matmul(
                            psB[0][:], Mt[kp - 1][:], Ut[kp - 1][:],
                            start=True, stop=True), after=pe[kp - 1])
                        ue = ph.op("vector", lambda e, kp=kp: e.tensor_copy(
                            Ut[kp][:], psB[0][:]), after=(tu_,), token=True)
                        psg["B0"] = ue
                        tm2 = MM("B1", lambda e, kp=kp: e.matmul(
                            psB[1][:], Ut[kp - 1][:], Mt[kp - 1][:],
                            start=True, stop=True))
                        me = ph.op("scalar", lambda e, kp=kp: e.copy(
                            Mt[kp][:], psB[1][:]), after=(tm2,), token=True)
                        psg["B1"] = me
                        pe[kp] = (ue, me)
                    tx = MM("X", lambda e, aAb=aAb, aSSb=aSSb: e.matmul(
                        pX[:, 0:64], aAb, aSSb, start=True, stop=False),
                        after=(ssb_t, tge))
                    ph.op("tensor", lambda e, aVb=aVb: e.matmul(
                        pX[:, 0:64], Gt[:], aVb, start=False, stop=True),
                        after=(t_vb,))
                    tnx = ph.tok_last("tensor")
                    txe = ph.op("scalar", lambda e: e.copy(Xb[:],
                                                           pX[:, 0:64]),
                                after=(tnx,), token=True)
                    psg["X"] = txe
                    for kp in range(7):
                        MM("X", lambda e, kp=kp: e.matmul(
                            pX[:, 0:64], Mt[kp][:], Xb[:], start=True,
                            stop=False), after=(txe, pe[kp][1]), token=False)
                        ph.op("tensor", lambda e: e.matmul(
                            pX[:, 0:64], g["idb"][:], Xb[:], start=False,
                            stop=True))
                        tnp = ph.tok_last("tensor")
                        txe = ph.op("scalar", lambda e: e.copy(
                            Xb[:], pX[:, 0:64]), after=(tnp,), token=True)
                        psg["X"] = txe
                    tw = MM("B0", lambda e, aKc=aKc, aRb=aRb: e.matmul(
                        psB[0][:], aKc, aRb, start=True, stop=True),
                        after=(t_rb, pe[6][0]))
                    twe = ph.op("vector", lambda e: e.tensor_tensor(
                        Wt[:], psB[0][:], g["mask_iu"][:], op=ALU.mult),
                        after=(tw,), token=True)
                    psg["B0"] = twe
                    tw2 = MM("B1", lambda e, aBc=aBc, aRb=aRb: e.matmul(
                        psB[1][:], aBc, aRb, start=True, stop=True),
                        after=(pe[6][1],))
                    tw2e = ph.op("vector", lambda e: e.tensor_tensor(
                        W2t[:], psB[1][:], g["mask_iu"][:], op=ALU.mult),
                        after=(tw2,), token=True)
                    psg["B1"] = tw2e
                    MM("X", lambda e, aRb=aRb, aSSb=aSSb: e.matmul(
                        pX[:, 0:64], aRb, aSSb, start=True, stop=False),
                        after=(txe, twe, tw2e), token=False)
                    ph.op("tensor", lambda e, aVb=aVb: e.matmul(
                        pX[:, 0:64], Wt[:], aVb, start=False, stop=False))
                    ph.op("tensor", lambda e: e.matmul(
                        pX[:, 0:64], W2t[:], Xb[:], start=False, stop=True))
                    tny = ph.tok_last("tensor")
                    tye = ph.op("scalar", lambda e, co=co, h=h: e.copy(
                        y_sc[:, co * D + h * 64:co * D + (h + 1) * 64],
                        pX[:, 0:64]), after=(tny,) + CG, token=True)
                    psg["X"] = tye
                    MM("B0", lambda e, aKt=aKt, aVb=aVb: e.matmul(
                        psB[0][0:64, 0:64], aKt, aVb, start=True,
                        stop=False), after=(tye,), token=False)
                    ph.op("tensor", lambda e, aBt=aBt: e.matmul(
                        psB[0][0:64, 0:64], aBt, Xb[:], start=False,
                        stop=True))
                    tns = ph.tok_last("tensor")
                    tse = ph.op("vector", lambda e, aSS=aSS: e.tensor_tensor(
                        aSS, psB[0][0:64, 0:64], aSS, op=ALU.add),
                        after=(tns,), token=True)
                    tse = ph.op("vector", lambda e, aSS=aSS, agl=agl:
                                e.tensor_scalar(
                        aSS, aSS, agl, None, op0=ALU.mult), token=True)
                    psg["B0"] = tse
                    ssdone.append(tse)
                SSD = tuple(ssdone)
                ssb_prev = SSD[-1]

                gtk = []
                for hh in range(H):
                    ys = y_sc[:, co * D + hh * 64:co * D + (hh + 1) * 64]
                    t1_ = ph.op("vector", lambda e, ys=ys: e.bn_stats(
                        mvg[:, 0:6], ys), after=SSD, token=True)
                    t1_ = ph.op("vector", lambda e: e.bn_aggr(
                        mvg[:, 6:8], mvg[:, 0:6]), after=(t1_,), force=True,
                        token=True)
                    t2_ = ph.op("scalar", lambda e: e.activation(
                        rsg[:], mvg[:, 7:8], AF.Ln, bias=GN_EPS),
                        after=(t1_,))
                    t2_ = ph.op("scalar", lambda e: e.activation(
                        rsg[:], rsg[:], AF.Exp, scale=-0.5), token=True)
                    gtk.append(ph.op("vector", lambda e, ys=ys:
                                     e.tensor_scalar(
                        ys, ys, mvg[:, 6:7], rsg[:], op0=ALU.subtract,
                        op1=ALU.mult), after=(t2_, t1_), force=True,
                        token=True))
                ysl = y_sc[:, co * D:(co + 1) * D]
                tgn = ph.op("vector", lambda e, ysl=ysl: e.tensor_tensor(
                    ysl, ysl, lnw[:], op=ALU.mult), after=tuple(gtk))
                tgn = ph.op("vector", lambda e, ysl=ysl: e.tensor_tensor(
                    ysl, ysl, lnb[:], op=ALU.add), token=True)
                t1_ = ph.op("vector", lambda e: e.tensor_tensor(
                    dview(E1), dview(rT), dview(ks), op=ALU.mult),
                    after=(pd[(0, 1)],) + pd[(2, "ks")] + tuple(btbt) +
                    (t_ab,), token=True)
                rft = []
                for dc in range(ND):
                    rft.append(ph.op("vector", lambda e, dc=dc:
                                     e.tensor_scalar(
                        Ab[:, dc * 128:(dc + 1) * 128],
                        E1[:, dc * 128:(dc + 1) * 128],
                        vecs["faaaa"][:, dc:dc + 1], None, op0=ALU.mult),
                        after=(t1_, SSD[-1]), token=True))
                for dc in range(ND):
                    if dc == 0:
                        MM("X", lambda e: e.matmul(
                            pX[:, 0:16], Ab[:, 0:128], hb[:, 0:H],
                            start=True, stop=False), after=tuple(rft),
                            token=False)
                    else:
                        ph.op("tensor", lambda e, dc=dc: e.matmul(
                            pX[:, 0:16], Ab[:, dc * 128:(dc + 1) * 128],
                            hb[:, dc * H:(dc + 1) * H], start=False,
                            stop=(dc == ND - 1)))
                tnb = ph.tok_last("tensor")
                tbs = ph.op("vector", lambda e: e.tensor_copy(
                    bscal[:], pX[:, 0:16]), after=(tnb,), token=True)
                psg["X"] = tbs
                ytk = []
                for hh in range(H):
                    ytk.append(ph.op("vector", lambda e, hh=hh, co=co:
                                     e.scalar_tensor_tensor(
                        y_sc[:, co * D + hh * 64:co * D + (hh + 1) * 64],
                        vtok[:, co * D + hh * 64:co * D + (hh + 1) * 64],
                        bscal[:, hh:hh + 1],
                        y_sc[:, co * D + hh * 64:co * D + (hh + 1) * 64],
                        op0=ALU.mult, op1=ALU.add), after=(tbs, tgn),
                        token=True))
                t_tapy = None
                if tap_y is not None:
                    t_tapy = ph.dma("gpsimd", tap_y[ci * 128:(ci + 1) * 128, :],
                                    ysl, after=tuple(ytk), stream="tap")
                ygt = []
                trg3 = [None, None]
                for dc in range(ND):
                    b = dc % 2
                    bank = "B" + str(b)
                    ttr = MM(bank, lambda e, b=b, dc=dc, co=co: e.transpose(
                        psB[b][:],
                        y_sc[:, co * D + dc * 128:co * D + (dc + 1) * 128],
                        g["idf"][:]), after=tuple(ytk))
                    tcp = ph.op("vector", lambda e, b=b, dc=dc, co=co:
                                e.tensor_tensor(
                        ygb[:, dc * 128:(dc + 1) * 128], psB[b][:],
                        chs(gT, dc)[:, co * 128:(co + 1) * 128],
                        op=ALU.mult),
                        after=(ttr, pd[(5, 0)], pd[(5, 1)]), token=True)
                    psg[bank] = tcp
                    ygt.append(tcp)
                YGT = tuple(ygt)
                attk = []
                for fo in range(2):
                    b = fo % 2
                    bank = "A" + str(b)
                    for dc in range(ND):
                        if dc == 0:
                            MM(bank, lambda e, fo=fo, b=b: e.matmul(
                                psA[b][:], ygb[:, 0:128],
                                wo[:, fo * 512:(fo + 1) * 512],
                                start=True, stop=False),
                                after=YGT + (prev_sc,), token=False)
                        else:
                            ph.op("tensor", lambda e, b=b, dc=dc, fo=fo:
                                  e.matmul(
                                psA[b][:], ygb[:, dc * 128:(dc + 1) * 128],
                                wo[:, dc * D + fo * 512:
                                   dc * D + (fo + 1) * 512],
                                start=False, stop=(dc == ND - 1)))
                    tnm = ph.tok_last("tensor")
                    tcp = ph.op("vector", lambda e, b=b, fo=fo, co=co:
                                e.tensor_copy(
                        y_sc[:, co * D + fo * 512:co * D + (fo + 1) * 512],
                        psA[b][:]), after=(tnm, prev_sc, t_tapy), token=True)
                    psg[bank] = tcp
                    attk.append(tcp)
                std = ph.dma("gpsimd",
                             dram["att1"][ci * 128:(ci + 1) * 128, :],
                             ysl, after=tuple(attk), stream="at")
                if tap_att1 is not None:
                    std = ph.dma("gpsimd", tap_att1[ci * 128:(ci + 1) * 128, :],
                                 ysl, after=tuple(attk), stream="at")
                prev_sc = std
                CG = (SSD[-1], std, tbs, tgn, attk[-1])
        ph.emit()


def kernel(**inputs):
    shared, percore = prep_inputs(inputs)
    cn = make_consts()
    base = {("c_" + k): np.asarray(v) for k, v in cn.items()}
    base.update(shared)
    nc = build_program()
    in_maps = []
    for b in range(8):
        m = dict(base)
        m["x"] = percore[b]
        in_maps.append(m)
    res = run_bass_kernel_spmd(nc, in_maps, core_ids=list(range(8)))
    return np.stack([r["out"] for r in res.results], axis=0)


# version 03
# speedup vs baseline: 1.0857x; 1.0857x over previous
import contextlib
import numpy as np
import ml_dtypes
import concourse.bass as bass
import concourse.mybir as mybir
from concourse.bass_utils import run_bass_kernel_spmd

F32 = mybir.dt.float32
BF16 = mybir.dt.bfloat16
AF = mybir.ActivationFunctionType
ALU = mybir.AluOpType
AX = mybir.AxisListType

T = 2048
D = 1024
NT = T // 128
ND = D // 128
H = 16
FF = 4096
NE = 8
LCH = 128
SCN = 128
NSC = T // SCN
NSL = 4
CPS = SCN // LCH
GN_EPS = 1e-5 * 64.0

ENGINES = ["tensor", "vector", "scalar", "gpsimd", "sync"]

DEBUG_TAPS = ()
UPTO = 99
SIM_SAFE = False
CHAIN_ENGINES = ("vector", "scalar", "gpsimd")
QKV_SUB = 99
QKV_NW = 2
QKV_NDO = 8


def _bf(x):
    return np.ascontiguousarray(np.asarray(x).astype(ml_dtypes.bfloat16))



class Sem:
    def __init__(self, handle, name):
        self.h = handle
        self.name = name
        self.n = 0

    def inc(self, k=1):
        self.n += k
        return self.n


class Ph:
    def __init__(self, p, name):
        self.p = p
        self.name = name
        self.items = {e: [] for e in ENGINES}
        self._laststream = {}

    def _sem(self, eng, dma):
        return self.p.sem(("d_" if dma else "e_") + eng)

    def op(self, eng, fn, after=(), token=False, force=False):
        sem = self._sem(eng, False)
        tok = None
        if token or SIM_SAFE or eng in CHAIN_ENGINES:
            tok = (sem, sem.n + 1, eng, False)
            sem.inc(1)
        self.items[eng].append(
            (fn, tuple(a for a in after if a), tok, False, force))
        return tok

    def tok_last(self, eng):
        items = self.items[eng]
        fn, after, tok0, isdma, force = items[-1]
        if tok0 is not None:
            return tok0
        sem = self._sem(eng, False)
        tok = (sem, sem.n + 1, eng, False)
        sem.inc(1)
        items[-1] = (fn, after, tok, isdma, force)
        return tok

    def dma(self, eng, out, in_, after=(), transpose=False, stream="m"):
        sem = self.p.sem("d_" + eng + "_" + stream)
        if SIM_SAFE:
            after = tuple(after) + (self._laststream.get(sem.name),)
        tok = (sem, sem.n + 16, eng, True)
        sem.inc(16)
        self._laststream[sem.name] = tok

        def fn(e, out=out, in_=in_, tr=transpose):
            if tr:
                return e.dma_start(out, in_, transpose=True)
            return e.dma_start(out, in_)

        self.items[eng].append(
            (fn, tuple(a for a in after if a), tok, True, False))
        return tok

    def emit(self):
        nc = self.p.nc
        active = [e for e in ENGINES if self.items[e]]
        with nc.Block(self.name) as blk:
            for eng in active:
                items = self.items[eng]

                def body(e, eng=eng, items=items):
                    water = {}
                    last_dma = {}
                    prev_tok = None
                    chain = SIM_SAFE or eng in CHAIN_ENGINES
                    for fn, after, tok, isdma, force in items:
                        if chain and prev_tok is not None:
                            e.wait_ge(prev_tok[0].h, prev_tok[1])
                        need = {}
                        for (s, v, peng, pdma) in after:
                            if peng == eng and not pdma and not force:
                                continue
                            if water.get(s.name, -1) >= v:
                                continue
                            if need.get(s.name, (None, -1))[1] < v:
                                need[s.name] = (s, v)
                        for s, v in need.values():
                            e.wait_ge(s.h, v)
                            water[s.name] = v
                        inst = fn(e)
                        if tok is not None:
                            inst.then_inc(tok[0].h, 16 if isdma else 1)
                            if isdma:
                                last_dma[tok[0].name] = tok
                            else:
                                prev_tok = tok
                    for s, v, _, _ in last_dma.values():
                        if water.get(s.name, -1) < v:
                            e.wait_ge(s.h, v)
                            water[s.name] = v

                getattr(blk, eng)(body)


class P:
    def __init__(self, nc, stack):
        self.nc = nc
        self.stack = stack
        self.sems = {}
        self.dram = {}
        self.taps = {}

    def sem(self, name):
        if name not in self.sems:
            self.sems[name] = Sem(
                self.stack.enter_context(self.nc.semaphore(name)), name)
        return self.sems[name]

    def sb(self, name, shape, dtype, stack=None):
        return (stack or self.stack).enter_context(
            self.nc.sbuf_tensor(name, shape, dtype))

    def ps(self, name, shape, stack=None, dtype=F32):
        return (stack or self.stack).enter_context(
            self.nc.psum_tensor(name, shape, dtype))

    def tap(self, name, shape, dtype=F32):
        if name in DEBUG_TAPS and name not in self.taps:
            self.taps[name] = self.nc.dram_tensor(
                "tap_" + name, list(shape), dtype, kind="ExternalOutput").ap()
        return self.taps.get(name)



def make_consts():
    c = {}
    c["ident_f32"] = np.eye(128, dtype=np.float32)
    c["ident_bf16"] = _bf(np.eye(128))
    c["ones_1x128"] = np.ones((1, 128), dtype=np.float32)
    c["onescol_bf"] = _bf(np.ones((128, 1)))
    for off in range(4):
        pk = off * 128 + np.arange(128)[:, None]
        fq = np.arange(512)[None, :]
        c[f"cmask{off}"] = _bf((pk <= fq).astype(np.float32))
    tt = np.arange(LCH)
    c["tri_incl"] = (tt[:, None] <= tt[None, :]).astype(np.float32)
    c["mask_su"] = (tt[:, None] < tt[None, :]).astype(np.float32)
    c["mask_iu"] = (tt[:, None] <= tt[None, :]).astype(np.float32)
    c["mask_sl"] = (tt[:, None] > tt[None, :]).astype(np.float32)
    blk = np.zeros((ND, 128, H), dtype=np.float32)
    for dc in range(ND):
        for j in range(2):
            blk[dc, j * 64:(j + 1) * 64, 2 * dc + j] = 1.0
    c["headblk_bf"] = _bf(blk.transpose(1, 0, 2).reshape(128, ND * H))
    c["fvals"] = np.tile(np.array(FCONSTS, dtype=np.float32), (128, 1))
    return c


FCONSTS = [1e-5, GN_EPS, 1.0, -1.0, 0.5, -0.5, 1e12, -1e30, float(D), 2.0,
           -2.0, 1.0 / 32.0, -1.0 / 32.0, 0.0, 3.0, -3e38]


def prep_inputs(inputs):
    g = {}
    x = np.asarray(inputs["L0_attn_x"], dtype=np.float32)
    percore = [np.ascontiguousarray(x[b]) for b in range(8)]

    def A(name, arr):
        g[name] = np.ascontiguousarray(np.asarray(arr, dtype=np.float32))

    for w in ["Wq", "Wk", "Wv"]:
        W = np.asarray(inputs[f"L0_ffn_{w}"], dtype=np.float32)
        A(f"l0_{w.lower()}", W.transpose(1, 0, 2).reshape(D, D))
    A("l0_gate", np.asarray(inputs["L0_ffn_gate_W"], dtype=np.float32)
      .reshape(ND, 128, NE).transpose(1, 0, 2).reshape(128, ND * NE))
    A("l0_w1", inputs["L0_ffn_W1"])
    A("l0_w2", inputs["L0_ffn_W2"])
    b1 = np.asarray(inputs["L0_ffn_b1"], dtype=np.float32)
    A("l0_b1", b1.reshape(NE, 32, 128).transpose(2, 0, 1).reshape(128, NE * 32))
    A("l0_b2", inputs["L0_ffn_b2"])
    for v in ["ln1_g", "ln1_b", "ln2_g", "ln2_b"]:
        A(f"l0_{v}", np.asarray(inputs[f"L0_ffn_{v}"]).reshape(1, D))

    def chvec(name, arr):
        v = np.asarray(arr, dtype=np.float32).reshape(D)
        g[name] = np.ascontiguousarray(v.reshape(ND, 128).T)

    for v in ["time_maa_x", "time_maa_r", "time_maa_w", "time_maa_k",
              "time_maa_v", "time_maa_a", "time_maa_g", "time_decay",
              "time_aaaaa"]:
        chvec("l1_" + v, inputs["L1_attn_" + v])
    chvec("l1_faaaa", np.asarray(inputs["L1_attn_time_faaaa"]).reshape(D))
    A("l1_maa_w1", inputs["L1_attn_time_maa_w1"])
    A("l1_maa_w2", np.asarray(inputs["L1_attn_time_maa_w2"], dtype=np.float32)
      .reshape(3, 2, 32, D).transpose(1, 2, 0, 3).reshape(64, 3 * D))
    A("l1_decay_w1", inputs["L1_attn_time_decay_w1"])
    A("l1_decay_w2", inputs["L1_attn_time_decay_w2"])
    A("l1_aaa_w1", inputs["L1_attn_time_aaa_w1"])
    A("l1_aaa_w2", inputs["L1_attn_time_aaa_w2"])
    A("l1_kkk_w1", inputs["L1_attn_time_kkk_w1"])
    A("l1_kkk_w2", inputs["L1_attn_time_kkk_w2"])
    A("l1_gate_w1", inputs["L1_attn_gate_w1"])
    A("l1_gate_w2", inputs["L1_attn_gate_w2"])
    for w in ["W_r", "W_k", "W_v", "W_o"]:
        A("l1_" + w.lower(), inputs["L1_attn_" + w])
    A("l1_ln_w", np.asarray(inputs["L1_attn_ln_w"]).reshape(1, D))
    A("l1_ln_b", np.asarray(inputs["L1_attn_ln_b"]).reshape(1, D))
    A("l1_gate", np.asarray(inputs["L1_ffn_w_router"], dtype=np.float32)
      .reshape(ND, 128, NE).transpose(1, 0, 2).reshape(128, ND * NE))
    A("l1_w1", inputs["L1_ffn_w1"])
    A("l1_w2", inputs["L1_ffn_w2"])
    b1 = np.asarray(inputs["L1_ffn_b1"], dtype=np.float32)
    A("l1_b1", b1.reshape(NE, 32, 128).transpose(2, 0, 1).reshape(128, NE * 32))
    A("l1_b2", inputs["L1_ffn_b2"])
    return g, percore



def build_program(upto=None):
    if upto is None:
        upto = UPTO
    nc = bass.Bass("TRN2", target_bir_lowering=False)
    stack = contextlib.ExitStack()
    p = P(nc, stack)
    dram = p.dram

    def din(name, shape, dtype=F32):
        dram[name] = nc.dram_tensor(name, shape, dtype, kind="ExternalInput").ap()

    din("x", [T, D])
    for w in ["l0_wq", "l0_wk", "l0_wv"]:
        din(w, [D, D])
    din("l0_gate", [128, ND * NE])
    din("l0_w1", [NE, D, FF])
    din("l0_w2", [NE, FF, D])
    din("l0_b1", [128, NE * 32])
    din("l0_b2", [NE, D])
    for v in ["ln1_g", "ln1_b", "ln2_g", "ln2_b"]:
        din("l0_" + v, [1, D])
    for v in ["time_maa_x", "time_maa_r", "time_maa_w", "time_maa_k",
              "time_maa_v", "time_maa_a", "time_maa_g", "time_decay",
              "time_aaaaa", "faaaa"]:
        din("l1_" + v, [128, ND])
    din("l1_maa_w1", [D, 192])
    din("l1_maa_w2", [64, 3 * D])
    din("l1_decay_w1", [D, 64])
    din("l1_decay_w2", [64, D])
    din("l1_aaa_w1", [D, 64])
    din("l1_aaa_w2", [64, D])
    din("l1_kkk_w1", [D, 64])
    din("l1_kkk_w2", [64, D])
    din("l1_gate_w1", [D, 128])
    din("l1_gate_w2", [128, D])
    for w in ["l1_w_r", "l1_w_k", "l1_w_v", "l1_w_o"]:
        din(w, [D, D])
    din("l1_ln_w", [1, D])
    din("l1_ln_b", [1, D])
    din("l1_gate", [128, ND * NE])
    din("l1_w1", [NE, D, FF])
    din("l1_w2", [NE, FF, D])
    din("l1_b1", [128, NE * 32])
    din("l1_b2", [NE, D])
    cn = make_consts()
    for k, v in cn.items():
        din("c_" + k, list(v.shape), BF16 if v.dtype == ml_dtypes.bfloat16 else F32)

    dram["out"] = nc.dram_tensor("out", [T, D], F32, kind="ExternalOutput").ap()

    def dscr(name, shape, dtype=F32):
        dram[name] = nc.dram_tensor(name, shape, dtype).ap()

    dscr("xn0_f", [T, D])
    dscr("xn0_b", [T, D], BF16)
    dscr("y0_f", [T, D])
    dscr("y0_b", [T, D], BF16)
    dscr("base0", [T, D])
    dscr("x1_f", [T, D])
    dscr("xn1_b", [T, D], BF16)
    dscr("xn1_ch", [D, T + 4])
    dscr("att1", [T, D])

    g = {}
    g["stats"] = p.sb("stats", [128, 8], F32)
    g["idf"] = p.sb("idf", [128, 128], F32)
    g["idb"] = p.sb("idb", [128, 128], BF16)
    g["ones1"] = p.sb("ones1", [1, 128], F32)
    g["onescol_bf"] = p.sb("onescol_bf", [128, 1], BF16)
    g["tri_incl"] = p.sb("tri_incl", [128, 128], F32)
    g["mask_su"] = p.sb("mask_su", [128, 128], F32)
    g["mask_iu"] = p.sb("mask_iu", [128, 128], F32)
    g["mask_sl"] = p.sb("mask_sl", [128, 128], F32)
    g["fvals"] = p.sb("fvals", [128, len(FCONSTS)], F32)
    for i, v in enumerate(FCONSTS):
        nc.const_aps.register(F32, v, g["fvals"][:, i:i + 1])

    ph = Ph(p, "consts")
    for nm, dst in [("c_ident_f32", g["idf"]), ("c_ident_bf16", g["idb"]),
                    ("c_ones_1x128", g["ones1"]),
                    ("c_onescol_bf", g["onescol_bf"]),
                    ("c_tri_incl", g["tri_incl"]), ("c_mask_su", g["mask_su"]),
                    ("c_mask_iu", g["mask_iu"]), ("c_mask_sl", g["mask_sl"]),
                    ("c_fvals", g["fvals"])]:
        ph.dma("sync", dst[:], dram[nm])
    ph.emit()

    if not globals().get('SKIP_LN0', False):
        phase_ln0(p, g)
    with contextlib.ExitStack() as qkv_stack:
        if upto >= 2:
            phase_qkv(p, g, qkv_stack)
        if upto >= 3:
            phase_mha(p, g)
    if upto >= 4:
        phase_attn0(p, g)
    if upto >= 5:
        phase_moe(p, g, layer=0)
    if upto >= 6:
        phase_rwkv(p, g)
    if upto >= 7:
        phase_moe(p, g, layer=1)

    if upto < 7:
        srcs = {1: "xn0_f", 2: "xn0_f", 3: "y0_f", 4: "base0", 5: "x1_f",
                6: "att1"}
        _copy_out(p, srcs.get(upto, "x1_f"))

    stack.close()
    return nc


def _copy_out(p, src):
    ph = Ph(p, "fin")
    with contextlib.ExitStack() as s_:
        buf = [p.sb(f"fin{i}", [128, D], F32, s_) for i in range(2)]
        toks = [None, None]
        for i in range(NT):
            t = ph.dma("sync", buf[i % 2][:],
                       p.dram[src][i * 128:(i + 1) * 128, :],
                       after=(toks[i % 2],), stream=f"f{i % 2}")
            toks[i % 2] = ph.dma("sync", p.dram["out"][i * 128:(i + 1) * 128, :],
                                 buf[i % 2][:], after=(t,), stream=f"f{i % 2}")
        ph.emit()



def bcast_row(ph, p, g, src_1xD, dst_128xD, ps2, after=(), guard=(None, None)):
    toks = []
    for fo in range(D // 512):
        tk = ph.op("tensor", lambda e, fo=fo: e.matmul(
            ps2[fo % 2][:, 0:512], g["ones1"][:],
            src_1xD[0:1, fo * 512:(fo + 1) * 512],
            start=True, stop=True), after=tuple(after) + (guard[fo % 2],),
            token=True)
        toks.append(ph.op("vector", lambda e, fo=fo: e.tensor_copy(
            dst_128xD[:, fo * 512:(fo + 1) * 512], ps2[fo % 2][:, 0:512]),
            after=(tk,), token=True))
    return toks


def ln_stats(ph, mv, x_ap, rstd, eps, extra_after=()):
    tb1 = ph.op("vector", lambda e: e.bn_stats(mv[:, 0:6], x_ap[:, 0:512]),
                after=extra_after, token=True)
    tb2 = ph.op("vector", lambda e: e.bn_stats(mv[:, 6:12], x_ap[:, 512:1024]),
                token=True)
    ts_ = ph.op("vector", lambda e: e.bn_aggr(mv[:, 12:14], mv[:, 0:12]),
                after=(tb1, tb2), force=True, token=True)
    ph.op("scalar", lambda e: e.activation(rstd[:], mv[:, 13:14], AF.Ln,
                                           bias=eps), after=(ts_,))
    tr = ph.op("scalar", lambda e: e.activation(rstd[:], rstd[:], AF.Exp,
                                                scale=-0.5), token=True)
    return ts_, tr



def phase_ln0(p, g):
    dram = p.dram
    ph = Ph(p, "ln0")
    with contextlib.ExitStack() as s_:
        xa = [p.sb(f"a_x{i}", [128, D], F32, s_) for i in range(2)]
        xn = [p.sb(f"a_xn{i}", [128, D], F32, s_) for i in range(2)]
        xnb = [p.sb(f"a_xnb{i}", [128, D], BF16, s_) for i in range(2)]
        mv = [p.sb(f"a_mv{i}", [128, 16], F32, s_) for i in range(2)]
        rs = [p.sb(f"a_rs{i}", [128, 1], F32, s_) for i in range(2)]
        n2st = p.sb("a_n2", [128, NT], F32, s_)
        gpad = p.sb("a_gpad", [128, 128], F32, s_)
        cmax = p.sb("a_cmax", [1, 1], F32, s_)
        pt = p.ps("a_pt", [128, 128], s_)
        pb = p.ps("a_pb", [128, 1], s_)

        tap_xn = p.tap("xn0", [T, D])
        ph.op("vector", lambda e: e.memset(gpad[:], -3e38))
        done = [None, None]
        vtok = [None, None]
        for i in range(NT):
            b = i % 2
            tl = ph.dma("sync", xa[b][:], dram["x"][i * 128:(i + 1) * 128, :],
                        after=(vtok[b],), stream=f"x{b}")
            tsa, tr = ln_stats(ph, mv[b], xa[b], rs[b], 1e-5,
                               extra_after=(tl, done[b]))
            ph.op("vector", lambda e, b=b: e.tensor_scalar(
                xn[b][:], xa[b][:], mv[b][:, 12:13], rs[b][:],
                op0=ALU.subtract, op1=ALU.mult), after=(tr, tsa), force=True)
            ph.op("vector", lambda e, b=b: e.tensor_copy(xnb[b][:], xn[b][:]))
            ph.op("vector", lambda e, b=b: e.tensor_tensor(
                mv[b][:, 14:15], rs[b][:], rs[b][:], op=ALU.mult))
            ph.op("vector", lambda e, b=b: e.tensor_tensor(
                mv[b][:, 15:16], mv[b][:, 13:14], mv[b][:, 14:15], op=ALU.mult))
            vtok[b] = ph.op("vector", lambda e, b=b, i=i: e.tensor_scalar(
                n2st[:, i:i + 1], mv[b][:, 15:16], float(D), None,
                op0=ALU.mult), token=True)
            ph.dma("gpsimd", dram["xn0_f"][i * 128:(i + 1) * 128, :],
                   xn[b][:], after=(vtok[b],), stream=f"s{b}")
            done[b] = ph.dma("gpsimd", dram["xn0_b"][i * 128:(i + 1) * 128, :],
                             xnb[b][:], after=(vtok[b],), stream=f"s{b}")
            if tap_xn is not None:
                done[b] = ph.dma("gpsimd", tap_xn[i * 128:(i + 1) * 128, :],
                                 xn[b][:], after=(vtok[b],), stream=f"s{b}")
        tm1 = ph.op("vector", lambda e: e.tensor_reduce(
            gpad[:, 0:1], n2st[:], axis=AX.X, op=ALU.max),
            after=tuple(vtok), token=True)
        tt = ph.op("tensor", lambda e: e.transpose(pt[:], gpad[:], g["idf"][:]),
                   after=(tm1,), token=True)
        tv = ph.op("vector", lambda e: e.tensor_reduce(
            cmax[:], pt[0:1, :], axis=AX.X, op=ALU.max), after=(tt,),
            token=True)
        tb = ph.op("tensor", lambda e: e.matmul(
            pb[:], g["ones1"][:], cmax[:], start=True, stop=True),
            after=(tv,), token=True)
        ph.op("scalar", lambda e: e.mul(g["stats"][:, 0:1], pb[:], -1.0),
              after=(tb,))
        ph.emit()



def phase_qkv(p, g, qkv_stack):
    dram = p.dram
    g["qT"] = p.sb("qT", [128, ND * T], BF16, qkv_stack)
    g["kT"] = p.sb("kT", [128, ND * T], BF16, qkv_stack)
    g["vtok"] = p.sb("vtok", [128, NT * D], BF16, qkv_stack)

    ph = Ph(p, "qkv")
    with contextlib.ExitStack() as s_:
        xnch = p.sb("b_xnch", [128, ND * T], BF16, s_)
        wbf = [p.sb(f"b_w{i}", [128, ND * D], BF16, s_) for i in range(2)]
        hb = p.sb("b_hb", [128, ND * H], BF16, s_)
        qsq = p.sb("b_qsq", [128, T], BF16, s_)
        nrm = p.sb("b_nrm", [128, 2 * NT * H], F32, s_)
        gpad = p.sb("b_gpad", [128, 128], F32, s_)
        cm = p.sb("b_cm", [1, 4], F32, s_)
        ps = [p.ps(f"b_ps{i}", [128, 512], s_) for i in range(2)]
        pn = [p.ps(f"b_pn{i}", [128, H], s_) for i in range(2)]
        pt = p.ps("b_pt", [128, 128], s_)
        pb = p.ps("b_pb", [128, 1], s_)

        xt = []
        for dc in range(ND):
            xt.append(ph.dma("sync", xnch[:, dc * T:(dc + 1) * T],
                             dram["xn0_b"][:, dc * 128:(dc + 1) * 128],
                             transpose=True, stream="xnch"))
        hbt = ph.dma("sync", hb[:], dram["c_headblk_bf"], stream="xnch")

        if QKV_SUB < 1:
            ph.emit()
            return
        wtok = {}
        consumed = [None, None]
        for wi, wname in enumerate(["l0_wq", "l0_wk"]):
            for dc in range(ND):
                wtok[(wi, dc)] = ph.dma(
                    "gpsimd", wbf[wi % 2][:, dc * D:(dc + 1) * D],
                    dram[wname][dc * 128:(dc + 1) * 128, :],
                    stream=f"w{wi % 2}")
        if QKV_SUB < 2:
            ph.emit()
            return

        evtok = [None, None]
        for wi in range(QKV_NW):
            dst = g["qT"] if wi == 0 else g["kT"]
            lastmm = None
            for do in range(QKV_NDO):
                for tq in range(4):
                    k = do * 4 + tq
                    pp = ps[k % 2]
                    aft = tuple(wtok[(wi, dc)] for dc in range(ND)) + \
                        (evtok[k % 2],) + tuple(xt)
                    for dc in range(ND):
                        ph.op("tensor", lambda e, pp=pp, wi=wi, dc=dc, do=do,
                              tq=tq: e.matmul(
                            pp[:],
                            wbf[wi % 2][:, dc * D + do * 128:
                                        dc * D + (do + 1) * 128],
                            xnch[:, dc * T + tq * 512:dc * T + (tq + 1) * 512],
                            start=(dc == 0), stop=(dc == ND - 1)),
                            after=aft if dc == 0 else ())
                    lastmm = ph.tok_last("tensor")
                    evtok[k % 2] = ph.op(
                        "scalar", lambda e, dst=dst, do=do, tq=tq, pp=pp:
                        e.copy(dst[:, do * T + tq * 512:do * T + (tq + 1) * 512],
                               pp[:]), after=(lastmm,), token=True)
            consumed[wi % 2] = lastmm
        if QKV_SUB < 3:
            ph.emit()
            return
        for dc in range(ND):
            wtok[(2, dc)] = ph.dma(
                "gpsimd", wbf[0][:, dc * D:(dc + 1) * D],
                dram["l0_wv"][dc * 128:(dc + 1) * 128, :],
                after=(consumed[0],), stream="w0")
        for ti in range(NT):
            for fo in range(2):
                k = ti * 2 + fo
                pp = ps[k % 2]
                aft = tuple(wtok[(2, dc)] for dc in range(ND)) + (evtok[k % 2],)
                for dc in range(ND):
                    ph.op("tensor", lambda e, pp=pp, dc=dc, ti=ti, fo=fo:
                          e.matmul(
                        pp[:],
                        xnch[:, dc * T + ti * 128:dc * T + (ti + 1) * 128],
                        wbf[0][:, dc * D + fo * 512:dc * D + (fo + 1) * 512],
                        start=(dc == 0), stop=(dc == ND - 1)),
                        after=aft if dc == 0 else ())
                lastmm = ph.tok_last("tensor")
                evtok[k % 2] = ph.op(
                    "scalar", lambda e, ti=ti, fo=fo, pp=pp: e.copy(
                        g["vtok"][:, ti * D + fo * 512:ti * D + (fo + 1) * 512],
                        pp[:]), after=(lastmm,), token=True)

        if QKV_SUB < 4:
            ph.emit()
            return
        mz = ph.op("vector", lambda e: e.memset(nrm[:], 0.0), token=True)
        ph.op("vector", lambda e: e.memset(gpad[:], -3e38))
        pndone = [mz, mz]
        sqdone = None
        addtoks = []
        for wi in range(2):
            src = g["qT"] if wi == 0 else g["kT"]
            for dc in range(ND):
                sq = ph.op("scalar", lambda e, src=src, dc=dc: e.activation(
                    qsq[:], src[:, dc * T:(dc + 1) * T], AF.Square),
                    after=(evtok[0], evtok[1], sqdone), token=True)
                for ti in range(NT):
                    b = ti % 2
                    tmm = ph.op("tensor", lambda e, b=b, ti=ti, dc=dc:
                                e.matmul(
                        pn[b][:], qsq[:, ti * 128:(ti + 1) * 128],
                        hb[:, dc * H:(dc + 1) * H], start=True, stop=True),
                        after=(sq, pndone[b], hbt), token=True)
                    col = (wi * NT + ti) * H
                    pndone[b] = ph.op(
                        "vector", lambda e, col=col, b=b: e.tensor_tensor(
                            nrm[:, col:col + H], nrm[:, col:col + H],
                            pn[b][:], op=ALU.add), after=(tmm, mz), token=True)
                    addtoks.append(pndone[b])
                sqdone = tmm
        t1 = ph.op("vector", lambda e: e.tensor_reduce(
            gpad[:, 0:1], nrm[:, 0:NT * H], axis=AX.X, op=ALU.max),
            after=tuple(addtoks[-NT:]), token=True)
        t2 = ph.op("vector", lambda e: e.tensor_reduce(
            gpad[:, 64:65], nrm[:, NT * H:], axis=AX.X, op=ALU.max),
            token=True)
        tt = ph.op("tensor", lambda e: e.transpose(pt[:], gpad[:], g["idf"][:]),
                   after=(t1, t2), token=True)
        tr1 = ph.op("vector", lambda e: e.tensor_reduce(
            cm[:, 0:1], pt[0:1, :], axis=AX.X, op=ALU.max), after=(tt,),
            token=True)
        tr2 = ph.op("vector", lambda e: e.tensor_reduce(
            cm[:, 1:2], pt[64:65, :], axis=AX.X, op=ALU.max), token=True)
        tv = ph.op("vector", lambda e: e.tensor_tensor(
            cm[:, 2:3], cm[:, 0:1], cm[:, 1:2], op=ALU.mult),
            after=(tr1, tr2), force=True, token=True)
        tsq = ph.op("scalar", lambda e: e.activation(cm[:, 3:4], cm[:, 2:3],
                                                     AF.Sqrt),
                    after=(tv,), token=True)
        tb = ph.op("tensor", lambda e: e.matmul(
            pb[:], g["ones1"][:], cm[:, 3:4], start=True, stop=True),
            after=(tsq,), token=True)
        ph.op("scalar", lambda e: e.mul(g["stats"][:, 1:2], pb[:],
                                        -1.0 / 32.0), after=(tb,))
        ph.emit()



def phase_mha(p, g):
    dram = p.dram
    ph = Ph(p, "mha")
    with contextlib.ExitStack() as s_:
        ET = p.sb("c_et", [128, NT * 512], BF16, s_)
        stage = [p.sb(f"c_st{i}", [128, 512], F32, s_) for i in range(2)]
        mha = p.sb("c_mha", [128, 4 * D], F32, s_)
        rz = p.sb("c_rz", [128, 64], F32, s_)
        xnb_ = [p.sb(f"c_xn{i}", [128, D], F32, s_) for i in range(2)]
        y0 = [p.sb(f"c_y0{i}", [128, D], F32, s_) for i in range(2)]
        y0b = [p.sb(f"c_y0b{i}", [128, D], BF16, s_) for i in range(2)]
        hat = p.sb("c_hat", [128, D], F32, s_)
        mv = p.sb("c_mv", [128, 16], F32, s_)
        rs = p.sb("c_rs", [128, 1], F32, s_)
        g1 = p.sb("c_g1", [128, D], F32, s_)
        b1 = p.sb("c_b1", [128, D], F32, s_)
        lg1 = p.sb("c_lg1", [1, D], F32, s_)
        lb1 = p.sb("c_lb1", [1, D], F32, s_)
        cmask = p.sb("c_cmask", [128, 4 * 512], BF16, s_)
        psS = [p.ps(f"c_psS{i}", [128, 512], s_) for i in range(2)]
        psO = [p.ps(f"c_psO{i}", [128, 512], s_) for i in range(2)]
        psT = [p.ps(f"c_psT{i}", [128, 128], s_) for i in range(2)]

        tap_y0 = p.tap("y0", [T, D])
        tap_mha = p.tap("mha", [T, D])
        tap_et = p.tap("et0", [128, 4 * 512])
        tap_st = p.tap("st0", [128, 512])
        tap_mr = p.tap("mhraw", [T, D])
        tap_g1 = p.tap("g1rep", [128, D])
        tap_b1 = p.tap("b1rep", [128, D])

        t1 = ph.dma("sync", lg1[:], dram["l0_ln1_g"], stream="p0")
        t2 = ph.dma("sync", lb1[:], dram["l0_ln1_b"], stream="p1")
        cmt = []
        for off in range(4):
            cmt.append(ph.dma("sync", cmask[:, off * 512:(off + 1) * 512],
                              dram[f"c_cmask{off}"], stream="p2"))
        bt1 = bcast_row(ph, p, g, lg1, g1, psS, after=(t1,))
        gtoks = bcast_row(ph, p, g, lb1, b1, psS, after=(t2,),
                          guard=(bt1[0], bt1[1]))
        if tap_g1 is not None:
            ph.dma("gpsimd", tap_g1, g1[:], after=tuple(bt1), stream="tap")
            ph.dma("gpsimd", tap_b1, b1[:], after=tuple(gtoks), stream="tap")
        gtoks = gtoks + cmt

        prev_qb = None
        prev_qb2 = ()
        avdone = [None, None]
        trdone = [None, None]
        xtoks = [None, None]
        ystore = [None, None]
        for qb in range(4):
            ktmax = 4 * qb + 4
            prev_h_av = None
            exp_h = {}
            norm_ready = []
            for h in range(H):
                dt_ = h // 2
                off = (h % 2) * 64
                masktoks = []
                for kt in range(ktmax):
                    b = kt % 2
                    tmm = ph.op("tensor", lambda e, b=b, kt=kt, dt_=dt_,
                                off=off, qb=qb: e.matmul(
                        psS[b][:],
                        g["kT"][off:off + 64,
                                dt_ * T + kt * 128:dt_ * T + (kt + 1) * 128],
                        g["qT"][off:off + 64,
                                dt_ * T + qb * 512:dt_ * T + (qb + 1) * 512],
                        start=True, stop=True),
                        after=(exp_h.get(kt - 2), prev_qb) +
                        (prev_qb2 if h == 0 and kt == 0 else ()) +
                        (tuple(gtoks) if qb == 0 and h == 0 else ()),
                        token=True)
                    exp_h[kt] = ph.op(
                        "scalar", lambda e, b=b, kt=kt: e.activation(
                            ET[:, kt * 512:(kt + 1) * 512], psS[b][:], AF.Exp,
                            scale=1.0 / 32.0, bias=g["stats"][:, 1:2]),
                        after=(tmm,), token=True)
                    if kt >= 4 * qb:
                        mo = kt - 4 * qb
                        masktoks.append(ph.op(
                            "vector", lambda e, kt=kt, mo=mo: e.tensor_tensor(
                                ET[:, kt * 512:(kt + 1) * 512],
                                ET[:, kt * 512:(kt + 1) * 512],
                                cmask[:, mo * 512:(mo + 1) * 512],
                                op=ALU.mult),
                            after=(exp_h[kt],), token=True))
                ob = h % 2
                aft0 = tuple(exp_h[kt] for kt in range(ktmax)) + \
                    tuple(masktoks) + (avdone[ob],)
                for kt in range(ktmax):
                    ph.op("tensor", lambda e, ob=ob, kt=kt, h=h, ktmax=ktmax:
                          e.matmul(
                        psO[ob][0:64, :],
                        g["vtok"][:, kt * D + h * 64:kt * D + (h + 1) * 64],
                        ET[:, kt * 512:(kt + 1) * 512],
                        start=(kt == 0), stop=(kt == ktmax - 1)),
                        after=aft0 if kt == 0 else ())
                for kt in range(ktmax):
                    ph.op("tensor", lambda e, ob=ob, kt=kt, ktmax=ktmax:
                          e.matmul(
                        psO[ob][64:65, :], g["onescol_bf"][:],
                        ET[:, kt * 512:(kt + 1) * 512],
                        start=(kt == 0), stop=(kt == ktmax - 1)))
                prev_h_av = ph.tok_last("tensor")
                if tap_et is not None and qb == 0 and h == 0:
                    ph.dma("gpsimd", tap_et, ET[:, 0:4 * 512],
                           after=tuple(masktoks) + (prev_h_av,), stream="tap")
                ev = ph.op("vector", lambda e, ob=ob: e.tensor_copy(
                    stage[ob][0:65, :], psO[ob][0:65, :]),
                    after=(prev_h_av,), token=True)
                if tap_st is not None and qb == 0 and h == 0:
                    ph.dma("gpsimd", tap_st, stage[ob][:, :], after=(ev,),
                           stream="tap")
                avdone[ob] = ev
                for j in range(4):
                    tb_ = j % 2
                    ttr = ph.op("tensor", lambda e, tb_=tb_, ob=ob, j=j:
                                e.transpose(
                        psT[tb_][:, 0:65],
                        stage[ob][0:65, j * 128:(j + 1) * 128],
                        g["idf"][0:65, 0:65]),
                        after=(ev,) + (trdone[tb_] or ()), token=True)
                    sc = ph.op("scalar", lambda e, tb_=tb_, j=j, h=h: e.copy(
                        mha[:, j * D + h * 64:j * D + (h + 1) * 64],
                        psT[tb_][:, 0:64]), after=(ttr,), token=True)
                    vr = ph.op("vector", lambda e, tb_=tb_, j=j, h=h:
                               e.reciprocal(
                        rz[:, j * 16 + h:j * 16 + h + 1],
                        psT[tb_][:, 64:65]), after=(ttr,), token=True)
                    trdone[tb_] = (sc, vr)
                    norm_ready += [sc, vr]
            ytoks = []
            for j in range(4):
                i = qb * 4 + j
                b = j % 2
                ntk = []
                for h in range(H):
                    ntk.append(ph.op("vector", lambda e, j=j, h=h:
                                     e.tensor_scalar(
                        mha[:, j * D + h * 64:j * D + (h + 1) * 64],
                        mha[:, j * D + h * 64:j * D + (h + 1) * 64],
                        rz[:, j * 16 + h:j * 16 + h + 1], None, op0=ALU.mult),
                        after=tuple(norm_ready), token=True))
                tl = ph.dma("sync", xnb_[b][:],
                            dram["xn0_f"][i * 128:(i + 1) * 128, :],
                            after=(xtoks[b],), stream=f"xn{b}")
                if tap_mr is not None:
                    ytoks.append(ph.dma(
                        "gpsimd", tap_mr[i * 128:(i + 1) * 128, :],
                        mha[:, j * D:(j + 1) * D], after=tuple(ntk),
                        stream="tap"))
                tsa, tr = ln_stats(ph, mv, mha[:, j * D:(j + 1) * D], rs,
                                   1e-5, extra_after=tuple(ntk))
                ph.op("vector", lambda e, j=j: e.tensor_scalar(
                    hat[:], mha[:, j * D:(j + 1) * D], mv[:, 12:13], rs[:],
                    op0=ALU.subtract, op1=ALU.mult), after=(tr, tsa),
                    force=True)
                if tap_mha is not None:
                    tmp = ph.op("vector", lambda e: e.nop(), token=True)
                    ph.dma("gpsimd", tap_mha[i * 128:(i + 1) * 128, :], hat[:],
                           after=(tmp,), stream="tap")
                ph.op("vector", lambda e: e.tensor_tensor(
                    hat[:], hat[:], g1[:], op=ALU.mult))
                ph.op("vector", lambda e: e.tensor_tensor(
                    hat[:], hat[:], b1[:], op=ALU.add))
                xtoks[b] = ph.op("vector", lambda e, b=b: e.tensor_tensor(
                    y0[b][:], hat[:], xnb_[b][:], op=ALU.add),
                    after=(tl, ystore[b]), token=True)
                yt = ph.op("vector", lambda e, b=b: e.tensor_copy(
                    y0b[b][:], y0[b][:]), token=True)
                ph.dma("gpsimd", dram["y0_f"][i * 128:(i + 1) * 128, :],
                       y0[b][:], after=(yt,), stream=f"y{b}")
                t3 = ph.dma("gpsimd", dram["y0_b"][i * 128:(i + 1) * 128, :],
                            y0b[b][:], after=(yt,), stream=f"y{b}")
                if tap_y0 is not None:
                    t3 = ph.dma("gpsimd", tap_y0[i * 128:(i + 1) * 128, :],
                                y0[b][:], after=(yt,), stream=f"y{b}")
                ystore[b] = t3
                ytoks.append(t3)
            prev_qb = ytoks[-1]
            prev_qb2 = tuple(ytoks[-3:])
        ph.emit()



def phase_attn0(p, g):
    dram = p.dram
    ph = Ph(p, "attn0")
    with contextlib.ExitStack() as s_:
        xnch = p.sb("d_xnch", [128, ND * T], BF16, s_)
        xntok = p.sb("d_xntok", [128, NT * D], BF16, s_)
        ET = p.sb("d_et", [128, NT * 512], BF16, s_)
        att = [p.sb(f"d_att{i}", [128, D], F32, s_) for i in range(2)]
        xb = [p.sb(f"d_xb{i}", [128, D], F32, s_) for i in range(2)]
        yb = [p.sb(f"d_yb{i}", [128, D], F32, s_) for i in range(2)]
        rz = p.sb("d_rz", [128, 2], F32, s_)
        psS = [p.ps(f"d_psS{i}", [128, 512], s_) for i in range(2)]
        psU = [p.ps(f"d_psU{i}", [128, 1024], s_) for i in range(2)]
        psZ = [p.ps(f"d_psZ{i}", [128, 1], s_) for i in range(2)]

        tap_att = p.tap("attn0", [T, D])

        xt = []
        for dc in range(ND):
            xt.append(ph.dma("sync", xnch[:, dc * T:(dc + 1) * T],
                             dram["xn0_b"][:, dc * 128:(dc + 1) * 128],
                             transpose=True, stream="xt"))
        for ti in range(NT):
            xt.append(ph.dma("sync", xntok[:, ti * D:(ti + 1) * D],
                             dram["xn0_b"][ti * 128:(ti + 1) * 128, :],
                             stream="xt"))
        XT = tuple(xt)

        evdone = [None, None]
        for qb in range(4):
            exp_t = {}
            for kt in range(NT):
                b = kt % 2
                for dc in range(ND):
                    ph.op("tensor", lambda e, b=b, kt=kt, qb=qb, dc=dc:
                          e.matmul(
                        psS[b][:],
                        xnch[:, dc * T + kt * 128:dc * T + (kt + 1) * 128],
                        xnch[:, dc * T + qb * 512:dc * T + (qb + 1) * 512],
                        start=(dc == 0), stop=(dc == ND - 1)),
                        after=(XT + (exp_t.get(kt - 2),)) if dc == 0 else ())
                tmm = ph.tok_last("tensor")
                exp_t[kt] = ph.op("scalar", lambda e, b=b, kt=kt:
                                  e.activation(
                    ET[:, kt * 512:(kt + 1) * 512], psS[b][:], AF.Exp,
                    bias=g["stats"][:, 0:1]), after=(tmm,), token=True)
            aft0 = tuple(exp_t[kt] for kt in range(NT))
            for j in range(4):
                ob = j % 2
                i = qb * 4 + j
                for kt in range(NT):
                    for fo in range(2):
                        ph.op("tensor", lambda e, ob=ob, kt=kt, j=j, fo=fo:
                              e.matmul(
                            psU[ob][:, fo * 512:(fo + 1) * 512],
                            ET[:, kt * 512 + j * 128:kt * 512 + (j + 1) * 128],
                            xntok[:, kt * D + fo * 512:kt * D + (fo + 1) * 512],
                            start=(kt == 0), stop=(kt == NT - 1)),
                            after=(aft0 + (evdone[ob],))
                            if (kt == 0 and fo == 0) else ())
                    ph.op("tensor", lambda e, ob=ob, kt=kt, j=j: e.matmul(
                        psZ[ob][:],
                        ET[:, kt * 512 + j * 128:kt * 512 + (j + 1) * 128],
                        g["onescol_bf"][:],
                        start=(kt == 0), stop=(kt == NT - 1)))
                tav = ph.tok_last("tensor")
                tl1 = ph.dma("sync", xb[ob][:],
                             dram["x"][i * 128:(i + 1) * 128, :],
                             after=(evdone[ob],), stream=f"a{ob}")
                tl2 = ph.dma("sync", yb[ob][:],
                             dram["y0_f"][i * 128:(i + 1) * 128, :],
                             after=(evdone[ob],), stream=f"c{ob}")
                ph.op("vector", lambda e, ob=ob: e.reciprocal(
                    rz[:, ob:ob + 1], psZ[ob][:]), after=(tav,))
                tv = ph.op("vector", lambda e, ob=ob: e.tensor_scalar(
                    att[ob][:], psU[ob][:], rz[:, ob:ob + 1], None,
                    op0=ALU.mult), token=True)
                taptok = None
                if tap_att is not None:
                    taptok = ph.dma("gpsimd", tap_att[i * 128:(i + 1) * 128, :],
                                    att[ob][:], after=(tv,), stream="tap")
                ph.op("vector", lambda e, ob=ob: e.tensor_tensor(
                    att[ob][:], att[ob][:], xb[ob][:], op=ALU.add),
                    after=(tl1, taptok))
                tv2 = ph.op("vector", lambda e, ob=ob: e.tensor_tensor(
                    att[ob][:], att[ob][:], yb[ob][:], op=ALU.add),
                    after=(tl2,), token=True)
                evdone[ob] = ph.dma(
                    "gpsimd", dram["base0"][i * 128:(i + 1) * 128, :],
                    att[ob][:], after=(tv2,), stream=f"b{ob}")
        ph.emit()



def phase_moe(p, g, layer):
    dram = p.dram
    L = f"l{layer}"
    ph = Ph(p, f"moe{layer}")
    with contextlib.ExitStack() as s_:
        es_ = s_.enter_context(contextlib.ExitStack())
        acc = p.sb(f"m{layer}_acc", [128, NT * D], F32, s_)
        comb = p.sb(f"m{layer}_comb", [128, NT * NE], F32, s_)
        aux = ([p.sb(f"m{layer}_aux{i}", [128, D], F32, s_) for i in range(2)]
               if layer == 1 else None)
        psA = [p.ps(f"m{layer}_psA{i}", [128, 512], s_) for i in range(2)]
        psO = [p.ps(f"m{layer}_psO{i}", [128, 1024], s_) for i in range(2)]
        psL = p.ps(f"m{layer}_psL", [128, NE], s_)
        psC = p.ps(f"m{layer}_psC", [8, 128], s_)
        xchb = p.sb(f"m{layer}_xchb", [128, ND * T], BF16, es_)
        w1b = p.sb(f"m{layer}_w1", [128, ND * 2048], BF16, es_)
        w2b = p.sb(f"m{layer}_w2", [128, 16 * D], BF16, es_)
        hT = p.sb(f"m{layer}_hT", [128, 16 * 512], BF16, es_)
        combT = p.sb(f"m{layer}_combT", [8, T], F32, es_)
        gsb = p.sb(f"m{layer}_gate", [128, ND * NE], F32, es_)
        b1sb = p.sb(f"m{layer}_b1", [128, NE * 32], F32, es_)
        b2sb = p.sb(f"m{layer}_b2", [8, D], F32, es_)
        ytile = (p.sb(f"m{layer}_yt", [128, D], F32, es_)
                 if layer == 0 else None)
        stg = p.sb(f"m{layer}_stg", [128, ND * 128], F32, es_)
        lgt = p.sb(f"m{layer}_lgt", [128, 16], F32, es_)

        src_b = "y0_b" if layer == 0 else "xn1_b"
        xt = []
        for dc in range(ND):
            xt.append(ph.dma("sync", xchb[:, dc * T:(dc + 1) * T],
                             dram[src_b][:, dc * 128:(dc + 1) * 128],
                             transpose=True, stream="xc"))
        gt_tok = ph.dma("sync", gsb[:], dram[L + "_gate"], stream="g1")
        b1_tok = ph.dma("sync", b1sb[:], dram[L + "_b1"], stream="g2")
        b2_tok = ph.dma("sync", b2sb[:], dram[L + "_b2"], stream="g3")
        tap_cmb = p.tap(f"combine{layer}", [T, NE])
        tap_lg = p.tap(f"logits{layer}", [T, 16])

        ctoks = []
        tprev = [None, None]
        lgdone = None
        lgtap = None
        for ti in range(NT):
            if layer == 0:
                tl = ph.dma("sync", ytile[:],
                            dram["y0_f"][ti * 128:(ti + 1) * 128, :],
                            after=(lgdone,), stream="yt")
                for dc in range(ND):
                    b = dc % 2
                    ttr = ph.op("tensor", lambda e, b=b, dc=dc: e.transpose(
                        psA[b][:, 0:128], ytile[:, dc * 128:(dc + 1) * 128],
                        g["idf"][:]), after=(tl, tprev[b]), token=True)
                    tprev[b] = ph.op("vector", lambda e, b=b, dc=dc:
                                     e.tensor_copy(
                        stg[:, dc * 128:(dc + 1) * 128], psA[b][:, 0:128]),
                        after=(ttr,), token=True)
                aft = tuple(tprev) + (gt_tok,)
                for dc in range(ND):
                    ph.op("tensor", lambda e, dc=dc: e.matmul(
                        psL[:], stg[:, dc * 128:(dc + 1) * 128],
                        gsb[:, dc * NE:(dc + 1) * NE],
                        start=(dc == 0), stop=(dc == ND - 1)),
                        after=aft if dc == 0 else ())
            else:
                stok = []
                for dc in range(ND):
                    stok.append(ph.dma(
                        "sync", stg[:, dc * 128:(dc + 1) * 128],
                        dram["xn1_ch"][dc * 128:(dc + 1) * 128,
                                       4 + ti * 128:4 + (ti + 1) * 128],
                        after=(lgdone,), stream="yt"))
                for dc in range(ND):
                    ph.op("tensor", lambda e, dc=dc: e.matmul(
                        psL[:], stg[:, dc * 128:(dc + 1) * 128],
                        gsb[:, dc * NE:(dc + 1) * NE],
                        start=(dc == 0), stop=(dc == ND - 1)),
                        after=tuple(stok) + (gt_tok,) if dc == 0 else ())
            tmm = ph.tok_last("tensor")
            cc = comb[:, ti * NE:(ti + 1) * NE]
            tv = ph.op("vector", lambda e: e.tensor_copy(lgt[:, 0:8], psL[:]),
                       after=(tmm, lgtap), token=True)
            lgdone = tv
            trm = ph.op("vector", lambda e: e.tensor_reduce(
                lgt[:, 8:9], lgt[:, 0:8], axis=AX.X, op=ALU.max), token=True)
            tmsk = ph.op("vector", lambda e, cc=cc: e.tensor_scalar(
                cc, lgt[:, 0:8], lgt[:, 8:9], None, op0=ALU.is_ge),
                after=(trm,), force=True, token=True)
            if layer == 0:
                ph.op("vector", lambda e, cc=cc: e.scalar_tensor_tensor(
                    lgt[:, 0:8], cc, -1e30, lgt[:, 0:8], op0=ALU.mult,
                    op1=ALU.add))
                trm2 = ph.op("vector", lambda e: e.tensor_reduce(
                    lgt[:, 9:10], lgt[:, 0:8], axis=AX.X, op=ALU.max),
                    token=True)
                tvv = ph.op("vector", lambda e: e.tensor_scalar(
                    lgt[:, 10:11], lgt[:, 9:10], lgt[:, 8:9], None,
                    op0=ALU.subtract), after=(trm2,), force=True, token=True)
                te = ph.op("scalar", lambda e: e.activation(
                    lgt[:, 11:12], lgt[:, 10:11], AF.Exp), after=(tvv,),
                    token=True)
                ph.op("vector", lambda e: e.tensor_scalar(
                    lgt[:, 12:13], lgt[:, 11:12], 1.0, None, op0=ALU.add),
                    after=(te,))
                ph.op("vector", lambda e: e.reciprocal(lgt[:, 13:14],
                                                       lgt[:, 12:13]))
                ph.op("vector", lambda e: e.tensor_tensor(
                    lgt[:, 14:15], lgt[:, 11:12], lgt[:, 13:14], op=ALU.mult))
                ph.op("vector", lambda e, cc=cc: e.tensor_scalar(
                    cc, cc, lgt[:, 13:14], None, op0=ALU.mult))
                ph.op("vector", lambda e: e.tensor_scalar(
                    lgt[:, 0:8], lgt[:, 0:8], lgt[:, 9:10], None,
                    op0=ALU.is_ge))
                tv2 = ph.op("vector", lambda e, cc=cc: e.scalar_tensor_tensor(
                    cc, lgt[:, 0:8], lgt[:, 14:15], cc, op0=ALU.mult,
                    op1=ALU.add), token=True)
            else:
                tn = ph.op("scalar", lambda e: e.mul(
                    lgt[:, 9:10], lgt[:, 8:9], -1.0), after=(trm,), token=True)
                te = ph.op("scalar", lambda e: e.activation(
                    lgt[:, 0:8], lgt[:, 0:8], AF.Exp, bias=lgt[:, 9:10],
                    accum_out=lgt[:, 11:12]), after=(tn, tmsk), token=True)
                ph.op("vector", lambda e: e.reciprocal(
                    lgt[:, 12:13], lgt[:, 11:12]), after=(te,))
                tv2 = ph.op("vector", lambda e, cc=cc: e.tensor_scalar(
                    cc, cc, lgt[:, 12:13], None, op0=ALU.mult), token=True)
            if tap_cmb is not None:
                ph.dma("gpsimd", tap_cmb[ti * 128:(ti + 1) * 128, :], cc,
                       after=(tv2,), stream="tap")
            if tap_lg is not None:
                lgtap = ph.dma("gpsimd", tap_lg[ti * 128:(ti + 1) * 128, :],
                               lgt[:], after=(tv2,), stream="tap")
            ttr = ph.op("tensor", lambda e, cc=cc: e.transpose(
                psC[:], cc, g["idf"][:]),
                after=(tv2, ctoks[-1] if ctoks else None), token=True)
            ctoks.append(ph.op("vector", lambda e, ti=ti: e.tensor_copy(
                combT[:, ti * 128:(ti + 1) * 128], psC[:]),
                after=(ttr,), token=True))

        initoks = []
        psO_guard = [None, None]
        auxg = [None, None]
        for ti in range(NT):
            b = ti % 2
            for fo in range(2):
                ph.op("tensor", lambda e, b=b, ti=ti, fo=fo: e.matmul(
                    psO[b][:, fo * 512:(fo + 1) * 512],
                    combT[:, ti * 128:(ti + 1) * 128],
                    b2sb[:, fo * 512:(fo + 1) * 512], start=True, stop=True),
                    after=(ctoks[ti], psO_guard[b], b2_tok)
                    if fo == 0 else ())
            tmm = ph.tok_last("tensor")
            if layer == 0:
                tv = ph.op("vector", lambda e, b=b, ti=ti: e.tensor_copy(
                    acc[:, ti * D:(ti + 1) * D], psO[b][:]),
                    after=(tmm,), token=True)
            else:
                ta = ph.dma("sync", aux[0][:],
                            dram["att1"][ti * 128:(ti + 1) * 128, :],
                            after=(auxg[0],), stream="ax0")
                tb = ph.dma("sync", aux[1][:],
                            dram["x1_f"][ti * 128:(ti + 1) * 128, :],
                            after=(auxg[1],), stream="ax1")
                auxg[0] = ph.op("vector", lambda e, b=b, ti=ti:
                                e.tensor_tensor(
                    acc[:, ti * D:(ti + 1) * D], psO[b][:], aux[0][:],
                    op=ALU.add), after=(tmm, ta), token=True)
                tv = ph.op("vector", lambda e, ti=ti: e.tensor_tensor(
                    acc[:, ti * D:(ti + 1) * D], acc[:, ti * D:(ti + 1) * D],
                    aux[1][:], op=ALU.add), after=(tb,), token=True)
                auxg[1] = tv
            psO_guard[b] = tv
            initoks.append(tv)

        acctoks = list(initoks)
        accdone = [initoks[-2], initoks[-1]]
        w1tok = {}
        w2tok = {}
        mm1done = None
        mm2done = None
        relu_prev = {}
        if layer == 0:
            relu_prev[-2] = tprev[0]
            relu_prev[-1] = tprev[1]
        for e_ in range(NE):
            for fh in range(2):
                for dc in range(ND):
                    w1tok[dc] = ph.dma(
                        "gpsimd", w1b[:, dc * 2048:(dc + 1) * 2048],
                        dram[L + "_w1"][e_, dc * 128:(dc + 1) * 128,
                                        fh * 2048:(fh + 1) * 2048],
                        after=(mm1done,), stream="w1")
                for ft in range(16):
                    w2tok[ft] = ph.dma(
                        "gpsimd", w2b[:, ft * D:(ft + 1) * D],
                        dram[L + "_w2"][e_, fh * 2048 + ft * 128:
                                        fh * 2048 + (ft + 1) * 128, :],
                        after=(mm2done,), stream="w2")
                for tc in range(4):
                    for ft in range(16):
                        b = ft % 2
                        aft = tuple(w1tok[dc] for dc in range(ND)) + \
                            (relu_prev.get(ft - 2), mm1done) + tuple(xt)
                        for dc in range(ND):
                            ph.op("tensor", lambda e, b=b, dc=dc, ft=ft,
                                  tc=tc: e.matmul(
                                psA[b][:],
                                w1b[:, dc * 2048 + ft * 128:
                                    dc * 2048 + (ft + 1) * 128],
                                xchb[:, dc * T + tc * 512:
                                     dc * T + (tc + 1) * 512],
                                start=(dc == 0), stop=(dc == ND - 1)),
                                after=aft if dc == 0 else ())
                        tmm = ph.tok_last("tensor")
                        col = e_ * 32 + fh * 16 + ft
                        relu_prev[ft] = ph.op(
                            "scalar", lambda e, b=b, ft=ft, col=col:
                            e.activation(
                                hT[:, ft * 512:(ft + 1) * 512], psA[b][:],
                                AF.Relu, bias=b1sb[:, col:col + 1]),
                            after=(tmm, b1_tok), token=True)
                    mm1done = tmm
                    for tt in range(4):
                        ti = tc * 4 + tt
                        ob = tt % 2
                        aft = tuple(relu_prev[ft] for ft in range(16)) + \
                            tuple(w2tok[ft] for ft in range(16)) + \
                            (accdone[ob],)
                        for ft in range(16):
                            for fo in range(2):
                                ph.op("tensor", lambda e, ob=ob, ft=ft, fo=fo,
                                      tt=tt: e.matmul(
                                    psO[ob][:, fo * 512:(fo + 1) * 512],
                                    hT[:, ft * 512 + tt * 128:
                                        ft * 512 + (tt + 1) * 128],
                                    w2b[:, ft * D + fo * 512:
                                        ft * D + (fo + 1) * 512],
                                    start=(ft == 0), stop=(ft == 15)),
                                    after=aft if (ft == 0 and fo == 0) else ())
                        tmm2 = ph.tok_last("tensor")
                        accdone[ob] = ph.op(
                            "vector", lambda e, ob=ob, ti=ti, e2=e_:
                            e.scalar_tensor_tensor(
                                acc[:, ti * D:(ti + 1) * D], psO[ob][:],
                                comb[:, ti * NE + e2:ti * NE + e2 + 1],
                                acc[:, ti * D:(ti + 1) * D],
                                op0=ALU.mult, op1=ALU.add),
                            after=(tmm2, acctoks[ti]), token=True)
                        acctoks[ti] = accdone[ob]
                    mm2done = tmm2

        es_.close()
        if layer == 0:
            tail_l0(p, g, ph, s_, acc, acctoks, psA)
        else:
            for ti in range(NT):
                ph.dma("sync", dram["out"][ti * 128:(ti + 1) * 128, :],
                       acc[:, ti * D:(ti + 1) * D],
                       after=(acctoks[ti], acctoks[-1]), stream="out")
        ph.emit()


def tail_l0(p, g, ph, s_, acc, acctoks, psA):
    dram = p.dram
    g2 = p.sb("t0_g2", [128, D], F32, s_)
    b2r = p.sb("t0_b2r", [128, D], F32, s_)
    lg2 = p.sb("t0_lg2", [1, D], F32, s_)
    lb2 = p.sb("t0_lb2", [1, D], F32, s_)
    mv = p.sb("t0_mv", [128, 16], F32, s_)
    rs = p.sb("t0_rs", [128, 1], F32, s_)
    hat = p.sb("t0_hat", [128, D], F32, s_)
    bb = [p.sb(f"t0_bb{i}", [128, D], F32, s_) for i in range(2)]
    x1 = [p.sb(f"t0_x1{i}", [128, D], F32, s_) for i in range(2)]
    xn1 = p.sb("t0_xn1", [128, D], F32, s_)
    xn1b = [p.sb(f"t0_xn1b{i}", [128, D], BF16, s_) for i in range(2)]
    stg2 = [p.sb(f"t0_stg{i}", [128, ND * 128], F32, s_) for i in range(2)]
    zc = p.sb("t0_zc", [128, 4], F32, s_)

    tap_x1 = p.tap("x1", [T, D])
    tap_xn1 = p.tap("xn1", [T, D])

    TG = acctoks[-1]
    t1 = ph.dma("sync", lg2[:], dram["l0_ln2_g"], after=(TG,), stream="p0")
    t2 = ph.dma("sync", lb2[:], dram["l0_ln2_b"], stream="p1")
    bt1 = bcast_row(ph, p, g, lg2, g2, psA, after=(t1, TG))
    bt = tuple(bt1)
    bt2 = bcast_row(ph, p, g, lb2, b2r, psA, after=(t2,),
                    guard=(bt1[0], bt1[1]))
    bt += tuple(bt2)
    zt = ph.op("vector", lambda e: e.memset(zc[:], 0.0), after=(TG,),
               token=True)
    for dc in range(ND):
        ph.dma("gpsimd", dram["xn1_ch"][dc * 128:(dc + 1) * 128, 0:4], zc[:],
               after=(zt,), stream="zc")

    bbg = [None, None]
    x1g = [None, None]
    stgg = [None, None]
    nbg = [None, None]
    x1st = [None, None]
    xn1tap = None
    trg = [bt2[-2], bt2[-1]]
    for ti in range(NT):
        b = ti % 2
        atile = acc[:, ti * D:(ti + 1) * D]
        tl = ph.dma("sync", bb[b][:],
                    dram["base0"][ti * 128:(ti + 1) * 128, :],
                    after=(bbg[b], TG), stream=f"bb{b}")
        tsa, tr = ln_stats(ph, mv, atile, rs, 1e-5,
                           extra_after=(acctoks[ti], TG) + bt)
        ph.op("vector", lambda e, atile=atile: e.tensor_scalar(
            hat[:], atile, mv[:, 12:13], rs[:], op0=ALU.subtract,
            op1=ALU.mult), after=(tr, tsa), force=True)
        ph.op("vector", lambda e: e.tensor_tensor(hat[:], hat[:], g2[:],
                                                  op=ALU.mult))
        ph.op("vector", lambda e: e.tensor_tensor(hat[:], hat[:], b2r[:],
                                                  op=ALU.add))
        tx = ph.op("vector", lambda e, b=b: e.tensor_tensor(
            x1[b][:], hat[:], bb[b][:], op=ALU.add),
            after=(tl, x1g[b], x1st[b]), token=True)
        bbg[b] = tx
        x1st[b] = ph.dma("gpsimd", dram["x1_f"][ti * 128:(ti + 1) * 128, :],
                         x1[b][:], after=(tx,), stream=f"xs{b}")
        if tap_x1 is not None:
            x1st[b] = ph.dma("gpsimd", tap_x1[ti * 128:(ti + 1) * 128, :],
                             x1[b][:], after=(tx,), stream=f"xs{b}")
        tsb, tr2 = ln_stats(ph, mv, x1[b], rs, 1e-5, extra_after=(tx,))
        tn = ph.op("vector", lambda e, b=b: e.tensor_scalar(
            xn1[:], x1[b][:], mv[:, 12:13], rs[:], op0=ALU.subtract,
            op1=ALU.mult), after=(tr2, tsb, xn1tap), force=True, token=True)
        x1g[b] = tn
        tc = ph.op("vector", lambda e, b=b: e.tensor_copy(xn1b[b][:], xn1[:]),
                   after=(nbg[b],), token=True)
        nbg[b] = ph.dma("gpsimd", dram["xn1_b"][ti * 128:(ti + 1) * 128, :],
                        xn1b[b][:], after=(tc,), stream=f"nb{b}")
        if tap_xn1 is not None:
            xn1tap = ph.dma("gpsimd", tap_xn1[ti * 128:(ti + 1) * 128, :],
                            xn1[:], after=(tn, stgg[b]), stream="tap")
        trtoks = []
        for dc in range(ND):
            tb_ = dc % 2
            ttr = ph.op("tensor", lambda e, tb_=tb_, dc=dc: e.transpose(
                psA[tb_][:, 0:128], xn1[:, dc * 128:(dc + 1) * 128],
                g["idf"][:]), after=(tn, trg[tb_], stgg[b]), token=True)
            trg[tb_] = ph.op("vector", lambda e, tb_=tb_, dc=dc, b=b:
                             e.tensor_copy(
                stg2[b][:, dc * 128:(dc + 1) * 128], psA[tb_][:, 0:128]),
                after=(ttr,), token=True)
            trtoks.append(trg[tb_])
        last = None
        for dc in range(ND):
            last = ph.dma(
                "gpsimd",
                dram["xn1_ch"][dc * 128:(dc + 1) * 128,
                               4 + ti * 128:4 + (ti + 1) * 128],
                stg2[b][:, dc * 128:(dc + 1) * 128],
                after=(trtoks[dc],), stream=f"ch{b}")
        stgg[b] = last


def phase_rwkv(p, g):
    dram = p.dram
    ph = Ph(p, "rwkv")
    with contextlib.ExitStack() as s_:
        wmaa1 = p.sb("r_wmaa1", [128, ND * 192], BF16, s_)
        wmaa2 = p.sb("r_wmaa2", [64, 3 * D], BF16, s_)
        wd1 = p.sb("r_wd1", [128, ND * 64], BF16, s_)
        wd2 = p.sb("r_wd2", [64, D], BF16, s_)
        wa1 = p.sb("r_wa1", [128, ND * 64], BF16, s_)
        wa2 = p.sb("r_wa2", [64, D], BF16, s_)
        wk1 = p.sb("r_wk1", [128, ND * 64], BF16, s_)
        wk2 = p.sb("r_wk2", [64, D], BF16, s_)
        wg1 = p.sb("r_wg1", [128, ND * 128], BF16, s_)
        wg2 = p.sb("r_wg2", [128, D], BF16, s_)
        wr = p.sb("r_wr", [128, ND * D], BF16, s_)
        wkk = p.sb("r_wk", [128, ND * D], BF16, s_)
        wv = p.sb("r_wv", [128, ND * D], BF16, s_)
        wo = p.sb("r_wo", [128, ND * D], BF16, s_)
        vecs = {}
        for v in ["time_maa_x", "time_maa_r", "time_maa_w", "time_maa_k",
                  "time_maa_v", "time_maa_a", "time_maa_g", "time_decay",
                  "time_aaaaa", "faaaa"]:
            vecs[v] = p.sb("r_" + v, [128, ND], F32, s_)
        lnw = p.sb("r_lnw", [128, D], F32, s_)
        lnb = p.sb("r_lnb", [128, D], F32, s_)
        l1v = p.sb("r_l1v", [1, D], F32, s_)
        l2v = p.sb("r_l2v", [1, D], F32, s_)
        hb = p.sb("r_hb", [128, ND * H], BF16, s_)
        SS = p.sb("r_SS", [128, 512], F32, s_)
        SSb = p.sb("r_SSb", [128, 512], BF16, s_)

        def scbuf(n, w=SCN, dt=F32):
            return p.sb("r_" + n, [128, ND * w], dt, s_)

        xch = scbuf("xch", SCN + 1)
        xx = scbuf("xx")
        rT = scbuf("rT")
        wraw = scbuf("wraw")
        ks = scbuf("ks")
        kkn = scbuf("kkn")
        ag = scbuf("ag")
        gT = scbuf("gT")
        vtok = p.sb("r_vtok", [128, CPS * D], F32, s_)
        kkn_tok = p.sb("r_kkntok", [128, CPS * D], F32, s_)
        y_sc = p.sb("r_ysc", [128, CPS * D], F32, s_)
        xs = p.sb("r_xs", [128, ND * SCN], BF16, s_)
        mixbuf = p.sb("r_mixbuf", [64, 3 * SCN], BF16, s_)
        tnh = p.sb("r_tnh", [64, SCN], BF16, s_)
        tnh128 = p.sb("r_tnh128", [128, SCN], BF16, s_)
        tmpa = [p.sb(f"r_tmpa{i}", [128, SCN], F32, s_) for i in range(3)]

        p_ch = p.sb("r_pch", [128, D], F32, s_)
        p_tok = p.sb("r_ptok", [128, D], F32, s_)
        E0 = p.sb("r_E0", [128, D], F32, s_)
        E1 = p.sb("r_E1", [128, D], F32, s_)
        E2 = p.sb("r_E2", [128, D], F32, s_)
        gl = p.sb("r_gl", [128, ND], F32, s_)
        Ab = p.sb("r_Ab", [128, D], BF16, s_)
        Rb = p.sb("r_Rb", [128, D], BF16, s_)
        Kcb = p.sb("r_Kcb", [128, D], BF16, s_)
        Bcb = p.sb("r_Bcb", [128, D], BF16, s_)
        Ktb = p.sb("r_Ktb", [128, D], BF16, s_)
        Btb = p.sb("r_Btb", [128, D], BF16, s_)
        Vb = p.sb("r_Vb", [128, D], BF16, s_)
        GtS = [p.sb(f"r_Gt{s}", [128, 128], BF16, s_) for s in range(NSL)]
        UtS = [[p.sb(f"r_Ut{s}_{i}", [128, 128], BF16, s_) for i in range(2)]
               for s in range(NSL)]
        MtS = [[p.sb(f"r_Mt{s}_{i}", [128, 128], BF16, s_) for i in range(7)]
               for s in range(NSL)]
        WtS = [p.sb(f"r_Wt{s}", [128, 128], BF16, s_) for s in range(NSL)]
        W2tS = [p.sb(f"r_W2t{s}", [128, 128], BF16, s_) for s in range(NSL)]
        XbS = [p.sb(f"r_Xb{s}", [128, 64], BF16, s_) for s in range(NSL)]
        bscal = p.sb("r_bscal", [128, H], F32, s_)
        ygb = p.sb("r_ygb", [128, D], BF16, s_)
        mvg = p.sb("r_mvg", [128, 16], F32, s_)
        rsg = p.sb("r_rsg", [128, 1], F32, s_)

        psA = [p.ps(f"r_psA{i}", [128, 512], s_) for i in range(2)]
        psB = [p.ps(f"r_psB{i}", [128, 128], s_) for i in range(2)]
        pX = p.ps("r_pX", [128, 64], s_)
        pP7 = p.ps("r_pP7", [128, 64], s_)
        ps96 = [p.ps(f"r_ps96{i}", [128, 128], s_) for i in range(2)]

        psg = {"A0": None, "A1": None, "B0": None, "B1": None,
               "X": None, "P7": None, "960": None, "961": None}
        bankD1 = {"B0": lambda: psB[0][:], "B1": lambda: psB[1][:],
                  "A0": lambda: psA[0][:, 0:128],
                  "A1": lambda: psA[1][:, 0:128]}
        bankD2 = {"X": lambda: pX[:], "P7": lambda: pP7[:]}
        bankD3 = {"960": lambda: ps96[0][0:64, 0:64],
                  "961": lambda: ps96[1][0:64, 0:64]}
        _pc = {"1": 0, "2": 0, "3": 0}

        def nbank(d):
            pool, m = {"1": (["B0", "B1", "A0", "A1"], bankD1),
                       "2": (["X", "P7"], bankD2),
                       "3": (["960", "961"], bankD3)}[d]
            k = pool[_pc[d] % len(pool)]
            _pc[d] += 1
            return k, m[k]()

        tap_att1 = p.tap("att1", [T, D])
        tap_y = p.tap("rwkv_y", [T, D])

        wt = []
        for dc in range(ND):
            d0 = dc * 128
            wt.append(ph.dma("gpsimd", wmaa1[:, dc * 192:(dc + 1) * 192],
                             dram["l1_maa_w1"][d0:d0 + 128, :], stream="wl"))
            wt.append(ph.dma("gpsimd", wd1[:, dc * 64:(dc + 1) * 64],
                             dram["l1_decay_w1"][d0:d0 + 128, :], stream="wl"))
            wt.append(ph.dma("gpsimd", wa1[:, dc * 64:(dc + 1) * 64],
                             dram["l1_aaa_w1"][d0:d0 + 128, :], stream="wl"))
            wt.append(ph.dma("gpsimd", wk1[:, dc * 64:(dc + 1) * 64],
                             dram["l1_kkk_w1"][d0:d0 + 128, :], stream="wl"))
            wt.append(ph.dma("gpsimd", wg1[:, dc * 128:(dc + 1) * 128],
                             dram["l1_gate_w1"][d0:d0 + 128, :], stream="wl"))
            for wnm, wdst in [("l1_w_r", wr), ("l1_w_k", wkk),
                              ("l1_w_v", wv), ("l1_w_o", wo)]:
                wt.append(ph.dma("gpsimd", wdst[:, dc * D:(dc + 1) * D],
                                 dram[wnm][d0:d0 + 128, :], stream="wl"))
        for wnm, wdst in [("l1_maa_w2", wmaa2), ("l1_decay_w2", wd2),
                          ("l1_aaa_w2", wa2), ("l1_kkk_w2", wk2),
                          ("l1_gate_w2", wg2)]:
            wt.append(ph.dma("gpsimd", wdst[:], dram[wnm], stream="wl"))
        wt.append(ph.dma("sync", hb[:], dram["c_headblk_bf"], stream="wl2"))
        for v in vecs:
            wt.append(ph.dma("sync", vecs[v][:], dram["l1_" + v],
                             stream="wl2"))
        tl1 = ph.dma("sync", l1v[:], dram["l1_ln_w"], stream="lv1")
        tl2 = ph.dma("sync", l2v[:], dram["l1_ln_b"], stream="lv2")
        bc1 = bcast_row(ph, p, g, l1v, lnw, psA, after=(tl1,))
        bc2 = bcast_row(ph, p, g, l2v, lnb, psA, after=(tl2,),
                        guard=(bc1[0], bc1[1]))
        wt += bc1 + bc2
        psg["A0"] = bc2[-2]
        psg["A1"] = bc2[-1]
        sst = ph.op("vector", lambda e: e.memset(SS[:], 0.0), token=True)
        wt.append(sst)
        WT = tuple(wt)

        def chs(buf, dc, c0=0, w=SCN, ext=0):
            base = dc * (SCN + ext)
            return buf[:, base + c0:base + c0 + w]

        def dview(buf):
            return buf[:].rearrange("p (a b) -> p a b", b=128)

        def MM(bank, fn, after=(), token=True):
            return ph.op("tensor", fn, after=tuple(after) + (psg[bank],),
                         token=token)

        SNAMES = ["r", "w", "k", "v", "a", "g"]
        tmpg = {0: None, 1: None, 2: None}

        prev_sc = None
        CG = ()
        ssb_prev = sst
        for sc in range(NSC):
            ci = sc
            xtl = []
            for dc in range(ND):
                xtl.append(ph.dma(
                    "sync", chs(xch, dc, 0, SCN + 1, 1),
                    dram["xn1_ch"][dc * 128:(dc + 1) * 128,
                                   sc * SCN + 3:sc * SCN + SCN + 4],
                    after=(prev_sc,), stream="xc"))
            xxt = []
            for dc in range(ND):
                xxt.append(ph.op("vector", lambda e, dc=dc: e.tensor_tensor(
                    chs(xx, dc), chs(xch, dc, 0, SCN, 1),
                    chs(xch, dc, 1, SCN, 1), op=ALU.subtract),
                    after=tuple(xtl) + WT + CG, token=True))
            mmxt = []
            for dc in range(ND):
                mmxt.append(ph.op("vector", lambda e, dc=dc:
                                  e.scalar_tensor_tensor(
                    chs(xs, dc), chs(xx, dc),
                    vecs["time_maa_x"][:, dc:dc + 1], chs(xch, dc, 1, SCN, 1),
                    op0=ALU.mult, op1=ALU.add), after=(xxt[dc],), token=True))
            mixt = {}
            for g2_ in range(2):
                bank = "96" + str(g2_)
                for dc in range(ND):
                    if dc == 0:
                        MM(bank, lambda e, g2_=g2_: e.matmul(
                            ps96[g2_][0:96, :],
                            wmaa1[:, g2_ * 96:g2_ * 96 + 96],
                            chs(xs, 0), start=True, stop=False),
                            after=tuple(mmxt), token=False)
                    else:
                        ph.op("tensor", lambda e, g2_=g2_, dc=dc: e.matmul(
                            ps96[g2_][0:96, :],
                            wmaa1[:, dc * 192 + g2_ * 96:
                                  dc * 192 + (g2_ + 1) * 96],
                            chs(xs, dc), start=False, stop=(dc == ND - 1)))
                tmm = ph.tok_last("tensor")
                for s3 in range(3):
                    s6 = g2_ * 3 + s3
                    par = (s6 % 2) * 32
                    blk_ = (s6 // 2) * SCN
                    mixt[s6] = ph.op("scalar", lambda e, g2_=g2_, s3=s3,
                                     par=par, blk_=blk_: e.activation(
                        mixbuf[par:par + 32, blk_:blk_ + SCN],
                        ps96[g2_][s3 * 32:(s3 + 1) * 32, :], AF.Tanh),
                        after=(tmm,), token=True)
                psg[bank] = mixt[g2_ * 3 + 2]

            xs_guard = None
            pd = {}
            for s6 in range(6):
                vname = "time_maa_" + SNAMES[s6]
                par = (s6 % 2) * 32
                blk_ = (s6 // 2) * SCN
                mslice = mixbuf[par:par + 32, blk_:blk_ + SCN]
                w2slice = lambda dc, par=par, s6=s6: wmaa2[
                    par:par + 32,
                    (s6 // 2) * D + dc * 128:(s6 // 2) * D + (dc + 1) * 128]
                xst = []
                for dc in range(ND):
                    b = dc % 2
                    bank = "A" + str(b)
                    tmm = MM(bank, lambda e, dc=dc, b=b, mslice=mslice,
                             w2s=w2slice(dc): e.matmul(
                        psA[b][:, 0:SCN], w2s, mslice, start=True, stop=True),
                        after=(mixt[s6], xs_guard))
                    tv = ph.op("vector", lambda e, b=b, dc=dc, vname=vname:
                               e.tensor_scalar(
                        tmpa[b][:], psA[b][:, 0:SCN],
                        vecs[vname][:, dc:dc + 1], None, op0=ALU.add),
                        after=(tmm, tmpg[b]), token=True)
                    psg[bank] = tv
                    ph.op("vector", lambda e, b=b, dc=dc: e.tensor_tensor(
                        tmpa[b][:], tmpa[b][:], chs(xx, dc), op=ALU.mult),
                        after=(tv,))
                    xst.append(ph.op("vector", lambda e, b=b, dc=dc:
                                     e.tensor_tensor(
                        chs(xs, dc), tmpa[b][:], chs(xch, dc, 1, SCN, 1),
                        op=ALU.add), token=True))
                    tmpg[b] = xst[-1]
                XST = tuple(xst)

                if s6 in (0, 2):
                    W_ = wr if s6 == 0 else wkk
                    dst = rT if s6 == 0 else ks
                    ev = [None, None]
                    for do in range(ND):
                        b = do % 2
                        bank = "A" + str(b)
                        for dc in range(ND):
                            if dc == 0:
                                MM(bank, lambda e, b=b, do=do, W_=W_:
                                   e.matmul(
                                    psA[b][:, 0:SCN],
                                    W_[:, do * 128:(do + 1) * 128],
                                    chs(xs, 0), start=True, stop=False),
                                    after=XST, token=False)
                            else:
                                ph.op("tensor", lambda e, b=b, dc=dc, do=do,
                                      W_=W_: e.matmul(
                                    psA[b][:, 0:SCN],
                                    W_[:, dc * D + do * 128:
                                       dc * D + (do + 1) * 128],
                                    chs(xs, dc), start=False,
                                    stop=(dc == ND - 1)))
                        tmm = ph.tok_last("tensor")
                        ev[b] = ph.op("vector", lambda e, b=b, do=do, dst=dst:
                                      e.tensor_copy(chs(dst, do),
                                                    psA[b][:, 0:SCN]),
                                      after=(tmm,), token=True)
                        psg[bank] = ev[b]
                    if s6 == 0:
                        pd[(0, 0)], pd[(0, 1)] = ev[0], ev[1]
                    else:
                        for dc in range(ND):
                            if dc == 0:
                                MM("A0", lambda e: e.matmul(
                                    psA[0][0:64, 0:SCN], wk1[:, 0:64],
                                    chs(xs, 0), start=True, stop=False),
                                    after=XST, token=False)
                            else:
                                ph.op("tensor", lambda e, dc=dc: e.matmul(
                                    psA[0][0:64, 0:SCN],
                                    wk1[:, dc * 64:(dc + 1) * 64],
                                    chs(xs, dc), start=False,
                                    stop=(dc == ND - 1)))
                        tmm = ph.tok_last("tensor")
                        tt1 = ph.op("scalar", lambda e: e.activation(
                            tnh[:], psA[0][0:64, 0:SCN], AF.Tanh),
                            after=(tmm,), token=True)
                        psg["A0"] = tt1
                        kkt = []
                        kst = []
                        for do in range(ND):
                            b = do % 2
                            bank = f"B{b}"
                            tmm = MM(bank, lambda e, do=do, b=b: e.matmul(
                                psB[b][:, 0:SCN],
                                wk2[:, do * 128:(do + 1) * 128], tnh[:],
                                start=True, stop=True), after=(tt1,))
                            kkt.append(ph.op(
                                "vector", lambda e, do=do, b=b:
                                e.tensor_tensor(
                                    chs(kkn, do), psB[b][:, 0:SCN],
                                    chs(ks, do), op=ALU.add),
                                after=(tmm, ev[do % 2]), token=True))
                            psg[bank] = kkt[-1]
                            t1_ = ph.op("vector", lambda e, do=do, b=b:
                                        e.tensor_scalar(
                                tmpa[b][:], chs(wraw, do), 0.5, 0.0,
                                op0=ALU.mult, op1=ALU.min),
                                after=(pd[(1, "wr", do)], tmpg[b]),
                                token=True)
                            t2_ = ph.op("scalar", lambda e, b=b: e.activation(
                                tmpa[b][:], tmpa[b][:], AF.Exp),
                                after=(t1_,), token=True)
                            kst.append(ph.op("vector", lambda e, do=do, b=b:
                                             e.tensor_tensor(
                                chs(ks, do), chs(ks, do), tmpa[b][:],
                                op=ALU.mult),
                                after=(t2_, kkt[-1]), token=True))
                            tmpg[b] = kst[-1]
                        pd[(2, 0)] = kkt[-1]
                        pd[(2, 1)] = kst[-1]
                        pd[(2, "kk")] = tuple(kkt)
                        pd[(2, "ks")] = tuple(kst)
                elif s6 == 1:
                    for dc in range(ND):
                        if dc == 0:
                            MM("A0", lambda e: e.matmul(
                                psA[0][0:64, 0:SCN], wd1[:, 0:64],
                                chs(xs, 0), start=True, stop=False),
                                after=XST, token=False)
                        else:
                            ph.op("tensor", lambda e, dc=dc: e.matmul(
                                psA[0][0:64, 0:SCN],
                                wd1[:, dc * 64:(dc + 1) * 64],
                                chs(xs, dc), start=False,
                                stop=(dc == ND - 1)))
                    tmm = ph.tok_last("tensor")
                    tt1 = ph.op("scalar", lambda e: e.activation(
                        tnh[:], psA[0][0:64, 0:SCN], AF.Tanh), after=(tmm,),
                        token=True)
                    psg["A0"] = tt1
                    wrt = []
                    for do in range(ND):
                        b = do % 2
                        bank = f"B{b}"
                        tmm = MM(bank, lambda e, do=do, b=b: e.matmul(
                            psB[b][:, 0:SCN],
                            wd2[:, do * 128:(do + 1) * 128], tnh[:],
                            start=True, stop=True), after=(tt1,))
                        tz = ph.op("vector", lambda e, b=b, do=do:
                                   e.tensor_scalar(
                            tmpa[b][:], psB[b][:, 0:SCN],
                            vecs["time_decay"][:, do:do + 1], None,
                            op0=ALU.add), after=(tmm, tmpg[b]), token=True)
                        psg[bank] = tz
                        ta1 = ph.op("scalar", lambda e, b=b: e.activation(
                            tmpa[2][:], tmpa[b][:], AF.Abs),
                            after=(tmpg[2], tz), token=True)
                        ph.op("scalar", lambda e: e.activation(
                            tmpa[2][:], tmpa[2][:], AF.Exp, scale=-1.0))
                        ta = ph.op("scalar", lambda e: e.activation(
                            tmpa[2][:], tmpa[2][:], AF.Ln, bias=1.0),
                            token=True)
                        ph.op("vector", lambda e, b=b: e.tensor_scalar(
                            tmpa[b][:], tmpa[b][:], -1.0, 0.0,
                            op0=ALU.mult, op1=ALU.max), after=(tz, ta1))
                        tadd = ph.op("vector", lambda e, b=b:
                                     e.tensor_tensor(
                            tmpa[b][:], tmpa[b][:], tmpa[2][:], op=ALU.add),
                            after=(ta,), token=True)
                        tmpg[2] = tadd
                        wrt.append(ph.op("vector", lambda e, b=b, do=do:
                                         e.tensor_scalar(
                            chs(wraw, do), tmpa[b][:], -1.0, -0.5,
                            op0=ALU.mult, op1=ALU.add), token=True))
                        pd[(1, "wr", do)] = wrt[-1]
                        tmpg[b] = wrt[-1]
                    pd[(1, 0)] = wrt[-2]
                    pd[(1, 1)] = wrt[-1]
                elif s6 == 3:
                    vt = []
                    for fo in range(2):
                        b = fo % 2
                        bank = "A" + str(b)
                        for dc in range(ND):
                            if dc == 0:
                                MM(bank, lambda e, b=b, fo=fo: e.matmul(
                                    psA[b][:], chs(xs, 0),
                                    wv[:, fo * 512:(fo + 1) * 512],
                                    start=True, stop=False),
                                    after=XST, token=False)
                            else:
                                ph.op("tensor", lambda e, b=b, dc=dc, fo=fo:
                                      e.matmul(
                                    psA[b][:], chs(xs, dc),
                                    wv[:, dc * D + fo * 512:
                                       dc * D + (fo + 1) * 512],
                                    start=False, stop=(dc == ND - 1)))
                        tmm = ph.tok_last("tensor")
                        vt.append(ph.op(
                            "vector", lambda e, fo=fo, b=b: e.tensor_copy(
                                vtok[:, fo * 512:(fo + 1) * 512], psA[b][:]),
                            after=(tmm,), token=True))
                        psg[bank] = vt[-1]
                    pd[(3, 0)] = vt[-2]
                    pd[(3, 1)] = vt[-1]
                elif s6 == 4:
                    for dc in range(ND):
                        if dc == 0:
                            MM("A0", lambda e: e.matmul(
                                psA[0][0:64, 0:SCN], wa1[:, 0:64],
                                chs(xs, 0), start=True, stop=False),
                                after=XST, token=False)
                        else:
                            ph.op("tensor", lambda e, dc=dc: e.matmul(
                                psA[0][0:64, 0:SCN],
                                wa1[:, dc * 64:(dc + 1) * 64],
                                chs(xs, dc), start=False,
                                stop=(dc == ND - 1)))
                    tmm = ph.tok_last("tensor")
                    tt1 = ph.op("scalar", lambda e: e.activation(
                        tnh[:], psA[0][0:64, 0:SCN], AF.Copy), after=(tmm,),
                        token=True)
                    psg["A0"] = tt1
                    agt = []
                    for do in range(ND):
                        b = do % 2
                        bank = f"B{b}"
                        tmm = MM(bank, lambda e, do=do, b=b: e.matmul(
                            psB[b][:, 0:SCN],
                            wa2[:, do * 128:(do + 1) * 128], tnh[:],
                            start=True, stop=True), after=(tt1,))
                        ts_ = ph.op("scalar", lambda e, b=b, do=do:
                                    e.activation(
                            chs(ag, do), psB[b][:, 0:SCN], AF.Sigmoid,
                            bias=vecs["time_aaaaa"][:, do:do + 1]),
                            after=(tmm,), token=True)
                        psg[bank] = ts_
                        agt.append(ph.op("scalar", lambda e, do=do: e.mul(
                            chs(ag, do), chs(ag, do), 2.0), token=True))
                    pd[(4, 0)] = agt[-2]
                    pd[(4, 1)] = agt[-1]
                else:
                    for dc in range(ND):
                        if dc == 0:
                            MM("A0", lambda e: e.matmul(
                                psA[0][:, 0:SCN], wg1[:, 0:128],
                                chs(xs, 0), start=True, stop=False),
                                after=XST, token=False)
                        else:
                            ph.op("tensor", lambda e, dc=dc: e.matmul(
                                psA[0][:, 0:SCN],
                                wg1[:, dc * 128:(dc + 1) * 128],
                                chs(xs, dc), start=False,
                                stop=(dc == ND - 1)))
                    tmm = ph.tok_last("tensor")
                    tt1 = ph.op("scalar", lambda e: e.activation(
                        tnh128[:], psA[0][:, 0:SCN], AF.Tanh), after=(tmm,),
                        token=True)
                    psg["A0"] = tt1
                    ggt = []
                    for do in range(ND):
                        b = do % 2
                        bank = f"B{b}"
                        tmm = MM(bank, lambda e, do=do, b=b: e.matmul(
                            psB[b][:, 0:SCN],
                            wg2[:, do * 128:(do + 1) * 128], tnh128[:],
                            start=True, stop=True), after=(tt1,))
                        ggt.append(ph.op("vector", lambda e, b=b, do=do:
                                         e.tensor_copy(
                            chs(gT, do), psB[b][:, 0:SCN]), after=(tmm,),
                            token=True))
                        psg[bank] = ggt[-1]
                    pd[(5, 0)] = ggt[-2]
                    pd[(5, 1)] = ggt[-1]
                xs_guard = pd[(s6, 1)]

            kknt = []
            for co in range(CPS):
                fw = []
                for dc in range(ND):
                    b = dc % 2
                    bank = f"B{b}"
                    ttr = MM(bank, lambda e, dc=dc, co=co, b=b: e.transpose(
                        psB[b][:], chs(kkn, dc)[:, co * 128:(co + 1) * 128],
                        g["idf"][:]), after=pd[(2, "kk")] + pd[(2, "ks")])
                    tcp = ph.op("vector", lambda e, b=b, dc=dc, co=co:
                                e.tensor_copy(
                        kkn_tok[:, co * D + dc * 128:co * D + (dc + 1) * 128],
                        psB[b][:]), after=(ttr,) + CG, token=True)
                    psg[bank] = tcp
                    fw.append(tcp)
                tq = ph.op("scalar", lambda e, co=co: e.activation(
                    ygb[:], kkn_tok[:, co * D:(co + 1) * D], AF.Square,
                    accum_out=rsg[:]), after=tuple(fw) + CG, token=True)
                tq = ph.op("scalar", lambda e: e.activation(
                    rsg[:], rsg[:], AF.Ln), after=(tq,), force=True)
                tq = ph.op("scalar", lambda e: e.activation(
                    rsg[:], rsg[:], AF.Exp, scale=-0.5), token=True)
                tq2 = ph.op("vector", lambda e: e.tensor_scalar(
                    rsg[:], rsg[:], 1e12, None, op0=ALU.min), after=(tq,))
                tq2 = ph.op("vector", lambda e, co=co: e.tensor_scalar(
                    kkn_tok[:, co * D:(co + 1) * D],
                    kkn_tok[:, co * D:(co + 1) * D], rsg[:], None,
                    op0=ALU.mult), token=True)
                for dc in range(ND):
                    b = dc % 2
                    bank = f"B{b}"
                    ttr = MM(bank, lambda e, b=b, dc=dc, co=co: e.transpose(
                        psB[b][:],
                        kkn_tok[:, co * D + dc * 128:co * D + (dc + 1) * 128],
                        g["idf"][:]), after=(tq2,))
                    tcp = ph.op("vector", lambda e, b=b, dc=dc, co=co:
                                e.tensor_copy(
                        chs(kkn, dc)[:, co * 128:(co + 1) * 128], psB[b][:]),
                        after=(ttr,), token=True)
                    psg[bank] = tcp
                    kknt.append(tcp)
            KKNT = tuple(kknt)

            for co in range(CPS):
                ssb_t = ph.op("scalar", lambda e: e.copy(SSb[:], SS[:]),
                              after=(ssb_prev,) + CG, token=True)
                pcht = []
                for dc in range(ND):
                    pcht.append(ph.op("scalar", lambda e, dc=dc, co=co:
                                      e.activation(
                        p_ch[:, dc * 128:(dc + 1) * 128],
                        chs(wraw, dc)[:, co * 128:(co + 1) * 128],
                        AF.Exp), after=(pd[(1, 0)], pd[(1, 1)]) + CG,
                        token=True))
                ptt = []
                trg2 = [None, None]
                for dc in range(ND):
                    b = dc % 2
                    bank = f"B{b}"
                    ttr = MM(bank, lambda e, b=b, dc=dc: e.transpose(
                        psB[b][:], p_ch[:, dc * 128:(dc + 1) * 128],
                        g["idf"][:]), after=(pcht[dc],))
                    tcp = ph.op("vector", lambda e, b=b, dc=dc:
                                e.tensor_copy(
                        p_tok[:, dc * 128:(dc + 1) * 128], psB[b][:]),
                        after=(ttr,) + CG, token=True)
                    psg[bank] = tcp
                    ptt.append(tcp)
                pchT = []
                for dc in range(ND):
                    b = dc % 2
                    bank = f"B{b}"
                    sl = slice(dc * 128, (dc + 1) * 128)
                    tmm = MM(bank, lambda e, b=b, dc=dc: e.matmul(
                        psB[b][:], p_tok[:, dc * 128:(dc + 1) * 128],
                        g["tri_incl"][:], start=True, stop=True),
                        after=tuple(ptt))
                    ph.op("scalar", lambda e, b=b, sl=sl: e.activation(
                        E0[:, sl], psB[b][:], AF.Exp, scale=-1.0),
                        after=(tmm,) + CG)
                    ph.op("scalar", lambda e, b=b, sl=sl: e.activation(
                        E2[:, sl], psB[b][:], AF.Exp), token=False)
                    t3 = ph.op("scalar", lambda e, b=b, sl=sl: e.activation(
                        E1[:, sl], psB[b][:], AF.Copy, accum_out=None),
                        token=True)
                    tcp = ph.op("vector", lambda e, b=b, sl=sl:
                                e.tensor_tensor(
                        E1[:, sl], E1[:, sl], p_ch[:, sl], op=ALU.subtract),
                        after=(t3, pcht[dc]) + CG, token=True)
                    psg[bank] = tcp
                    pchT.append(tcp)
                t_e0 = ph.op("scalar", lambda e: e.nop(),
                             after=tuple(pchT), token=True)
                t_e2 = t_e0
                t_e1 = ph.op("scalar", lambda e: e.activation(
                    E1[:], E1[:], AF.Exp, scale=-1.0),
                    after=tuple(pchT) + CG, token=True)
                glt = []
                for dc in range(ND):
                    glt.append(ph.op("vector", lambda e, dc=dc:
                                     e.tensor_copy(
                        gl[:, dc:dc + 1],
                        E0[:, dc * 128 + 127:dc * 128 + 128]),
                        after=(t_e0,), token=True))
                t_ab = ph.op("vector", lambda e: e.scalar_tensor_tensor(
                    dview(Ab), dview(kkn), -1.0, dview(E1),
                    op0=ALU.mult, op1=ALU.mult), after=KKNT + (t_e1,) + CG,
                    token=True)
                t_rb = ph.op("vector", lambda e: e.tensor_tensor(
                    dview(Rb), dview(rT), dview(E0), op=ALU.mult),
                    after=(t_e0, pd[(0, 0)], pd[(0, 1)]) + CG, token=True)
                t_kcb = ph.op("vector", lambda e: e.tensor_tensor(
                    dview(Kcb), dview(ks), dview(E2), op=ALU.mult),
                    after=(t_e2,) + pd[(2, "ks")] + CG, token=True)
                t_tk = ph.op("vector", lambda e: e.tensor_tensor(
                    dview(p_tok), dview(kkn), dview(ag), op=ALU.mult),
                    after=KKNT + (t_e1, pd[(4, 0)], pd[(4, 1)]), token=True)
                t_bcb = ph.op("vector", lambda e: e.tensor_tensor(
                    Bcb[:], p_tok[:], E2[:], op=ALU.mult),
                    after=(t_tk, t_e2) + CG, token=True)
                t_vb = ph.op("scalar", lambda e, co=co: e.copy(
                    Vb[:], vtok[:, co * D:(co + 1) * D]),
                    after=(pd[(3, 0)], pd[(3, 1)]) + CG, token=True)
                ktbt = []
                for dc in range(ND):
                    b = dc % 2
                    bank = f"B{b}"
                    ttr = MM(bank, lambda e, b=b, dc=dc: e.matmul(
                        psB[b][:], Kcb[:, dc * 128:(dc + 1) * 128],
                        g["idb"][:], start=True, stop=True), after=(t_kcb,))
                    tcp = ph.op("vector", lambda e, b=b, dc=dc:
                                e.tensor_copy(
                        Ktb[:, dc * 128:(dc + 1) * 128], psB[b][:]),
                        after=(ttr,) + CG, token=True)
                    psg[bank] = tcp
                    ktbt.append(tcp)
                btbt = []
                for dc in range(ND):
                    b = dc % 2
                    bank = f"B{b}"
                    ttr = MM(bank, lambda e, b=b, dc=dc: e.matmul(
                        psB[b][:], Bcb[:, dc * 128:(dc + 1) * 128],
                        g["idb"][:], start=True, stop=True), after=(t_bcb,))
                    btbt.append(ph.op("vector", lambda e, b=b, dc=dc:
                                      e.tensor_copy(
                        Btb[:, dc * 128:(dc + 1) * 128], psB[b][:]),
                        after=(ttr,) + CG, token=True))
                    psg[bank] = btbt[-1]
                ARR = (t_ab, t_rb, t_kcb, t_bcb, t_vb, ssb_t) + tuple(ktbt) \
                    + tuple(btbt) + tuple(glt)

                ssdone = []

                def head_gen(h):
                    slot = h % NSL
                    off = (h % 2) * 64
                    dct = h // 2
                    aKc = Kcb[off:off + 64, dct * 128:(dct + 1) * 128]
                    aAb = Ab[off:off + 64, dct * 128:(dct + 1) * 128]
                    aBc = Bcb[off:off + 64, dct * 128:(dct + 1) * 128]
                    aRb = Rb[off:off + 64, dct * 128:(dct + 1) * 128]
                    aSSb = SSb[off:off + 64, dct * 64:(dct + 1) * 64]
                    aSS = SS[off:off + 64, dct * 64:(dct + 1) * 64]
                    aVb = Vb[:, h * 64:(h + 1) * 64]
                    aKt = Ktb[:, h * 64:(h + 1) * 64]
                    aBt = Btb[:, h * 64:(h + 1) * 64]
                    agl = gl[off:off + 64, dct:dct + 1]
                    Gt, Wt, W2t, Xb = GtS[slot], WtS[slot], W2tS[slot], \
                        XbS[slot]
                    Ut, Mt = UtS[slot], MtS[slot]

                    kb, bp = nbank("1")
                    tg = MM(kb, lambda e, bp=bp: e.matmul(
                        bp, aKc, aAb, start=True, stop=True), after=ARR)
                    tge = ph.op("vector", lambda e, bp=bp: e.tensor_tensor(
                        Gt[:], bp, g["mask_su"][:], op=ALU.mult),
                        after=(tg,), token=True)
                    psg[kb] = tge
                    yield
                    kb, bp = nbank("1")
                    tu = MM(kb, lambda e, bp=bp: e.matmul(
                        bp, aBc, aAb, start=True, stop=True), after=ARR)
                    tue = ph.op("vector", lambda e, bp=bp: e.tensor_tensor(
                        Ut[0][:], bp, g["mask_su"][:], op=ALU.mult),
                        after=(tu,), token=True)
                    psg[kb] = tue
                    yield
                    kb, bp = nbank("1")
                    tm_ = MM(kb, lambda e, bp=bp: e.matmul(
                        bp, aAb, aBc, start=True, stop=True))
                    tme = ph.op("vector", lambda e, bp=bp: e.tensor_tensor(
                        Mt[0][:], bp, g["mask_sl"][:], op=ALU.mult),
                        after=(tm_,), token=True)
                    psg[kb] = tme
                    yield
                    pe = {0: (tue, tme)}
                    for kp in range(1, 7):
                        kb, bp = nbank("1")
                        tu_ = MM(kb, lambda e, kp=kp, bp=bp: e.matmul(
                            bp, Mt[kp - 1][:], Ut[(kp - 1) % 2][:],
                            start=True, stop=True), after=pe[kp - 1])
                        ue = ph.op("vector", lambda e, kp=kp, bp=bp:
                                   e.tensor_copy(
                            Ut[kp % 2][:], bp), after=(tu_,), token=True)
                        psg[kb] = ue
                        kb2, bp2 = nbank("1")
                        tm2 = MM(kb2, lambda e, kp=kp, bp2=bp2: e.matmul(
                            bp2, Ut[(kp - 1) % 2][:], Mt[kp - 1][:],
                            start=True, stop=True))
                        me = ph.op("scalar", lambda e, kp=kp, bp2=bp2: e.copy(
                            Mt[kp][:], bp2), after=(tm2,), token=True)
                        psg[kb2] = me
                        pe[kp] = (ue, me)
                        yield
                    kb, bp = nbank("2")
                    tx = MM(kb, lambda e, bp=bp: e.matmul(
                        bp, aAb, aSSb, start=True, stop=False),
                        after=(ssb_t, tge))
                    ph.op("tensor", lambda e, bp=bp: e.matmul(
                        bp, Gt[:], aVb, start=False, stop=True),
                        after=(t_vb,))
                    tnx = ph.tok_last("tensor")
                    txe = ph.op("scalar", lambda e, bp=bp: e.copy(Xb[:], bp),
                                after=(tnx,), token=True)
                    psg[kb] = txe
                    yield
                    for kp in range(7):
                        kb, bp = nbank("2")
                        MM(kb, lambda e, kp=kp, bp=bp: e.matmul(
                            bp, Mt[kp][:], Xb[:], start=True,
                            stop=False), after=(txe, pe[kp][1]), token=False)
                        ph.op("tensor", lambda e, bp=bp: e.matmul(
                            bp, g["idb"][:], Xb[:], start=False,
                            stop=True))
                        tnp = ph.tok_last("tensor")
                        txe = ph.op("scalar", lambda e, bp=bp: e.copy(
                            Xb[:], bp), after=(tnp,), token=True)
                        psg[kb] = txe
                        yield
                    kb, bp = nbank("1")
                    tw = MM(kb, lambda e, bp=bp: e.matmul(
                        bp, aKc, aRb, start=True, stop=True),
                        after=(t_rb, pe[6][0]))
                    twe = ph.op("vector", lambda e, bp=bp: e.tensor_tensor(
                        Wt[:], bp, g["mask_iu"][:], op=ALU.mult),
                        after=(tw,), token=True)
                    psg[kb] = twe
                    kb2, bp2 = nbank("1")
                    tw2 = MM(kb2, lambda e, bp2=bp2: e.matmul(
                        bp2, aBc, aRb, start=True, stop=True),
                        after=(pe[6][1],))
                    tw2e = ph.op("vector", lambda e, bp2=bp2: e.tensor_tensor(
                        W2t[:], bp2, g["mask_iu"][:], op=ALU.mult),
                        after=(tw2,), token=True)
                    psg[kb2] = tw2e
                    yield
                    kb, bp = nbank("2")
                    MM(kb, lambda e, bp=bp: e.matmul(
                        bp, aRb, aSSb, start=True, stop=False),
                        after=(txe, twe, tw2e), token=False)
                    ph.op("tensor", lambda e, bp=bp: e.matmul(
                        bp, Wt[:], aVb, start=False, stop=False))
                    ph.op("tensor", lambda e, bp=bp: e.matmul(
                        bp, W2t[:], Xb[:], start=False, stop=True))
                    tny = ph.tok_last("tensor")
                    tye = ph.op("scalar", lambda e, co=co, h=h, bp=bp: e.copy(
                        y_sc[:, co * D + h * 64:co * D + (h + 1) * 64],
                        bp), after=(tny,) + CG, token=True)
                    psg[kb] = tye
                    yield
                    kb, bp = nbank("3")
                    MM(kb, lambda e, bp=bp: e.matmul(
                        bp, aKt, aVb, start=True,
                        stop=False), after=(tye,), token=False)
                    ph.op("tensor", lambda e, bp=bp: e.matmul(
                        bp, aBt, Xb[:], start=False,
                        stop=True))
                    tns = ph.tok_last("tensor")
                    tse = ph.op("vector", lambda e, bp=bp: e.tensor_tensor(
                        aSS, bp, aSS, op=ALU.add),
                        after=(tns,), token=True)
                    tse = ph.op("vector", lambda e: e.tensor_scalar(
                        aSS, aSS, agl, None, op0=ALU.mult), token=True)
                    psg[kb] = tse
                    ssdone.append(tse)

                for h0 in range(0, H, NSL):
                    alive = [head_gen(h) for h in range(h0, h0 + NSL)]
                    while alive:
                        nxt = []
                        for gen_ in alive:
                            try:
                                next(gen_)
                                nxt.append(gen_)
                            except StopIteration:
                                pass
                        alive = nxt
                SSD = tuple(ssdone)
                ssb_prev = SSD[-1]

                gtk = []
                for hh in range(H):
                    ys = y_sc[:, co * D + hh * 64:co * D + (hh + 1) * 64]
                    t1_ = ph.op("vector", lambda e, ys=ys: e.bn_stats(
                        mvg[:, 0:6], ys), after=SSD, token=True)
                    t1_ = ph.op("vector", lambda e: e.bn_aggr(
                        mvg[:, 6:8], mvg[:, 0:6]), after=(t1_,), force=True,
                        token=True)
                    t2_ = ph.op("scalar", lambda e: e.activation(
                        rsg[:], mvg[:, 7:8], AF.Ln, bias=GN_EPS),
                        after=(t1_,))
                    t2_ = ph.op("scalar", lambda e: e.activation(
                        rsg[:], rsg[:], AF.Exp, scale=-0.5), token=True)
                    gtk.append(ph.op("vector", lambda e, ys=ys:
                                     e.tensor_scalar(
                        ys, ys, mvg[:, 6:7], rsg[:], op0=ALU.subtract,
                        op1=ALU.mult), after=(t2_, t1_), force=True,
                        token=True))
                ysl = y_sc[:, co * D:(co + 1) * D]
                tgn = ph.op("vector", lambda e, ysl=ysl: e.tensor_tensor(
                    ysl, ysl, lnw[:], op=ALU.mult), after=tuple(gtk))
                tgn = ph.op("vector", lambda e, ysl=ysl: e.tensor_tensor(
                    ysl, ysl, lnb[:], op=ALU.add), token=True)
                t1_ = ph.op("vector", lambda e: e.tensor_tensor(
                    dview(E1), dview(rT), dview(ks), op=ALU.mult),
                    after=(pd[(0, 1)],) + pd[(2, "ks")] + tuple(btbt) +
                    (t_ab,), token=True)
                rft = []
                for dc in range(ND):
                    rft.append(ph.op("vector", lambda e, dc=dc:
                                     e.tensor_scalar(
                        Ab[:, dc * 128:(dc + 1) * 128],
                        E1[:, dc * 128:(dc + 1) * 128],
                        vecs["faaaa"][:, dc:dc + 1], None, op0=ALU.mult),
                        after=(t1_, SSD[-1]), token=True))
                for dc in range(ND):
                    if dc == 0:
                        MM("X", lambda e: e.matmul(
                            pX[:, 0:16], Ab[:, 0:128], hb[:, 0:H],
                            start=True, stop=False), after=tuple(rft),
                            token=False)
                    else:
                        ph.op("tensor", lambda e, dc=dc: e.matmul(
                            pX[:, 0:16], Ab[:, dc * 128:(dc + 1) * 128],
                            hb[:, dc * H:(dc + 1) * H], start=False,
                            stop=(dc == ND - 1)))
                tnb = ph.tok_last("tensor")
                tbs = ph.op("vector", lambda e: e.tensor_copy(
                    bscal[:], pX[:, 0:16]), after=(tnb,), token=True)
                psg["X"] = tbs
                ytk = []
                for hh in range(H):
                    ytk.append(ph.op("vector", lambda e, hh=hh, co=co:
                                     e.scalar_tensor_tensor(
                        y_sc[:, co * D + hh * 64:co * D + (hh + 1) * 64],
                        vtok[:, co * D + hh * 64:co * D + (hh + 1) * 64],
                        bscal[:, hh:hh + 1],
                        y_sc[:, co * D + hh * 64:co * D + (hh + 1) * 64],
                        op0=ALU.mult, op1=ALU.add), after=(tbs, tgn),
                        token=True))
                t_tapy = None
                if tap_y is not None:
                    t_tapy = ph.dma("gpsimd", tap_y[ci * 128:(ci + 1) * 128, :],
                                    ysl, after=tuple(ytk), stream="tap")
                ygt = []
                trg3 = [None, None]
                for dc in range(ND):
                    b = dc % 2
                    bank = f"B{b}"
                    ttr = MM(bank, lambda e, b=b, dc=dc, co=co: e.transpose(
                        psB[b][:],
                        y_sc[:, co * D + dc * 128:co * D + (dc + 1) * 128],
                        g["idf"][:]), after=tuple(ytk))
                    tcp = ph.op("vector", lambda e, b=b, dc=dc, co=co:
                                e.tensor_tensor(
                        ygb[:, dc * 128:(dc + 1) * 128], psB[b][:],
                        chs(gT, dc)[:, co * 128:(co + 1) * 128],
                        op=ALU.mult),
                        after=(ttr, pd[(5, 0)], pd[(5, 1)]), token=True)
                    psg[bank] = tcp
                    ygt.append(tcp)
                YGT = tuple(ygt)
                attk = []
                for fo in range(2):
                    b = fo % 2
                    bank = "A" + str(b)
                    for dc in range(ND):
                        if dc == 0:
                            MM(bank, lambda e, fo=fo, b=b: e.matmul(
                                psA[b][:], ygb[:, 0:128],
                                wo[:, fo * 512:(fo + 1) * 512],
                                start=True, stop=False),
                                after=YGT + (prev_sc,), token=False)
                        else:
                            ph.op("tensor", lambda e, b=b, dc=dc, fo=fo:
                                  e.matmul(
                                psA[b][:], ygb[:, dc * 128:(dc + 1) * 128],
                                wo[:, dc * D + fo * 512:
                                   dc * D + (fo + 1) * 512],
                                start=False, stop=(dc == ND - 1)))
                    tnm = ph.tok_last("tensor")
                    tcp = ph.op("vector", lambda e, b=b, fo=fo, co=co:
                                e.tensor_copy(
                        y_sc[:, co * D + fo * 512:co * D + (fo + 1) * 512],
                        psA[b][:]), after=(tnm, prev_sc, t_tapy), token=True)
                    psg[bank] = tcp
                    attk.append(tcp)
                std = ph.dma("gpsimd",
                             dram["att1"][ci * 128:(ci + 1) * 128, :],
                             ysl, after=tuple(attk), stream="at")
                if tap_att1 is not None:
                    std = ph.dma("gpsimd", tap_att1[ci * 128:(ci + 1) * 128, :],
                                 ysl, after=tuple(attk), stream="at")
                prev_sc = std
                CG = (SSD[-1], std, tbs, tgn, attk[-1])
        ph.emit()


def kernel(**inputs):
    shared, percore = prep_inputs(inputs)
    cn = make_consts()
    base = {("c_" + k): np.asarray(v) for k, v in cn.items()}
    base.update(shared)
    nc = build_program()
    in_maps = []
    for b in range(8):
        m = dict(base)
        m["x"] = percore[b]
        in_maps.append(m)
    res = run_bass_kernel_spmd(nc, in_maps, core_ids=list(range(8)))
    return np.stack([r["out"] for r in res.results], axis=0)


# version 04
# speedup vs baseline: 1.1168x; 1.0286x over previous
import contextlib
import numpy as np
import ml_dtypes
import concourse.bass as bass
import concourse.mybir as mybir
from concourse.bass_utils import run_bass_kernel_spmd

F32 = mybir.dt.float32
BF16 = mybir.dt.bfloat16
AF = mybir.ActivationFunctionType
ALU = mybir.AluOpType
AX = mybir.AxisListType

T = 2048
D = 1024
NT = T // 128
ND = D // 128
H = 16
FF = 4096
NE = 8
LCH = 128
SCN = 128
NSC = T // SCN
NSL = 4
CPS = SCN // LCH
GN_EPS = 1e-5 * 64.0

ENGINES = ["tensor", "vector", "scalar", "gpsimd", "sync"]

DEBUG_TAPS = ()
UPTO = 99
SIM_SAFE = False
CHAIN_ENGINES = ("vector", "scalar", "gpsimd")
QKV_SUB = 99
QKV_NW = 2
QKV_NDO = 8


def _bf(x):
    return np.ascontiguousarray(np.asarray(x).astype(ml_dtypes.bfloat16))



class Sem:
    def __init__(self, handle, name):
        self.h = handle
        self.name = name
        self.n = 0

    def inc(self, k=1):
        self.n += k
        return self.n


class Ph:
    def __init__(self, p, name):
        self.p = p
        self.name = name
        self.items = {e: [] for e in ENGINES}
        self._laststream = {}

    def _sem(self, eng, dma):
        return self.p.sem(("d_" if dma else "e_") + eng)

    def op(self, eng, fn, after=(), token=False, force=False):
        sem = self._sem(eng, False)
        tok = None
        if token or SIM_SAFE or eng in CHAIN_ENGINES:
            tok = (sem, sem.n + 1, eng, False)
            sem.inc(1)
        self.items[eng].append(
            (fn, tuple(a for a in after if a), tok, False, force))
        return tok

    def tok_last(self, eng):
        items = self.items[eng]
        fn, after, tok0, isdma, force = items[-1]
        if tok0 is not None:
            return tok0
        sem = self._sem(eng, False)
        tok = (sem, sem.n + 1, eng, False)
        sem.inc(1)
        items[-1] = (fn, after, tok, isdma, force)
        return tok

    def dma(self, eng, out, in_, after=(), transpose=False, stream="m"):
        sem = self.p.sem("d_" + eng + "_" + stream)
        if SIM_SAFE:
            after = tuple(after) + (self._laststream.get(sem.name),)
        tok = (sem, sem.n + 16, eng, True)
        sem.inc(16)
        self._laststream[sem.name] = tok

        def fn(e, out=out, in_=in_, tr=transpose):
            if tr:
                return e.dma_start(out, in_, transpose=True)
            return e.dma_start(out, in_)

        self.items[eng].append(
            (fn, tuple(a for a in after if a), tok, True, False))
        return tok

    def emit(self):
        nc = self.p.nc
        active = [e for e in ENGINES if self.items[e]]
        with nc.Block(self.name) as blk:
            for eng in active:
                items = self.items[eng]

                def body(e, eng=eng, items=items):
                    water = {}
                    last_dma = {}
                    prev_tok = None
                    chain = SIM_SAFE or eng in CHAIN_ENGINES
                    for fn, after, tok, isdma, force in items:
                        if chain and prev_tok is not None:
                            e.wait_ge(prev_tok[0].h, prev_tok[1])
                        need = {}
                        for (s, v, peng, pdma) in after:
                            if peng == eng and not pdma and not force:
                                continue
                            if water.get(s.name, -1) >= v:
                                continue
                            if need.get(s.name, (None, -1))[1] < v:
                                need[s.name] = (s, v)
                        for s, v in need.values():
                            e.wait_ge(s.h, v)
                            water[s.name] = v
                        inst = fn(e)
                        if tok is not None:
                            inst.then_inc(tok[0].h, 16 if isdma else 1)
                            if isdma:
                                last_dma[tok[0].name] = tok
                            else:
                                prev_tok = tok
                    for s, v, _, _ in last_dma.values():
                        if water.get(s.name, -1) < v:
                            e.wait_ge(s.h, v)
                            water[s.name] = v

                getattr(blk, eng)(body)


class P:
    def __init__(self, nc, stack):
        self.nc = nc
        self.stack = stack
        self.sems = {}
        self.dram = {}
        self.taps = {}

    def sem(self, name):
        if name not in self.sems:
            self.sems[name] = Sem(
                self.stack.enter_context(self.nc.semaphore(name)), name)
        return self.sems[name]

    def sb(self, name, shape, dtype, stack=None):
        return (stack or self.stack).enter_context(
            self.nc.sbuf_tensor(name, shape, dtype))

    def ps(self, name, shape, stack=None, dtype=F32):
        return (stack or self.stack).enter_context(
            self.nc.psum_tensor(name, shape, dtype))

    def tap(self, name, shape, dtype=F32):
        if name in DEBUG_TAPS and name not in self.taps:
            self.taps[name] = self.nc.dram_tensor(
                "tap_" + name, list(shape), dtype, kind="ExternalOutput").ap()
        return self.taps.get(name)



def make_consts():
    c = {}
    c["ident_f32"] = np.eye(128, dtype=np.float32)
    c["ident_bf16"] = _bf(np.eye(128))
    c["ones_1x128"] = np.ones((1, 128), dtype=np.float32)
    c["onescol_bf"] = _bf(np.ones((128, 1)))
    for off in range(4):
        pk = off * 128 + np.arange(128)[:, None]
        fq = np.arange(512)[None, :]
        c[f"cmask{off}"] = _bf((pk <= fq).astype(np.float32))
    tt = np.arange(LCH)
    c["tri_incl"] = (tt[:, None] <= tt[None, :]).astype(np.float32)
    c["mask_su"] = (tt[:, None] < tt[None, :]).astype(np.float32)
    c["mask_iu"] = (tt[:, None] <= tt[None, :]).astype(np.float32)
    c["mask_sl"] = (tt[:, None] > tt[None, :]).astype(np.float32)
    blk = np.zeros((ND, 128, H), dtype=np.float32)
    for dc in range(ND):
        for j in range(2):
            blk[dc, j * 64:(j + 1) * 64, 2 * dc + j] = 1.0
    c["headblk_bf"] = _bf(blk.transpose(1, 0, 2).reshape(128, ND * H))
    c["fvals"] = np.tile(np.array(FCONSTS, dtype=np.float32), (128, 1))
    return c


FCONSTS = [1e-5, GN_EPS, 1.0, -1.0, 0.5, -0.5, 1e12, -1e30, float(D), 2.0,
           -2.0, 1.0 / 32.0, -1.0 / 32.0, 0.0, 3.0, -3e38]


def prep_inputs(inputs):
    g = {}
    x = np.asarray(inputs["L0_attn_x"], dtype=np.float32)
    percore = [np.ascontiguousarray(x[b]) for b in range(8)]

    def A(name, arr):
        g[name] = np.ascontiguousarray(np.asarray(arr, dtype=np.float32))

    for w in ["Wq", "Wk", "Wv"]:
        W = np.asarray(inputs[f"L0_ffn_{w}"], dtype=np.float32)
        A(f"l0_{w.lower()}", W.transpose(1, 0, 2).reshape(D, D))
    A("l0_gate", np.asarray(inputs["L0_ffn_gate_W"], dtype=np.float32)
      .reshape(ND, 128, NE).transpose(1, 0, 2).reshape(128, ND * NE))
    A("l0_w1", inputs["L0_ffn_W1"])
    A("l0_w2", inputs["L0_ffn_W2"])
    b1 = np.asarray(inputs["L0_ffn_b1"], dtype=np.float32)
    A("l0_b1", b1.reshape(NE, 32, 128).transpose(2, 0, 1).reshape(128, NE * 32))
    A("l0_b2", inputs["L0_ffn_b2"])
    for v in ["ln1_g", "ln1_b", "ln2_g", "ln2_b"]:
        A(f"l0_{v}", np.asarray(inputs[f"L0_ffn_{v}"]).reshape(1, D))

    def chvec(name, arr):
        v = np.asarray(arr, dtype=np.float32).reshape(D)
        g[name] = np.ascontiguousarray(v.reshape(ND, 128).T)

    for v in ["time_maa_x", "time_maa_r", "time_maa_w", "time_maa_k",
              "time_maa_v", "time_maa_a", "time_maa_g", "time_decay",
              "time_aaaaa"]:
        chvec("l1_" + v, inputs["L1_attn_" + v])
    chvec("l1_faaaa", np.asarray(inputs["L1_attn_time_faaaa"]).reshape(D))
    A("l1_maa_w1", inputs["L1_attn_time_maa_w1"])
    A("l1_maa_w2", np.asarray(inputs["L1_attn_time_maa_w2"], dtype=np.float32)
      .reshape(3, 2, 32, D).transpose(1, 2, 0, 3).reshape(64, 3 * D))
    A("l1_decay_w1", inputs["L1_attn_time_decay_w1"])
    A("l1_decay_w2", inputs["L1_attn_time_decay_w2"])
    A("l1_aaa_w1", inputs["L1_attn_time_aaa_w1"])
    A("l1_aaa_w2", inputs["L1_attn_time_aaa_w2"])
    A("l1_kkk_w1", inputs["L1_attn_time_kkk_w1"])
    A("l1_kkk_w2", inputs["L1_attn_time_kkk_w2"])
    A("l1_gate_w1", inputs["L1_attn_gate_w1"])
    A("l1_gate_w2", inputs["L1_attn_gate_w2"])
    for w in ["W_r", "W_k", "W_v", "W_o"]:
        A("l1_" + w.lower(), inputs["L1_attn_" + w])
    A("l1_ln_w", np.asarray(inputs["L1_attn_ln_w"]).reshape(1, D))
    A("l1_ln_b", np.asarray(inputs["L1_attn_ln_b"]).reshape(1, D))
    A("l1_gate", np.asarray(inputs["L1_ffn_w_router"], dtype=np.float32)
      .reshape(ND, 128, NE).transpose(1, 0, 2).reshape(128, ND * NE))
    A("l1_w1", inputs["L1_ffn_w1"])
    A("l1_w2", inputs["L1_ffn_w2"])
    b1 = np.asarray(inputs["L1_ffn_b1"], dtype=np.float32)
    A("l1_b1", b1.reshape(NE, 32, 128).transpose(2, 0, 1).reshape(128, NE * 32))
    A("l1_b2", inputs["L1_ffn_b2"])
    return g, percore



def build_program(upto=None):
    if upto is None:
        upto = UPTO
    nc = bass.Bass("TRN2", target_bir_lowering=False)
    stack = contextlib.ExitStack()
    p = P(nc, stack)
    dram = p.dram

    def din(name, shape, dtype=F32):
        dram[name] = nc.dram_tensor(name, shape, dtype, kind="ExternalInput").ap()

    din("x", [T, D])
    for w in ["l0_wq", "l0_wk", "l0_wv"]:
        din(w, [D, D])
    din("l0_gate", [128, ND * NE])
    din("l0_w1", [NE, D, FF])
    din("l0_w2", [NE, FF, D])
    din("l0_b1", [128, NE * 32])
    din("l0_b2", [NE, D])
    for v in ["ln1_g", "ln1_b", "ln2_g", "ln2_b"]:
        din("l0_" + v, [1, D])
    for v in ["time_maa_x", "time_maa_r", "time_maa_w", "time_maa_k",
              "time_maa_v", "time_maa_a", "time_maa_g", "time_decay",
              "time_aaaaa", "faaaa"]:
        din("l1_" + v, [128, ND])
    din("l1_maa_w1", [D, 192])
    din("l1_maa_w2", [64, 3 * D])
    din("l1_decay_w1", [D, 64])
    din("l1_decay_w2", [64, D])
    din("l1_aaa_w1", [D, 64])
    din("l1_aaa_w2", [64, D])
    din("l1_kkk_w1", [D, 64])
    din("l1_kkk_w2", [64, D])
    din("l1_gate_w1", [D, 128])
    din("l1_gate_w2", [128, D])
    for w in ["l1_w_r", "l1_w_k", "l1_w_v", "l1_w_o"]:
        din(w, [D, D])
    din("l1_ln_w", [1, D])
    din("l1_ln_b", [1, D])
    din("l1_gate", [128, ND * NE])
    din("l1_w1", [NE, D, FF])
    din("l1_w2", [NE, FF, D])
    din("l1_b1", [128, NE * 32])
    din("l1_b2", [NE, D])
    cn = make_consts()
    for k, v in cn.items():
        din("c_" + k, list(v.shape), BF16 if v.dtype == ml_dtypes.bfloat16 else F32)

    dram["out"] = nc.dram_tensor("out", [T, D], F32, kind="ExternalOutput").ap()

    def dscr(name, shape, dtype=F32):
        dram[name] = nc.dram_tensor(name, shape, dtype).ap()

    dscr("xn0_f", [T, D])
    dscr("xn0_b", [T, D], BF16)
    dscr("y0_f", [T, D])
    dscr("y0_b", [T, D], BF16)
    dscr("base0", [T, D])
    dscr("x1_f", [T, D])
    dscr("xn1_b", [T, D], BF16)
    dscr("xn1_ch", [D, T + 4])
    dscr("att1", [T, D])

    g = {}
    g["stats"] = p.sb("stats", [128, 8], F32)
    g["idf"] = p.sb("idf", [128, 128], F32)
    g["idb"] = p.sb("idb", [128, 128], BF16)
    g["ones1"] = p.sb("ones1", [1, 128], F32)
    g["onescol_bf"] = p.sb("onescol_bf", [128, 1], BF16)
    g["tri_incl"] = p.sb("tri_incl", [128, 128], F32)
    g["mask_su"] = p.sb("mask_su", [128, 128], F32)
    g["mask_iu"] = p.sb("mask_iu", [128, 128], F32)
    g["mask_sl"] = p.sb("mask_sl", [128, 128], F32)
    g["fvals"] = p.sb("fvals", [128, len(FCONSTS)], F32)
    for i, v in enumerate(FCONSTS):
        nc.const_aps.register(F32, v, g["fvals"][:, i:i + 1])

    ph = Ph(p, "consts")
    for nm, dst in [("c_ident_f32", g["idf"]), ("c_ident_bf16", g["idb"]),
                    ("c_ones_1x128", g["ones1"]),
                    ("c_onescol_bf", g["onescol_bf"]),
                    ("c_tri_incl", g["tri_incl"]), ("c_mask_su", g["mask_su"]),
                    ("c_mask_iu", g["mask_iu"]), ("c_mask_sl", g["mask_sl"]),
                    ("c_fvals", g["fvals"])]:
        ph.dma("sync", dst[:], dram[nm])
    ph.emit()

    if not globals().get('SKIP_LN0', False):
        phase_ln0(p, g)
    with contextlib.ExitStack() as qkv_stack:
        if upto >= 2:
            phase_qkv(p, g, qkv_stack)
        if upto >= 3:
            phase_mha(p, g)
    if upto >= 4:
        phase_attn0(p, g)
    if upto >= 5:
        phase_moe(p, g, layer=0)
    if upto >= 6:
        phase_rwkv(p, g)
    if upto >= 7:
        phase_moe(p, g, layer=1)

    if upto < 7:
        srcs = {1: "xn0_f", 2: "xn0_f", 3: "y0_f", 4: "base0", 5: "x1_f",
                6: "att1"}
        _copy_out(p, srcs.get(upto, "x1_f"))

    stack.close()
    return nc


def _copy_out(p, src):
    ph = Ph(p, "fin")
    with contextlib.ExitStack() as s_:
        buf = [p.sb(f"fin{i}", [128, D], F32, s_) for i in range(2)]
        toks = [None, None]
        for i in range(NT):
            t = ph.dma("sync", buf[i % 2][:],
                       p.dram[src][i * 128:(i + 1) * 128, :],
                       after=(toks[i % 2],), stream=f"f{i % 2}")
            toks[i % 2] = ph.dma("sync", p.dram["out"][i * 128:(i + 1) * 128, :],
                                 buf[i % 2][:], after=(t,), stream=f"f{i % 2}")
        ph.emit()



def bcast_row(ph, p, g, src_1xD, dst_128xD, ps2, after=(), guard=(None, None)):
    toks = []
    for fo in range(D // 512):
        tk = ph.op("tensor", lambda e, fo=fo: e.matmul(
            ps2[fo % 2][:, 0:512], g["ones1"][:],
            src_1xD[0:1, fo * 512:(fo + 1) * 512],
            start=True, stop=True), after=tuple(after) + (guard[fo % 2],),
            token=True)
        toks.append(ph.op("vector", lambda e, fo=fo: e.tensor_copy(
            dst_128xD[:, fo * 512:(fo + 1) * 512], ps2[fo % 2][:, 0:512]),
            after=(tk,), token=True))
    return toks


def ln_stats(ph, mv, x_ap, rstd, eps, extra_after=()):
    tb1 = ph.op("vector", lambda e: e.bn_stats(mv[:, 0:6], x_ap[:, 0:512]),
                after=extra_after, token=True)
    tb2 = ph.op("vector", lambda e: e.bn_stats(mv[:, 6:12], x_ap[:, 512:1024]),
                token=True)
    ts_ = ph.op("vector", lambda e: e.bn_aggr(mv[:, 12:14], mv[:, 0:12]),
                after=(tb1, tb2), force=True, token=True)
    ph.op("scalar", lambda e: e.activation(rstd[:], mv[:, 13:14], AF.Ln,
                                           bias=eps), after=(ts_,))
    tr = ph.op("scalar", lambda e: e.activation(rstd[:], rstd[:], AF.Exp,
                                                scale=-0.5), token=True)
    return ts_, tr



def phase_ln0(p, g):
    dram = p.dram
    ph = Ph(p, "ln0")
    with contextlib.ExitStack() as s_:
        xa = [p.sb(f"a_x{i}", [128, D], F32, s_) for i in range(2)]
        xn = [p.sb(f"a_xn{i}", [128, D], F32, s_) for i in range(2)]
        xnb = [p.sb(f"a_xnb{i}", [128, D], BF16, s_) for i in range(2)]
        mv = [p.sb(f"a_mv{i}", [128, 16], F32, s_) for i in range(2)]
        rs = [p.sb(f"a_rs{i}", [128, 1], F32, s_) for i in range(2)]
        n2st = p.sb("a_n2", [128, NT], F32, s_)
        gpad = p.sb("a_gpad", [128, 128], F32, s_)
        cmax = p.sb("a_cmax", [1, 1], F32, s_)
        pt = p.ps("a_pt", [128, 128], s_)
        pb = p.ps("a_pb", [128, 1], s_)

        tap_xn = p.tap("xn0", [T, D])
        ph.op("vector", lambda e: e.memset(gpad[:], -3e38))
        done = [None, None]
        vtok = [None, None]
        for i in range(NT):
            b = i % 2
            tl = ph.dma("sync", xa[b][:], dram["x"][i * 128:(i + 1) * 128, :],
                        after=(vtok[b],), stream=f"x{b}")
            tsa, tr = ln_stats(ph, mv[b], xa[b], rs[b], 1e-5,
                               extra_after=(tl, done[b]))
            ph.op("vector", lambda e, b=b: e.tensor_scalar(
                xn[b][:], xa[b][:], mv[b][:, 12:13], rs[b][:],
                op0=ALU.subtract, op1=ALU.mult), after=(tr, tsa), force=True)
            ph.op("vector", lambda e, b=b: e.tensor_copy(xnb[b][:], xn[b][:]))
            ph.op("vector", lambda e, b=b: e.tensor_tensor(
                mv[b][:, 14:15], rs[b][:], rs[b][:], op=ALU.mult))
            ph.op("vector", lambda e, b=b: e.tensor_tensor(
                mv[b][:, 15:16], mv[b][:, 13:14], mv[b][:, 14:15], op=ALU.mult))
            vtok[b] = ph.op("vector", lambda e, b=b, i=i: e.tensor_scalar(
                n2st[:, i:i + 1], mv[b][:, 15:16], float(D), None,
                op0=ALU.mult), token=True)
            ph.dma("gpsimd", dram["xn0_f"][i * 128:(i + 1) * 128, :],
                   xn[b][:], after=(vtok[b],), stream=f"s{b}")
            done[b] = ph.dma("gpsimd", dram["xn0_b"][i * 128:(i + 1) * 128, :],
                             xnb[b][:], after=(vtok[b],), stream=f"s{b}")
            if tap_xn is not None:
                done[b] = ph.dma("gpsimd", tap_xn[i * 128:(i + 1) * 128, :],
                                 xn[b][:], after=(vtok[b],), stream=f"s{b}")
        tm1 = ph.op("vector", lambda e: e.tensor_reduce(
            gpad[:, 0:1], n2st[:], axis=AX.X, op=ALU.max),
            after=tuple(vtok), token=True)
        tt = ph.op("tensor", lambda e: e.transpose(pt[:], gpad[:], g["idf"][:]),
                   after=(tm1,), token=True)
        tv = ph.op("vector", lambda e: e.tensor_reduce(
            cmax[:], pt[0:1, :], axis=AX.X, op=ALU.max), after=(tt,),
            token=True)
        tb = ph.op("tensor", lambda e: e.matmul(
            pb[:], g["ones1"][:], cmax[:], start=True, stop=True),
            after=(tv,), token=True)
        ph.op("scalar", lambda e: e.mul(g["stats"][:, 0:1], pb[:], -1.0),
              after=(tb,))
        ph.emit()



def phase_qkv(p, g, qkv_stack):
    dram = p.dram
    g["qT"] = p.sb("qT", [128, ND * T], BF16, qkv_stack)
    g["kT"] = p.sb("kT", [128, ND * T], BF16, qkv_stack)
    g["vtok"] = p.sb("vtok", [128, NT * D], BF16, qkv_stack)

    ph = Ph(p, "qkv")
    with contextlib.ExitStack() as s_:
        xnch = p.sb("b_xnch", [128, ND * T], BF16, s_)
        wbf = [p.sb(f"b_w{i}", [128, ND * D], BF16, s_) for i in range(2)]
        hb = p.sb("b_hb", [128, ND * H], BF16, s_)
        qsq = p.sb("b_qsq", [128, T], BF16, s_)
        nrm = p.sb("b_nrm", [128, 2 * NT * H], F32, s_)
        gpad = p.sb("b_gpad", [128, 128], F32, s_)
        cm = p.sb("b_cm", [1, 4], F32, s_)
        ps = [p.ps(f"b_ps{i}", [128, 512], s_) for i in range(2)]
        pn = [p.ps(f"b_pn{i}", [128, H], s_) for i in range(2)]
        pt = p.ps("b_pt", [128, 128], s_)
        pb = p.ps("b_pb", [128, 1], s_)

        xt = []
        for dc in range(ND):
            xt.append(ph.dma("sync", xnch[:, dc * T:(dc + 1) * T],
                             dram["xn0_b"][:, dc * 128:(dc + 1) * 128],
                             transpose=True, stream="xnch"))
        hbt = ph.dma("sync", hb[:], dram["c_headblk_bf"], stream="xnch")

        if QKV_SUB < 1:
            ph.emit()
            return
        wtok = {}
        consumed = [None, None]
        for wi, wname in enumerate(["l0_wq", "l0_wk"]):
            for dc in range(ND):
                wtok[(wi, dc)] = ph.dma(
                    "gpsimd", wbf[wi % 2][:, dc * D:(dc + 1) * D],
                    dram[wname][dc * 128:(dc + 1) * 128, :],
                    stream=f"w{wi % 2}")
        if QKV_SUB < 2:
            ph.emit()
            return

        evtok = [None, None]
        for wi in range(QKV_NW):
            dst = g["qT"] if wi == 0 else g["kT"]
            lastmm = None
            for do in range(QKV_NDO):
                for tq in range(4):
                    k = do * 4 + tq
                    pp = ps[k % 2]
                    aft = tuple(wtok[(wi, dc)] for dc in range(ND)) + \
                        (evtok[k % 2],) + tuple(xt)
                    for dc in range(ND):
                        ph.op("tensor", lambda e, pp=pp, wi=wi, dc=dc, do=do,
                              tq=tq: e.matmul(
                            pp[:],
                            wbf[wi % 2][:, dc * D + do * 128:
                                        dc * D + (do + 1) * 128],
                            xnch[:, dc * T + tq * 512:dc * T + (tq + 1) * 512],
                            start=(dc == 0), stop=(dc == ND - 1)),
                            after=aft if dc == 0 else ())
                    lastmm = ph.tok_last("tensor")
                    evtok[k % 2] = ph.op(
                        "scalar", lambda e, dst=dst, do=do, tq=tq, pp=pp:
                        e.copy(dst[:, do * T + tq * 512:do * T + (tq + 1) * 512],
                               pp[:]), after=(lastmm,), token=True)
            consumed[wi % 2] = lastmm
        if QKV_SUB < 3:
            ph.emit()
            return
        for dc in range(ND):
            wtok[(2, dc)] = ph.dma(
                "gpsimd", wbf[0][:, dc * D:(dc + 1) * D],
                dram["l0_wv"][dc * 128:(dc + 1) * 128, :],
                after=(consumed[0],), stream="w0")
        for ti in range(NT):
            for fo in range(2):
                k = ti * 2 + fo
                pp = ps[k % 2]
                aft = tuple(wtok[(2, dc)] for dc in range(ND)) + (evtok[k % 2],)
                for dc in range(ND):
                    ph.op("tensor", lambda e, pp=pp, dc=dc, ti=ti, fo=fo:
                          e.matmul(
                        pp[:],
                        xnch[:, dc * T + ti * 128:dc * T + (ti + 1) * 128],
                        wbf[0][:, dc * D + fo * 512:dc * D + (fo + 1) * 512],
                        start=(dc == 0), stop=(dc == ND - 1)),
                        after=aft if dc == 0 else ())
                lastmm = ph.tok_last("tensor")
                evtok[k % 2] = ph.op(
                    "scalar", lambda e, ti=ti, fo=fo, pp=pp: e.copy(
                        g["vtok"][:, ti * D + fo * 512:ti * D + (fo + 1) * 512],
                        pp[:]), after=(lastmm,), token=True)

        if QKV_SUB < 4:
            ph.emit()
            return
        mz = ph.op("vector", lambda e: e.memset(nrm[:], 0.0), token=True)
        ph.op("vector", lambda e: e.memset(gpad[:], -3e38))
        pndone = [mz, mz]
        sqdone = None
        addtoks = []
        for wi in range(2):
            src = g["qT"] if wi == 0 else g["kT"]
            for dc in range(ND):
                sq = ph.op("scalar", lambda e, src=src, dc=dc: e.activation(
                    qsq[:], src[:, dc * T:(dc + 1) * T], AF.Square),
                    after=(evtok[0], evtok[1], sqdone), token=True)
                for ti in range(NT):
                    b = ti % 2
                    tmm = ph.op("tensor", lambda e, b=b, ti=ti, dc=dc:
                                e.matmul(
                        pn[b][:], qsq[:, ti * 128:(ti + 1) * 128],
                        hb[:, dc * H:(dc + 1) * H], start=True, stop=True),
                        after=(sq, pndone[b], hbt), token=True)
                    col = (wi * NT + ti) * H
                    pndone[b] = ph.op(
                        "vector", lambda e, col=col, b=b: e.tensor_tensor(
                            nrm[:, col:col + H], nrm[:, col:col + H],
                            pn[b][:], op=ALU.add), after=(tmm, mz), token=True)
                    addtoks.append(pndone[b])
                sqdone = tmm
        t1 = ph.op("vector", lambda e: e.tensor_reduce(
            gpad[:, 0:1], nrm[:, 0:NT * H], axis=AX.X, op=ALU.max),
            after=tuple(addtoks[-NT:]), token=True)
        t2 = ph.op("vector", lambda e: e.tensor_reduce(
            gpad[:, 64:65], nrm[:, NT * H:], axis=AX.X, op=ALU.max),
            token=True)
        tt = ph.op("tensor", lambda e: e.transpose(pt[:], gpad[:], g["idf"][:]),
                   after=(t1, t2), token=True)
        tr1 = ph.op("vector", lambda e: e.tensor_reduce(
            cm[:, 0:1], pt[0:1, :], axis=AX.X, op=ALU.max), after=(tt,),
            token=True)
        tr2 = ph.op("vector", lambda e: e.tensor_reduce(
            cm[:, 1:2], pt[64:65, :], axis=AX.X, op=ALU.max), token=True)
        tv = ph.op("vector", lambda e: e.tensor_tensor(
            cm[:, 2:3], cm[:, 0:1], cm[:, 1:2], op=ALU.mult),
            after=(tr1, tr2), force=True, token=True)
        tsq = ph.op("scalar", lambda e: e.activation(cm[:, 3:4], cm[:, 2:3],
                                                     AF.Sqrt),
                    after=(tv,), token=True)
        tb = ph.op("tensor", lambda e: e.matmul(
            pb[:], g["ones1"][:], cm[:, 3:4], start=True, stop=True),
            after=(tsq,), token=True)
        ph.op("scalar", lambda e: e.mul(g["stats"][:, 1:2], pb[:],
                                        -1.0 / 32.0), after=(tb,))
        ph.emit()



def phase_mha(p, g):
    dram = p.dram
    ph = Ph(p, "mha")
    with contextlib.ExitStack() as s_:
        ET = p.sb("c_et", [128, NT * 512], BF16, s_)
        stage = [p.sb(f"c_st{i}", [128, 512], F32, s_) for i in range(2)]
        mha = p.sb("c_mha", [128, 4 * D], F32, s_)
        rz = p.sb("c_rz", [128, 64], F32, s_)
        xnb_ = [p.sb(f"c_xn{i}", [128, D], F32, s_) for i in range(2)]
        y0 = [p.sb(f"c_y0{i}", [128, D], F32, s_) for i in range(2)]
        y0b = [p.sb(f"c_y0b{i}", [128, D], BF16, s_) for i in range(2)]
        hat = p.sb("c_hat", [128, D], F32, s_)
        mv = p.sb("c_mv", [128, 16], F32, s_)
        rs = p.sb("c_rs", [128, 1], F32, s_)
        g1 = p.sb("c_g1", [128, D], F32, s_)
        b1 = p.sb("c_b1", [128, D], F32, s_)
        lg1 = p.sb("c_lg1", [1, D], F32, s_)
        lb1 = p.sb("c_lb1", [1, D], F32, s_)
        cmask = p.sb("c_cmask", [128, 4 * 512], BF16, s_)
        psS = [p.ps(f"c_psS{i}", [128, 512], s_) for i in range(2)]
        psO = [p.ps(f"c_psO{i}", [128, 512], s_) for i in range(2)]
        psT = [p.ps(f"c_psT{i}", [128, 128], s_) for i in range(2)]

        tap_y0 = p.tap("y0", [T, D])
        tap_mha = p.tap("mha", [T, D])
        tap_et = p.tap("et0", [128, 4 * 512])
        tap_st = p.tap("st0", [128, 512])
        tap_mr = p.tap("mhraw", [T, D])
        tap_g1 = p.tap("g1rep", [128, D])
        tap_b1 = p.tap("b1rep", [128, D])

        t1 = ph.dma("sync", lg1[:], dram["l0_ln1_g"], stream="p0")
        t2 = ph.dma("sync", lb1[:], dram["l0_ln1_b"], stream="p1")
        cmt = []
        for off in range(4):
            cmt.append(ph.dma("sync", cmask[:, off * 512:(off + 1) * 512],
                              dram[f"c_cmask{off}"], stream="p2"))
        bt1 = bcast_row(ph, p, g, lg1, g1, psS, after=(t1,))
        gtoks = bcast_row(ph, p, g, lb1, b1, psS, after=(t2,),
                          guard=(bt1[0], bt1[1]))
        if tap_g1 is not None:
            ph.dma("gpsimd", tap_g1, g1[:], after=tuple(bt1), stream="tap")
            ph.dma("gpsimd", tap_b1, b1[:], after=tuple(gtoks), stream="tap")
        gtoks = gtoks + cmt

        prev_qb = None
        prev_qb2 = ()
        avdone = [None, None]
        trdone = [None, None]
        xtoks = [None, None]
        ystore = [None, None]
        for qb in range(4):
            ktmax = 4 * qb + 4
            prev_h_av = None
            exp_h = {}
            norm_ready = []
            for h in range(H):
                dt_ = h // 2
                off = (h % 2) * 64
                masktoks = []
                for kt in range(ktmax):
                    b = kt % 2
                    tmm = ph.op("tensor", lambda e, b=b, kt=kt, dt_=dt_,
                                off=off, qb=qb: e.matmul(
                        psS[b][:],
                        g["kT"][off:off + 64,
                                dt_ * T + kt * 128:dt_ * T + (kt + 1) * 128],
                        g["qT"][off:off + 64,
                                dt_ * T + qb * 512:dt_ * T + (qb + 1) * 512],
                        start=True, stop=True),
                        after=(exp_h.get(kt - 2), prev_qb) +
                        (prev_qb2 if h == 0 and kt == 0 else ()) +
                        (tuple(gtoks) if qb == 0 and h == 0 else ()),
                        token=True)
                    exp_h[kt] = ph.op(
                        "scalar", lambda e, b=b, kt=kt: e.activation(
                            ET[:, kt * 512:(kt + 1) * 512], psS[b][:], AF.Exp,
                            scale=1.0 / 32.0, bias=g["stats"][:, 1:2]),
                        after=(tmm,), token=True)
                    if kt >= 4 * qb:
                        mo = kt - 4 * qb
                        masktoks.append(ph.op(
                            "vector", lambda e, kt=kt, mo=mo: e.tensor_tensor(
                                ET[:, kt * 512:(kt + 1) * 512],
                                ET[:, kt * 512:(kt + 1) * 512],
                                cmask[:, mo * 512:(mo + 1) * 512],
                                op=ALU.mult),
                            after=(exp_h[kt],), token=True))
                ob = h % 2
                aft0 = tuple(exp_h[kt] for kt in range(ktmax)) + \
                    tuple(masktoks) + (avdone[ob],)
                for kt in range(ktmax):
                    ph.op("tensor", lambda e, ob=ob, kt=kt, h=h, ktmax=ktmax:
                          e.matmul(
                        psO[ob][0:64, :],
                        g["vtok"][:, kt * D + h * 64:kt * D + (h + 1) * 64],
                        ET[:, kt * 512:(kt + 1) * 512],
                        start=(kt == 0), stop=(kt == ktmax - 1)),
                        after=aft0 if kt == 0 else ())
                for kt in range(ktmax):
                    ph.op("tensor", lambda e, ob=ob, kt=kt, ktmax=ktmax:
                          e.matmul(
                        psO[ob][64:65, :], g["onescol_bf"][:],
                        ET[:, kt * 512:(kt + 1) * 512],
                        start=(kt == 0), stop=(kt == ktmax - 1)))
                prev_h_av = ph.tok_last("tensor")
                if tap_et is not None and qb == 0 and h == 0:
                    ph.dma("gpsimd", tap_et, ET[:, 0:4 * 512],
                           after=tuple(masktoks) + (prev_h_av,), stream="tap")
                ev = ph.op("vector", lambda e, ob=ob: e.tensor_copy(
                    stage[ob][0:65, :], psO[ob][0:65, :]),
                    after=(prev_h_av,), token=True)
                if tap_st is not None and qb == 0 and h == 0:
                    ph.dma("gpsimd", tap_st, stage[ob][:, :], after=(ev,),
                           stream="tap")
                avdone[ob] = ev
                for j in range(4):
                    tb_ = j % 2
                    ttr = ph.op("tensor", lambda e, tb_=tb_, ob=ob, j=j:
                                e.transpose(
                        psT[tb_][:, 0:65],
                        stage[ob][0:65, j * 128:(j + 1) * 128],
                        g["idf"][0:65, 0:65]),
                        after=(ev,) + (trdone[tb_] or ()), token=True)
                    sc = ph.op("scalar", lambda e, tb_=tb_, j=j, h=h: e.copy(
                        mha[:, j * D + h * 64:j * D + (h + 1) * 64],
                        psT[tb_][:, 0:64]), after=(ttr,), token=True)
                    vr = ph.op("vector", lambda e, tb_=tb_, j=j, h=h:
                               e.reciprocal(
                        rz[:, j * 16 + h:j * 16 + h + 1],
                        psT[tb_][:, 64:65]), after=(ttr,), token=True)
                    trdone[tb_] = (sc, vr)
                    norm_ready += [sc, vr]
            ytoks = []
            for j in range(4):
                i = qb * 4 + j
                b = j % 2
                ntk = []
                for h in range(H):
                    ntk.append(ph.op("vector", lambda e, j=j, h=h:
                                     e.tensor_scalar(
                        mha[:, j * D + h * 64:j * D + (h + 1) * 64],
                        mha[:, j * D + h * 64:j * D + (h + 1) * 64],
                        rz[:, j * 16 + h:j * 16 + h + 1], None, op0=ALU.mult),
                        after=tuple(norm_ready), token=True))
                tl = ph.dma("sync", xnb_[b][:],
                            dram["xn0_f"][i * 128:(i + 1) * 128, :],
                            after=(xtoks[b],), stream=f"xn{b}")
                if tap_mr is not None:
                    ytoks.append(ph.dma(
                        "gpsimd", tap_mr[i * 128:(i + 1) * 128, :],
                        mha[:, j * D:(j + 1) * D], after=tuple(ntk),
                        stream="tap"))
                tsa, tr = ln_stats(ph, mv, mha[:, j * D:(j + 1) * D], rs,
                                   1e-5, extra_after=tuple(ntk))
                ph.op("vector", lambda e, j=j: e.tensor_scalar(
                    hat[:], mha[:, j * D:(j + 1) * D], mv[:, 12:13], rs[:],
                    op0=ALU.subtract, op1=ALU.mult), after=(tr, tsa),
                    force=True)
                if tap_mha is not None:
                    tmp = ph.op("vector", lambda e: e.nop(), token=True)
                    ph.dma("gpsimd", tap_mha[i * 128:(i + 1) * 128, :], hat[:],
                           after=(tmp,), stream="tap")
                ph.op("vector", lambda e: e.tensor_tensor(
                    hat[:], hat[:], g1[:], op=ALU.mult))
                ph.op("vector", lambda e: e.tensor_tensor(
                    hat[:], hat[:], b1[:], op=ALU.add))
                xtoks[b] = ph.op("vector", lambda e, b=b: e.tensor_tensor(
                    y0[b][:], hat[:], xnb_[b][:], op=ALU.add),
                    after=(tl, ystore[b]), token=True)
                yt = ph.op("vector", lambda e, b=b: e.tensor_copy(
                    y0b[b][:], y0[b][:]), token=True)
                ph.dma("gpsimd", dram["y0_f"][i * 128:(i + 1) * 128, :],
                       y0[b][:], after=(yt,), stream=f"y{b}")
                t3 = ph.dma("gpsimd", dram["y0_b"][i * 128:(i + 1) * 128, :],
                            y0b[b][:], after=(yt,), stream=f"y{b}")
                if tap_y0 is not None:
                    t3 = ph.dma("gpsimd", tap_y0[i * 128:(i + 1) * 128, :],
                                y0[b][:], after=(yt,), stream=f"y{b}")
                ystore[b] = t3
                ytoks.append(t3)
            prev_qb = ytoks[-1]
            prev_qb2 = tuple(ytoks[-3:])
        ph.emit()



def phase_attn0(p, g):
    dram = p.dram
    ph = Ph(p, "attn0")
    with contextlib.ExitStack() as s_:
        xnch = p.sb("d_xnch", [128, ND * T], BF16, s_)
        xntok = p.sb("d_xntok", [128, NT * D], BF16, s_)
        ET = p.sb("d_et", [128, NT * 512], BF16, s_)
        att = [p.sb(f"d_att{i}", [128, D], F32, s_) for i in range(2)]
        xb = [p.sb(f"d_xb{i}", [128, D], F32, s_) for i in range(2)]
        yb = [p.sb(f"d_yb{i}", [128, D], F32, s_) for i in range(2)]
        rz = p.sb("d_rz", [128, 2], F32, s_)
        psS = [p.ps(f"d_psS{i}", [128, 512], s_) for i in range(2)]
        psU = [p.ps(f"d_psU{i}", [128, 1024], s_) for i in range(2)]
        psZ = [p.ps(f"d_psZ{i}", [128, 1], s_) for i in range(2)]

        tap_att = p.tap("attn0", [T, D])

        xt = []
        for dc in range(ND):
            xt.append(ph.dma("sync", xnch[:, dc * T:(dc + 1) * T],
                             dram["xn0_b"][:, dc * 128:(dc + 1) * 128],
                             transpose=True, stream="xt"))
        for ti in range(NT):
            xt.append(ph.dma("sync", xntok[:, ti * D:(ti + 1) * D],
                             dram["xn0_b"][ti * 128:(ti + 1) * 128, :],
                             stream="xt"))
        XT = tuple(xt)

        evdone = [None, None]
        for qb in range(4):
            exp_t = {}
            for kt in range(NT):
                b = kt % 2
                for dc in range(ND):
                    ph.op("tensor", lambda e, b=b, kt=kt, qb=qb, dc=dc:
                          e.matmul(
                        psS[b][:],
                        xnch[:, dc * T + kt * 128:dc * T + (kt + 1) * 128],
                        xnch[:, dc * T + qb * 512:dc * T + (qb + 1) * 512],
                        start=(dc == 0), stop=(dc == ND - 1)),
                        after=(XT + (exp_t.get(kt - 2),)) if dc == 0 else ())
                tmm = ph.tok_last("tensor")
                exp_t[kt] = ph.op("scalar", lambda e, b=b, kt=kt:
                                  e.activation(
                    ET[:, kt * 512:(kt + 1) * 512], psS[b][:], AF.Exp,
                    bias=g["stats"][:, 0:1]), after=(tmm,), token=True)
            aft0 = tuple(exp_t[kt] for kt in range(NT))
            for j in range(4):
                ob = j % 2
                i = qb * 4 + j
                for kt in range(NT):
                    for fo in range(2):
                        ph.op("tensor", lambda e, ob=ob, kt=kt, j=j, fo=fo:
                              e.matmul(
                            psU[ob][:, fo * 512:(fo + 1) * 512],
                            ET[:, kt * 512 + j * 128:kt * 512 + (j + 1) * 128],
                            xntok[:, kt * D + fo * 512:kt * D + (fo + 1) * 512],
                            start=(kt == 0), stop=(kt == NT - 1)),
                            after=(aft0 + (evdone[ob],))
                            if (kt == 0 and fo == 0) else ())
                    ph.op("tensor", lambda e, ob=ob, kt=kt, j=j: e.matmul(
                        psZ[ob][:],
                        ET[:, kt * 512 + j * 128:kt * 512 + (j + 1) * 128],
                        g["onescol_bf"][:],
                        start=(kt == 0), stop=(kt == NT - 1)))
                tav = ph.tok_last("tensor")
                tl1 = ph.dma("sync", xb[ob][:],
                             dram["x"][i * 128:(i + 1) * 128, :],
                             after=(evdone[ob],), stream=f"a{ob}")
                tl2 = ph.dma("sync", yb[ob][:],
                             dram["y0_f"][i * 128:(i + 1) * 128, :],
                             after=(evdone[ob],), stream=f"c{ob}")
                ph.op("vector", lambda e, ob=ob: e.reciprocal(
                    rz[:, ob:ob + 1], psZ[ob][:]), after=(tav,))
                tv = ph.op("vector", lambda e, ob=ob: e.tensor_scalar(
                    att[ob][:], psU[ob][:], rz[:, ob:ob + 1], None,
                    op0=ALU.mult), token=True)
                taptok = None
                if tap_att is not None:
                    taptok = ph.dma("gpsimd", tap_att[i * 128:(i + 1) * 128, :],
                                    att[ob][:], after=(tv,), stream="tap")
                ph.op("vector", lambda e, ob=ob: e.tensor_tensor(
                    att[ob][:], att[ob][:], xb[ob][:], op=ALU.add),
                    after=(tl1, taptok))
                tv2 = ph.op("vector", lambda e, ob=ob: e.tensor_tensor(
                    att[ob][:], att[ob][:], yb[ob][:], op=ALU.add),
                    after=(tl2,), token=True)
                evdone[ob] = ph.dma(
                    "gpsimd", dram["base0"][i * 128:(i + 1) * 128, :],
                    att[ob][:], after=(tv2,), stream=f"b{ob}")
        ph.emit()



def phase_moe(p, g, layer):
    dram = p.dram
    L = f"l{layer}"
    ph = Ph(p, f"moe{layer}")
    with contextlib.ExitStack() as s_:
        es_ = s_.enter_context(contextlib.ExitStack())
        acc = p.sb(f"m{layer}_acc", [128, NT * D], F32, s_)
        comb = p.sb(f"m{layer}_comb", [128, NT * NE], F32, s_)
        aux = ([p.sb(f"m{layer}_aux{i}", [128, D], F32, s_) for i in range(2)]
               if layer == 1 else None)
        psA = [p.ps(f"m{layer}_psA{i}", [128, 512], s_) for i in range(2)]
        psO = [p.ps(f"m{layer}_psO{i}", [128, 1024], s_) for i in range(2)]
        psL = p.ps(f"m{layer}_psL", [128, NE], s_)
        psC = p.ps(f"m{layer}_psC", [8, 128], s_)
        xchb = p.sb(f"m{layer}_xchb", [128, ND * T], BF16, es_)
        w1b = p.sb(f"m{layer}_w1", [128, ND * 2048], BF16, es_)
        w2b = p.sb(f"m{layer}_w2", [128, 16 * D], BF16, es_)
        hT = p.sb(f"m{layer}_hT", [128, 16 * 512], BF16, es_)
        combT = p.sb(f"m{layer}_combT", [8, T], F32, es_)
        gsb = p.sb(f"m{layer}_gate", [128, ND * NE], F32, es_)
        b1sb = p.sb(f"m{layer}_b1", [128, NE * 32], F32, es_)
        b2sb = p.sb(f"m{layer}_b2", [8, D], F32, es_)
        ytile = (p.sb(f"m{layer}_yt", [128, D], F32, es_)
                 if layer == 0 else None)
        stg = p.sb(f"m{layer}_stg", [128, ND * 128], F32, es_)
        lgt = p.sb(f"m{layer}_lgt", [128, 16], F32, es_)

        src_b = "y0_b" if layer == 0 else "xn1_b"
        xt = []
        for dc in range(ND):
            xt.append(ph.dma("sync", xchb[:, dc * T:(dc + 1) * T],
                             dram[src_b][:, dc * 128:(dc + 1) * 128],
                             transpose=True, stream="xc"))
        gt_tok = ph.dma("sync", gsb[:], dram[L + "_gate"], stream="g1")
        b1_tok = ph.dma("sync", b1sb[:], dram[L + "_b1"], stream="g2")
        b2_tok = ph.dma("sync", b2sb[:], dram[L + "_b2"], stream="g3")
        tap_cmb = p.tap(f"combine{layer}", [T, NE])
        tap_lg = p.tap(f"logits{layer}", [T, 16])

        ctoks = []
        tprev = [None, None]
        lgdone = None
        lgtap = None
        for ti in range(NT):
            if layer == 0:
                tl = ph.dma("sync", ytile[:],
                            dram["y0_f"][ti * 128:(ti + 1) * 128, :],
                            after=(lgdone,), stream="yt")
                for dc in range(ND):
                    b = dc % 2
                    ttr = ph.op("tensor", lambda e, b=b, dc=dc: e.transpose(
                        psA[b][:, 0:128], ytile[:, dc * 128:(dc + 1) * 128],
                        g["idf"][:]), after=(tl, tprev[b]), token=True)
                    tprev[b] = ph.op("vector", lambda e, b=b, dc=dc:
                                     e.tensor_copy(
                        stg[:, dc * 128:(dc + 1) * 128], psA[b][:, 0:128]),
                        after=(ttr,), token=True)
                aft = tuple(tprev) + (gt_tok,)
                for dc in range(ND):
                    ph.op("tensor", lambda e, dc=dc: e.matmul(
                        psL[:], stg[:, dc * 128:(dc + 1) * 128],
                        gsb[:, dc * NE:(dc + 1) * NE],
                        start=(dc == 0), stop=(dc == ND - 1)),
                        after=aft if dc == 0 else ())
            else:
                stok = []
                for dc in range(ND):
                    stok.append(ph.dma(
                        "sync", stg[:, dc * 128:(dc + 1) * 128],
                        dram["xn1_ch"][dc * 128:(dc + 1) * 128,
                                       4 + ti * 128:4 + (ti + 1) * 128],
                        after=(lgdone,), stream="yt"))
                for dc in range(ND):
                    ph.op("tensor", lambda e, dc=dc: e.matmul(
                        psL[:], stg[:, dc * 128:(dc + 1) * 128],
                        gsb[:, dc * NE:(dc + 1) * NE],
                        start=(dc == 0), stop=(dc == ND - 1)),
                        after=tuple(stok) + (gt_tok,) if dc == 0 else ())
            tmm = ph.tok_last("tensor")
            cc = comb[:, ti * NE:(ti + 1) * NE]
            tv = ph.op("vector", lambda e: e.tensor_copy(lgt[:, 0:8], psL[:]),
                       after=(tmm, lgtap), token=True)
            lgdone = tv
            trm = ph.op("vector", lambda e: e.tensor_reduce(
                lgt[:, 8:9], lgt[:, 0:8], axis=AX.X, op=ALU.max), token=True)
            tmsk = ph.op("vector", lambda e, cc=cc: e.tensor_scalar(
                cc, lgt[:, 0:8], lgt[:, 8:9], None, op0=ALU.is_ge),
                after=(trm,), force=True, token=True)
            if layer == 0:
                ph.op("vector", lambda e, cc=cc: e.scalar_tensor_tensor(
                    lgt[:, 0:8], cc, -1e30, lgt[:, 0:8], op0=ALU.mult,
                    op1=ALU.add))
                trm2 = ph.op("vector", lambda e: e.tensor_reduce(
                    lgt[:, 9:10], lgt[:, 0:8], axis=AX.X, op=ALU.max),
                    token=True)
                tvv = ph.op("vector", lambda e: e.tensor_scalar(
                    lgt[:, 10:11], lgt[:, 9:10], lgt[:, 8:9], None,
                    op0=ALU.subtract), after=(trm2,), force=True, token=True)
                te = ph.op("scalar", lambda e: e.activation(
                    lgt[:, 11:12], lgt[:, 10:11], AF.Exp), after=(tvv,),
                    token=True)
                ph.op("vector", lambda e: e.tensor_scalar(
                    lgt[:, 12:13], lgt[:, 11:12], 1.0, None, op0=ALU.add),
                    after=(te,))
                ph.op("vector", lambda e: e.reciprocal(lgt[:, 13:14],
                                                       lgt[:, 12:13]))
                ph.op("vector", lambda e: e.tensor_tensor(
                    lgt[:, 14:15], lgt[:, 11:12], lgt[:, 13:14], op=ALU.mult))
                ph.op("vector", lambda e, cc=cc: e.tensor_scalar(
                    cc, cc, lgt[:, 13:14], None, op0=ALU.mult))
                ph.op("vector", lambda e: e.tensor_scalar(
                    lgt[:, 0:8], lgt[:, 0:8], lgt[:, 9:10], None,
                    op0=ALU.is_ge))
                tv2 = ph.op("vector", lambda e, cc=cc: e.scalar_tensor_tensor(
                    cc, lgt[:, 0:8], lgt[:, 14:15], cc, op0=ALU.mult,
                    op1=ALU.add), token=True)
            else:
                tn = ph.op("scalar", lambda e: e.mul(
                    lgt[:, 9:10], lgt[:, 8:9], -1.0), after=(trm,), token=True)
                te = ph.op("scalar", lambda e: e.activation(
                    lgt[:, 0:8], lgt[:, 0:8], AF.Exp, bias=lgt[:, 9:10],
                    accum_out=lgt[:, 11:12]), after=(tn, tmsk), token=True)
                ph.op("vector", lambda e: e.reciprocal(
                    lgt[:, 12:13], lgt[:, 11:12]), after=(te,))
                tv2 = ph.op("vector", lambda e, cc=cc: e.tensor_scalar(
                    cc, cc, lgt[:, 12:13], None, op0=ALU.mult), token=True)
            if tap_cmb is not None:
                ph.dma("gpsimd", tap_cmb[ti * 128:(ti + 1) * 128, :], cc,
                       after=(tv2,), stream="tap")
            if tap_lg is not None:
                lgtap = ph.dma("gpsimd", tap_lg[ti * 128:(ti + 1) * 128, :],
                               lgt[:], after=(tv2,), stream="tap")
            ttr = ph.op("tensor", lambda e, cc=cc: e.transpose(
                psC[:], cc, g["idf"][:]),
                after=(tv2, ctoks[-1] if ctoks else None), token=True)
            ctoks.append(ph.op("vector", lambda e, ti=ti: e.tensor_copy(
                combT[:, ti * 128:(ti + 1) * 128], psC[:]),
                after=(ttr,), token=True))

        initoks = []
        psO_guard = [None, None]
        auxg = [None, None]
        for ti in range(NT):
            b = ti % 2
            for fo in range(2):
                ph.op("tensor", lambda e, b=b, ti=ti, fo=fo: e.matmul(
                    psO[b][:, fo * 512:(fo + 1) * 512],
                    combT[:, ti * 128:(ti + 1) * 128],
                    b2sb[:, fo * 512:(fo + 1) * 512], start=True, stop=True),
                    after=(ctoks[ti], psO_guard[b], b2_tok)
                    if fo == 0 else ())
            tmm = ph.tok_last("tensor")
            if layer == 0:
                tv = ph.op("vector", lambda e, b=b, ti=ti: e.tensor_copy(
                    acc[:, ti * D:(ti + 1) * D], psO[b][:]),
                    after=(tmm,), token=True)
            else:
                ta = ph.dma("sync", aux[0][:],
                            dram["att1"][ti * 128:(ti + 1) * 128, :],
                            after=(auxg[0],), stream="ax0")
                tb = ph.dma("sync", aux[1][:],
                            dram["x1_f"][ti * 128:(ti + 1) * 128, :],
                            after=(auxg[1],), stream="ax1")
                auxg[0] = ph.op("vector", lambda e, b=b, ti=ti:
                                e.tensor_tensor(
                    acc[:, ti * D:(ti + 1) * D], psO[b][:], aux[0][:],
                    op=ALU.add), after=(tmm, ta), token=True)
                tv = ph.op("vector", lambda e, ti=ti: e.tensor_tensor(
                    acc[:, ti * D:(ti + 1) * D], acc[:, ti * D:(ti + 1) * D],
                    aux[1][:], op=ALU.add), after=(tb,), token=True)
                auxg[1] = tv
            psO_guard[b] = tv
            initoks.append(tv)

        acctoks = list(initoks)
        accdone = [initoks[-2], initoks[-1]]
        w1tok = {}
        w2tok = {}
        mm1done = None
        mm2done = None
        relu_prev = {}
        if layer == 0:
            relu_prev[-2] = tprev[0]
            relu_prev[-1] = tprev[1]
        for e_ in range(NE):
            for fh in range(2):
                for dc in range(ND):
                    w1tok[dc] = ph.dma(
                        "gpsimd", w1b[:, dc * 2048:(dc + 1) * 2048],
                        dram[L + "_w1"][e_, dc * 128:(dc + 1) * 128,
                                        fh * 2048:(fh + 1) * 2048],
                        after=(mm1done,), stream="w1")
                for ft in range(16):
                    w2tok[ft] = ph.dma(
                        "gpsimd", w2b[:, ft * D:(ft + 1) * D],
                        dram[L + "_w2"][e_, fh * 2048 + ft * 128:
                                        fh * 2048 + (ft + 1) * 128, :],
                        after=(mm2done,), stream="w2")
                for tc in range(4):
                    for ft in range(16):
                        b = ft % 2
                        aft = tuple(w1tok[dc] for dc in range(ND)) + \
                            (relu_prev.get(ft - 2), mm1done) + tuple(xt)
                        for dc in range(ND):
                            ph.op("tensor", lambda e, b=b, dc=dc, ft=ft,
                                  tc=tc: e.matmul(
                                psA[b][:],
                                w1b[:, dc * 2048 + ft * 128:
                                    dc * 2048 + (ft + 1) * 128],
                                xchb[:, dc * T + tc * 512:
                                     dc * T + (tc + 1) * 512],
                                start=(dc == 0), stop=(dc == ND - 1)),
                                after=aft if dc == 0 else ())
                        tmm = ph.tok_last("tensor")
                        col = e_ * 32 + fh * 16 + ft
                        relu_prev[ft] = ph.op(
                            "scalar", lambda e, b=b, ft=ft, col=col:
                            e.activation(
                                hT[:, ft * 512:(ft + 1) * 512], psA[b][:],
                                AF.Relu, bias=b1sb[:, col:col + 1]),
                            after=(tmm, b1_tok), token=True)
                    mm1done = tmm
                    for tt in range(4):
                        ti = tc * 4 + tt
                        ob = tt % 2
                        aft = tuple(relu_prev[ft] for ft in range(16)) + \
                            tuple(w2tok[ft] for ft in range(16)) + \
                            (accdone[ob],)
                        for ft in range(16):
                            for fo in range(2):
                                ph.op("tensor", lambda e, ob=ob, ft=ft, fo=fo,
                                      tt=tt: e.matmul(
                                    psO[ob][:, fo * 512:(fo + 1) * 512],
                                    hT[:, ft * 512 + tt * 128:
                                        ft * 512 + (tt + 1) * 128],
                                    w2b[:, ft * D + fo * 512:
                                        ft * D + (fo + 1) * 512],
                                    start=(ft == 0), stop=(ft == 15)),
                                    after=aft if (ft == 0 and fo == 0) else ())
                        tmm2 = ph.tok_last("tensor")
                        accdone[ob] = ph.op(
                            "vector", lambda e, ob=ob, ti=ti, e2=e_:
                            e.scalar_tensor_tensor(
                                acc[:, ti * D:(ti + 1) * D], psO[ob][:],
                                comb[:, ti * NE + e2:ti * NE + e2 + 1],
                                acc[:, ti * D:(ti + 1) * D],
                                op0=ALU.mult, op1=ALU.add),
                            after=(tmm2, acctoks[ti]), token=True)
                        acctoks[ti] = accdone[ob]
                    mm2done = tmm2

        es_.close()
        if layer == 0:
            tail_l0(p, g, ph, s_, acc, acctoks, psA)
        else:
            for ti in range(NT):
                ph.dma("sync", dram["out"][ti * 128:(ti + 1) * 128, :],
                       acc[:, ti * D:(ti + 1) * D],
                       after=(acctoks[ti], acctoks[-1]), stream="out")
        ph.emit()


def tail_l0(p, g, ph, s_, acc, acctoks, psA):
    dram = p.dram
    g2 = p.sb("t0_g2", [128, D], F32, s_)
    b2r = p.sb("t0_b2r", [128, D], F32, s_)
    lg2 = p.sb("t0_lg2", [1, D], F32, s_)
    lb2 = p.sb("t0_lb2", [1, D], F32, s_)
    mv = p.sb("t0_mv", [128, 16], F32, s_)
    rs = p.sb("t0_rs", [128, 1], F32, s_)
    hat = p.sb("t0_hat", [128, D], F32, s_)
    bb = [p.sb(f"t0_bb{i}", [128, D], F32, s_) for i in range(2)]
    x1 = [p.sb(f"t0_x1{i}", [128, D], F32, s_) for i in range(2)]
    xn1 = p.sb("t0_xn1", [128, D], F32, s_)
    xn1b = [p.sb(f"t0_xn1b{i}", [128, D], BF16, s_) for i in range(2)]
    stg2 = [p.sb(f"t0_stg{i}", [128, ND * 128], F32, s_) for i in range(2)]
    zc = p.sb("t0_zc", [128, 4], F32, s_)

    tap_x1 = p.tap("x1", [T, D])
    tap_xn1 = p.tap("xn1", [T, D])

    TG = acctoks[-1]
    t1 = ph.dma("sync", lg2[:], dram["l0_ln2_g"], after=(TG,), stream="p0")
    t2 = ph.dma("sync", lb2[:], dram["l0_ln2_b"], stream="p1")
    bt1 = bcast_row(ph, p, g, lg2, g2, psA, after=(t1, TG))
    bt = tuple(bt1)
    bt2 = bcast_row(ph, p, g, lb2, b2r, psA, after=(t2,),
                    guard=(bt1[0], bt1[1]))
    bt += tuple(bt2)
    zt = ph.op("vector", lambda e: e.memset(zc[:], 0.0), after=(TG,),
               token=True)
    for dc in range(ND):
        ph.dma("gpsimd", dram["xn1_ch"][dc * 128:(dc + 1) * 128, 0:4], zc[:],
               after=(zt,), stream="zc")

    bbg = [None, None]
    x1g = [None, None]
    stgg = [None, None]
    nbg = [None, None]
    x1st = [None, None]
    xn1tap = None
    trg = [bt2[-2], bt2[-1]]
    for ti in range(NT):
        b = ti % 2
        atile = acc[:, ti * D:(ti + 1) * D]
        tl = ph.dma("sync", bb[b][:],
                    dram["base0"][ti * 128:(ti + 1) * 128, :],
                    after=(bbg[b], TG), stream=f"bb{b}")
        tsa, tr = ln_stats(ph, mv, atile, rs, 1e-5,
                           extra_after=(acctoks[ti], TG) + bt)
        ph.op("vector", lambda e, atile=atile: e.tensor_scalar(
            hat[:], atile, mv[:, 12:13], rs[:], op0=ALU.subtract,
            op1=ALU.mult), after=(tr, tsa), force=True)
        ph.op("vector", lambda e: e.tensor_tensor(hat[:], hat[:], g2[:],
                                                  op=ALU.mult))
        ph.op("vector", lambda e: e.tensor_tensor(hat[:], hat[:], b2r[:],
                                                  op=ALU.add))
        tx = ph.op("vector", lambda e, b=b: e.tensor_tensor(
            x1[b][:], hat[:], bb[b][:], op=ALU.add),
            after=(tl, x1g[b], x1st[b]), token=True)
        bbg[b] = tx
        x1st[b] = ph.dma("gpsimd", dram["x1_f"][ti * 128:(ti + 1) * 128, :],
                         x1[b][:], after=(tx,), stream=f"xs{b}")
        if tap_x1 is not None:
            x1st[b] = ph.dma("gpsimd", tap_x1[ti * 128:(ti + 1) * 128, :],
                             x1[b][:], after=(tx,), stream=f"xs{b}")
        tsb, tr2 = ln_stats(ph, mv, x1[b], rs, 1e-5, extra_after=(tx,))
        tn = ph.op("vector", lambda e, b=b: e.tensor_scalar(
            xn1[:], x1[b][:], mv[:, 12:13], rs[:], op0=ALU.subtract,
            op1=ALU.mult), after=(tr2, tsb, xn1tap), force=True, token=True)
        x1g[b] = tn
        tc = ph.op("vector", lambda e, b=b: e.tensor_copy(xn1b[b][:], xn1[:]),
                   after=(nbg[b],), token=True)
        nbg[b] = ph.dma("gpsimd", dram["xn1_b"][ti * 128:(ti + 1) * 128, :],
                        xn1b[b][:], after=(tc,), stream=f"nb{b}")
        if tap_xn1 is not None:
            xn1tap = ph.dma("gpsimd", tap_xn1[ti * 128:(ti + 1) * 128, :],
                            xn1[:], after=(tn, stgg[b]), stream="tap")
        trtoks = []
        for dc in range(ND):
            tb_ = dc % 2
            ttr = ph.op("tensor", lambda e, tb_=tb_, dc=dc: e.transpose(
                psA[tb_][:, 0:128], xn1[:, dc * 128:(dc + 1) * 128],
                g["idf"][:]), after=(tn, trg[tb_], stgg[b]), token=True)
            trg[tb_] = ph.op("vector", lambda e, tb_=tb_, dc=dc, b=b:
                             e.tensor_copy(
                stg2[b][:, dc * 128:(dc + 1) * 128], psA[tb_][:, 0:128]),
                after=(ttr,), token=True)
            trtoks.append(trg[tb_])
        last = None
        for dc in range(ND):
            last = ph.dma(
                "gpsimd",
                dram["xn1_ch"][dc * 128:(dc + 1) * 128,
                               4 + ti * 128:4 + (ti + 1) * 128],
                stg2[b][:, dc * 128:(dc + 1) * 128],
                after=(trtoks[dc],), stream=f"ch{b}")
        stgg[b] = last


def phase_rwkv(p, g):
    dram = p.dram
    ph = Ph(p, "rwkv")
    with contextlib.ExitStack() as s_:
        wmaa1 = p.sb("r_wmaa1", [128, ND * 192], BF16, s_)
        wmaa2 = p.sb("r_wmaa2", [64, 3 * D], BF16, s_)
        wd1 = p.sb("r_wd1", [128, ND * 64], BF16, s_)
        wd2 = p.sb("r_wd2", [64, D], BF16, s_)
        wa1 = p.sb("r_wa1", [128, ND * 64], BF16, s_)
        wa2 = p.sb("r_wa2", [64, D], BF16, s_)
        wk1 = p.sb("r_wk1", [128, ND * 64], BF16, s_)
        wk2 = p.sb("r_wk2", [64, D], BF16, s_)
        wg1 = p.sb("r_wg1", [128, ND * 128], BF16, s_)
        wg2 = p.sb("r_wg2", [128, D], BF16, s_)
        wr = p.sb("r_wr", [128, ND * D], BF16, s_)
        wkk = p.sb("r_wk", [128, ND * D], BF16, s_)
        wv = p.sb("r_wv", [128, ND * D], BF16, s_)
        wo = p.sb("r_wo", [128, ND * D], BF16, s_)
        vecs = {}
        for v in ["time_maa_x", "time_maa_r", "time_maa_w", "time_maa_k",
                  "time_maa_v", "time_maa_a", "time_maa_g", "time_decay",
                  "time_aaaaa", "faaaa"]:
            vecs[v] = p.sb("r_" + v, [128, ND], F32, s_)
        lnw = p.sb("r_lnw", [128, D], F32, s_)
        lnb = p.sb("r_lnb", [128, D], F32, s_)
        l1v = p.sb("r_l1v", [1, D], F32, s_)
        l2v = p.sb("r_l2v", [1, D], F32, s_)
        hb = p.sb("r_hb", [128, ND * H], BF16, s_)
        SS = p.sb("r_SS", [128, 512], F32, s_)
        SSb = p.sb("r_SSb", [128, 512], BF16, s_)

        def scbuf(n, w=SCN, dt=F32):
            return p.sb("r_" + n, [128, ND * w], dt, s_)

        xch = scbuf("xch", SCN + 1)
        xx = scbuf("xx")
        rT = scbuf("rT")
        wraw = scbuf("wraw")
        ks = scbuf("ks")
        kkn = scbuf("kkn")
        ag = scbuf("ag")
        gT = scbuf("gT")
        vtok = p.sb("r_vtok", [128, CPS * D], F32, s_)
        kkn_tok = p.sb("r_kkntok", [128, CPS * D], F32, s_)
        y_sc = p.sb("r_ysc", [128, CPS * D], F32, s_)
        xs = p.sb("r_xs", [128, ND * SCN], BF16, s_)
        mixbuf = p.sb("r_mixbuf", [64, 3 * SCN], BF16, s_)
        tnh = p.sb("r_tnh", [64, SCN], BF16, s_)
        tnh128 = p.sb("r_tnh128", [128, SCN], BF16, s_)
        tmpa = [p.sb(f"r_tmpa{i}", [128, SCN], F32, s_) for i in range(3)]

        p_ch = p.sb("r_pch", [128, D], F32, s_)
        p_tok = p.sb("r_ptok", [128, D], F32, s_)
        E0 = p.sb("r_E0", [128, D], F32, s_)
        E1 = p.sb("r_E1", [128, D], F32, s_)
        E2 = p.sb("r_E2", [128, D], F32, s_)
        gl = p.sb("r_gl", [128, ND], F32, s_)
        Ab = p.sb("r_Ab", [128, D], BF16, s_)
        Rb = p.sb("r_Rb", [128, D], BF16, s_)
        Kcb = p.sb("r_Kcb", [128, D], BF16, s_)
        Bcb = p.sb("r_Bcb", [128, D], BF16, s_)
        Ktb = p.sb("r_Ktb", [128, D], BF16, s_)
        Btb = p.sb("r_Btb", [128, D], BF16, s_)
        Vb = p.sb("r_Vb", [128, D], BF16, s_)
        GtS = [p.sb(f"r_Gt{s}", [128, 128], BF16, s_) for s in range(NSL)]
        UtS = [[p.sb(f"r_Ut{s}_{i}", [128, 128], BF16, s_) for i in range(2)]
               for s in range(NSL)]
        MtS = [[p.sb(f"r_Mt{s}_{i}", [128, 128], BF16, s_) for i in range(7)]
               for s in range(NSL)]
        WtS = [p.sb(f"r_Wt{s}", [128, 128], BF16, s_) for s in range(NSL)]
        W2tS = [p.sb(f"r_W2t{s}", [128, 128], BF16, s_) for s in range(NSL)]
        XbS = [p.sb(f"r_Xb{s}", [128, 64], BF16, s_) for s in range(NSL)]
        mvgS = [p.sb(f"r_mvgS{s}", [128, 8], F32, s_) for s in range(NSL)]
        rsgS = [p.sb(f"r_rsgS{s}", [128, 1], F32, s_) for s in range(NSL)]
        bscal = p.sb("r_bscal", [128, H], F32, s_)
        ygb = p.sb("r_ygb", [128, D], BF16, s_)
        mvg = p.sb("r_mvg", [128, 16], F32, s_)
        rsg = p.sb("r_rsg", [128, 1], F32, s_)

        psA = [p.ps(f"r_psA{i}", [128, 512], s_) for i in range(2)]
        psB = [p.ps(f"r_psB{i}", [128, 128], s_) for i in range(2)]
        pX = p.ps("r_pX", [128, 64], s_)
        pP7 = p.ps("r_pP7", [128, 64], s_)
        ps96 = [p.ps(f"r_ps96{i}", [128, 128], s_) for i in range(2)]

        psg = {"A0": None, "A1": None, "B0": None, "B1": None,
               "X": None, "P7": None, "960": None, "961": None}
        bankD1 = {"B0": lambda: psB[0][:], "B1": lambda: psB[1][:],
                  "A0": lambda: psA[0][:, 0:128],
                  "A1": lambda: psA[1][:, 0:128]}
        bankD2 = {"X": lambda: pX[:], "P7": lambda: pP7[:]}
        bankD3 = {"960": lambda: ps96[0][0:64, 0:64],
                  "961": lambda: ps96[1][0:64, 0:64]}
        _pc = {"1": 0, "2": 0, "3": 0}

        def nbank(d):
            pool, m = {"1": (["B0", "B1", "A0", "A1"], bankD1),
                       "2": (["X", "P7"], bankD2),
                       "3": (["960", "961"], bankD3)}[d]
            k = pool[_pc[d] % len(pool)]
            _pc[d] += 1
            return k, m[k]()

        tap_att1 = p.tap("att1", [T, D])
        tap_y = p.tap("rwkv_y", [T, D])

        wt = []
        for dc in range(ND):
            d0 = dc * 128
            wt.append(ph.dma("gpsimd", wmaa1[:, dc * 192:(dc + 1) * 192],
                             dram["l1_maa_w1"][d0:d0 + 128, :], stream="wl"))
            wt.append(ph.dma("gpsimd", wd1[:, dc * 64:(dc + 1) * 64],
                             dram["l1_decay_w1"][d0:d0 + 128, :], stream="wl"))
            wt.append(ph.dma("gpsimd", wa1[:, dc * 64:(dc + 1) * 64],
                             dram["l1_aaa_w1"][d0:d0 + 128, :], stream="wl"))
            wt.append(ph.dma("gpsimd", wk1[:, dc * 64:(dc + 1) * 64],
                             dram["l1_kkk_w1"][d0:d0 + 128, :], stream="wl"))
            wt.append(ph.dma("gpsimd", wg1[:, dc * 128:(dc + 1) * 128],
                             dram["l1_gate_w1"][d0:d0 + 128, :], stream="wl"))
            for wnm, wdst in [("l1_w_r", wr), ("l1_w_k", wkk),
                              ("l1_w_v", wv), ("l1_w_o", wo)]:
                wt.append(ph.dma("gpsimd", wdst[:, dc * D:(dc + 1) * D],
                                 dram[wnm][d0:d0 + 128, :], stream="wl"))
        for wnm, wdst in [("l1_maa_w2", wmaa2), ("l1_decay_w2", wd2),
                          ("l1_aaa_w2", wa2), ("l1_kkk_w2", wk2),
                          ("l1_gate_w2", wg2)]:
            wt.append(ph.dma("gpsimd", wdst[:], dram[wnm], stream="wl"))
        wt.append(ph.dma("sync", hb[:], dram["c_headblk_bf"], stream="wl2"))
        for v in vecs:
            wt.append(ph.dma("sync", vecs[v][:], dram["l1_" + v],
                             stream="wl2"))
        tl1 = ph.dma("sync", l1v[:], dram["l1_ln_w"], stream="lv1")
        tl2 = ph.dma("sync", l2v[:], dram["l1_ln_b"], stream="lv2")
        bc1 = bcast_row(ph, p, g, l1v, lnw, psA, after=(tl1,))
        bc2 = bcast_row(ph, p, g, l2v, lnb, psA, after=(tl2,),
                        guard=(bc1[0], bc1[1]))
        wt += bc1 + bc2
        psg["A0"] = bc2[-2]
        psg["A1"] = bc2[-1]
        sst = ph.op("vector", lambda e: e.memset(SS[:], 0.0), token=True)
        wt.append(sst)
        WT = tuple(wt)

        def chs(buf, dc, c0=0, w=SCN, ext=0):
            base = dc * (SCN + ext)
            return buf[:, base + c0:base + c0 + w]

        def dview(buf):
            return buf[:].rearrange("p (a b) -> p a b", b=128)

        def MM(bank, fn, after=(), token=True):
            return ph.op("tensor", fn, after=tuple(after) + (psg[bank],),
                         token=token)

        SNAMES = ["r", "w", "k", "v", "a", "g"]
        tmpg = {0: None, 1: None, 2: None}

        prev_sc = None
        CG = ()
        ssb_prev = sst
        for sc in range(NSC):
            ci = sc
            xtl = []
            for dc in range(ND):
                xtl.append(ph.dma(
                    "sync", chs(xch, dc, 0, SCN + 1, 1),
                    dram["xn1_ch"][dc * 128:(dc + 1) * 128,
                                   sc * SCN + 3:sc * SCN + SCN + 4],
                    after=(prev_sc,), stream="xc"))
            xxt = []
            for dc in range(ND):
                xxt.append(ph.op("vector", lambda e, dc=dc: e.tensor_tensor(
                    chs(xx, dc), chs(xch, dc, 0, SCN, 1),
                    chs(xch, dc, 1, SCN, 1), op=ALU.subtract),
                    after=tuple(xtl) + WT + CG, token=True))
            mmxt = []
            for dc in range(ND):
                mmxt.append(ph.op("vector", lambda e, dc=dc:
                                  e.scalar_tensor_tensor(
                    chs(xs, dc), chs(xx, dc),
                    vecs["time_maa_x"][:, dc:dc + 1], chs(xch, dc, 1, SCN, 1),
                    op0=ALU.mult, op1=ALU.add), after=(xxt[dc],), token=True))
            mixt = {}
            for g2_ in range(2):
                bank = "96" + str(g2_)
                for dc in range(ND):
                    if dc == 0:
                        MM(bank, lambda e, g2_=g2_: e.matmul(
                            ps96[g2_][0:96, :],
                            wmaa1[:, g2_ * 96:g2_ * 96 + 96],
                            chs(xs, 0), start=True, stop=False),
                            after=tuple(mmxt), token=False)
                    else:
                        ph.op("tensor", lambda e, g2_=g2_, dc=dc: e.matmul(
                            ps96[g2_][0:96, :],
                            wmaa1[:, dc * 192 + g2_ * 96:
                                  dc * 192 + (g2_ + 1) * 96],
                            chs(xs, dc), start=False, stop=(dc == ND - 1)))
                tmm = ph.tok_last("tensor")
                for s3 in range(3):
                    s6 = g2_ * 3 + s3
                    par = (s6 % 2) * 32
                    blk_ = (s6 // 2) * SCN
                    mixt[s6] = ph.op("scalar", lambda e, g2_=g2_, s3=s3,
                                     par=par, blk_=blk_: e.activation(
                        mixbuf[par:par + 32, blk_:blk_ + SCN],
                        ps96[g2_][s3 * 32:(s3 + 1) * 32, :], AF.Tanh),
                        after=(tmm,), token=True)
                psg[bank] = mixt[g2_ * 3 + 2]

            xs_guard = None
            pd = {}
            for s6 in range(6):
                vname = "time_maa_" + SNAMES[s6]
                par = (s6 % 2) * 32
                blk_ = (s6 // 2) * SCN
                mslice = mixbuf[par:par + 32, blk_:blk_ + SCN]
                w2slice = lambda dc, par=par, s6=s6: wmaa2[
                    par:par + 32,
                    (s6 // 2) * D + dc * 128:(s6 // 2) * D + (dc + 1) * 128]
                xst = []
                for dc in range(ND):
                    b = dc % 2
                    bank = "A" + str(b)
                    tmm = MM(bank, lambda e, dc=dc, b=b, mslice=mslice,
                             w2s=w2slice(dc): e.matmul(
                        psA[b][:, 0:SCN], w2s, mslice, start=True, stop=True),
                        after=(mixt[s6], xs_guard))
                    tv = ph.op("vector", lambda e, b=b, dc=dc, vname=vname:
                               e.scalar_tensor_tensor(
                        tmpa[b][:], psA[b][:, 0:SCN],
                        vecs[vname][:, dc:dc + 1], chs(xx, dc),
                        op0=ALU.add, op1=ALU.mult),
                        after=(tmm, tmpg[b], xxt[dc]), token=True)
                    psg[bank] = tv
                    xst.append(ph.op("vector", lambda e, b=b, dc=dc:
                                     e.tensor_tensor(
                        chs(xs, dc), tmpa[b][:], chs(xch, dc, 1, SCN, 1),
                        op=ALU.add), token=True))
                    tmpg[b] = xst[-1]
                XST = tuple(xst)

                if s6 in (0, 2):
                    W_ = wr if s6 == 0 else wkk
                    dst = rT if s6 == 0 else ks
                    ev = [None, None]
                    for do in range(ND):
                        b = do % 2
                        bank = "A" + str(b)
                        for dc in range(ND):
                            if dc == 0:
                                MM(bank, lambda e, b=b, do=do, W_=W_:
                                   e.matmul(
                                    psA[b][:, 0:SCN],
                                    W_[:, do * 128:(do + 1) * 128],
                                    chs(xs, 0), start=True, stop=False),
                                    after=XST, token=False)
                            else:
                                ph.op("tensor", lambda e, b=b, dc=dc, do=do,
                                      W_=W_: e.matmul(
                                    psA[b][:, 0:SCN],
                                    W_[:, dc * D + do * 128:
                                       dc * D + (do + 1) * 128],
                                    chs(xs, dc), start=False,
                                    stop=(dc == ND - 1)))
                        tmm = ph.tok_last("tensor")
                        ev[b] = ph.op("vector", lambda e, b=b, do=do, dst=dst:
                                      e.tensor_copy(chs(dst, do),
                                                    psA[b][:, 0:SCN]),
                                      after=(tmm,), token=True)
                        psg[bank] = ev[b]
                    if s6 == 0:
                        pd[(0, 0)], pd[(0, 1)] = ev[0], ev[1]
                    else:
                        for dc in range(ND):
                            if dc == 0:
                                MM("A0", lambda e: e.matmul(
                                    psA[0][0:64, 0:SCN], wk1[:, 0:64],
                                    chs(xs, 0), start=True, stop=False),
                                    after=XST, token=False)
                            else:
                                ph.op("tensor", lambda e, dc=dc: e.matmul(
                                    psA[0][0:64, 0:SCN],
                                    wk1[:, dc * 64:(dc + 1) * 64],
                                    chs(xs, dc), start=False,
                                    stop=(dc == ND - 1)))
                        tmm = ph.tok_last("tensor")
                        tt1 = ph.op("scalar", lambda e: e.activation(
                            tnh[:], psA[0][0:64, 0:SCN], AF.Tanh),
                            after=(tmm,), token=True)
                        psg["A0"] = tt1
                        kkt = []
                        kst = []
                        for do in range(ND):
                            b = do % 2
                            bank = f"B{b}"
                            tmm = MM(bank, lambda e, do=do, b=b: e.matmul(
                                psB[b][:, 0:SCN],
                                wk2[:, do * 128:(do + 1) * 128], tnh[:],
                                start=True, stop=True), after=(tt1,))
                            kkt.append(ph.op(
                                "vector", lambda e, do=do, b=b:
                                e.tensor_tensor(
                                    chs(kkn, do), psB[b][:, 0:SCN],
                                    chs(ks, do), op=ALU.add),
                                after=(tmm, ev[do % 2]), token=True))
                            psg[bank] = kkt[-1]
                            t1_ = ph.op("vector", lambda e, do=do, b=b:
                                        e.tensor_scalar(
                                tmpa[b][:], chs(wraw, do), 0.5, 0.0,
                                op0=ALU.mult, op1=ALU.min),
                                after=(pd[(1, "wr", do)], tmpg[b]),
                                token=True)
                            t2_ = ph.op("scalar", lambda e, b=b: e.activation(
                                tmpa[b][:], tmpa[b][:], AF.Exp),
                                after=(t1_,), token=True)
                            kst.append(ph.op("vector", lambda e, do=do, b=b:
                                             e.tensor_tensor(
                                chs(ks, do), chs(ks, do), tmpa[b][:],
                                op=ALU.mult),
                                after=(t2_, kkt[-1]), token=True))
                            tmpg[b] = kst[-1]
                        pd[(2, 0)] = kkt[-1]
                        pd[(2, 1)] = kst[-1]
                        pd[(2, "kk")] = tuple(kkt)
                        pd[(2, "ks")] = tuple(kst)
                elif s6 == 1:
                    for dc in range(ND):
                        if dc == 0:
                            MM("A0", lambda e: e.matmul(
                                psA[0][0:64, 0:SCN], wd1[:, 0:64],
                                chs(xs, 0), start=True, stop=False),
                                after=XST, token=False)
                        else:
                            ph.op("tensor", lambda e, dc=dc: e.matmul(
                                psA[0][0:64, 0:SCN],
                                wd1[:, dc * 64:(dc + 1) * 64],
                                chs(xs, dc), start=False,
                                stop=(dc == ND - 1)))
                    tmm = ph.tok_last("tensor")
                    tt1 = ph.op("scalar", lambda e: e.activation(
                        tnh[:], psA[0][0:64, 0:SCN], AF.Tanh), after=(tmm,),
                        token=True)
                    psg["A0"] = tt1
                    wrt = []
                    for do in range(ND):
                        b = do % 2
                        bank = f"B{b}"
                        tmm = MM(bank, lambda e, do=do, b=b: e.matmul(
                            psB[b][:, 0:SCN],
                            wd2[:, do * 128:(do + 1) * 128], tnh[:],
                            start=True, stop=True), after=(tt1,))
                        tz = ph.op("vector", lambda e, b=b, do=do:
                                   e.tensor_scalar(
                            tmpa[b][:], psB[b][:, 0:SCN],
                            vecs["time_decay"][:, do:do + 1], None,
                            op0=ALU.add), after=(tmm, tmpg[b]), token=True)
                        psg[bank] = tz
                        ta1 = ph.op("scalar", lambda e, b=b: e.activation(
                            tmpa[2][:], tmpa[b][:], AF.Abs),
                            after=(tmpg[2], tz), token=True)
                        ph.op("scalar", lambda e: e.activation(
                            tmpa[2][:], tmpa[2][:], AF.Exp, scale=-1.0))
                        ta = ph.op("scalar", lambda e: e.activation(
                            tmpa[2][:], tmpa[2][:], AF.Ln, bias=1.0),
                            token=True)
                        ph.op("vector", lambda e, b=b: e.tensor_scalar(
                            tmpa[b][:], tmpa[b][:], -1.0, 0.0,
                            op0=ALU.mult, op1=ALU.max), after=(tz, ta1))
                        tadd = ph.op("vector", lambda e, b=b:
                                     e.tensor_tensor(
                            tmpa[b][:], tmpa[b][:], tmpa[2][:], op=ALU.add),
                            after=(ta,), token=True)
                        tmpg[2] = tadd
                        wrt.append(ph.op("vector", lambda e, b=b, do=do:
                                         e.tensor_scalar(
                            chs(wraw, do), tmpa[b][:], -1.0, -0.5,
                            op0=ALU.mult, op1=ALU.add), token=True))
                        pd[(1, "wr", do)] = wrt[-1]
                        tmpg[b] = wrt[-1]
                    pd[(1, 0)] = wrt[-2]
                    pd[(1, 1)] = wrt[-1]
                elif s6 == 3:
                    vt = []
                    for fo in range(2):
                        b = fo % 2
                        bank = "A" + str(b)
                        for dc in range(ND):
                            if dc == 0:
                                MM(bank, lambda e, b=b, fo=fo: e.matmul(
                                    psA[b][:], chs(xs, 0),
                                    wv[:, fo * 512:(fo + 1) * 512],
                                    start=True, stop=False),
                                    after=XST, token=False)
                            else:
                                ph.op("tensor", lambda e, b=b, dc=dc, fo=fo:
                                      e.matmul(
                                    psA[b][:], chs(xs, dc),
                                    wv[:, dc * D + fo * 512:
                                       dc * D + (fo + 1) * 512],
                                    start=False, stop=(dc == ND - 1)))
                        tmm = ph.tok_last("tensor")
                        vt.append(ph.op(
                            "vector", lambda e, fo=fo, b=b: e.tensor_copy(
                                vtok[:, fo * 512:(fo + 1) * 512], psA[b][:]),
                            after=(tmm,), token=True))
                        psg[bank] = vt[-1]
                    pd[(3, 0)] = vt[-2]
                    pd[(3, 1)] = vt[-1]
                elif s6 == 4:
                    for dc in range(ND):
                        if dc == 0:
                            MM("A0", lambda e: e.matmul(
                                psA[0][0:64, 0:SCN], wa1[:, 0:64],
                                chs(xs, 0), start=True, stop=False),
                                after=XST, token=False)
                        else:
                            ph.op("tensor", lambda e, dc=dc: e.matmul(
                                psA[0][0:64, 0:SCN],
                                wa1[:, dc * 64:(dc + 1) * 64],
                                chs(xs, dc), start=False,
                                stop=(dc == ND - 1)))
                    tmm = ph.tok_last("tensor")
                    tt1 = ph.op("scalar", lambda e: e.activation(
                        tnh[:], psA[0][0:64, 0:SCN], AF.Copy), after=(tmm,),
                        token=True)
                    psg["A0"] = tt1
                    agt = []
                    for do in range(ND):
                        b = do % 2
                        bank = f"B{b}"
                        tmm = MM(bank, lambda e, do=do, b=b: e.matmul(
                            psB[b][:, 0:SCN],
                            wa2[:, do * 128:(do + 1) * 128], tnh[:],
                            start=True, stop=True), after=(tt1,))
                        ts_ = ph.op("scalar", lambda e, b=b, do=do:
                                    e.activation(
                            chs(ag, do), psB[b][:, 0:SCN], AF.Sigmoid,
                            bias=vecs["time_aaaaa"][:, do:do + 1]),
                            after=(tmm,), token=True)
                        psg[bank] = ts_
                        agt.append(ph.op("scalar", lambda e, do=do: e.mul(
                            chs(ag, do), chs(ag, do), 2.0), token=True))
                    pd[(4, 0)] = agt[-2]
                    pd[(4, 1)] = agt[-1]
                else:
                    for dc in range(ND):
                        if dc == 0:
                            MM("A0", lambda e: e.matmul(
                                psA[0][:, 0:SCN], wg1[:, 0:128],
                                chs(xs, 0), start=True, stop=False),
                                after=XST, token=False)
                        else:
                            ph.op("tensor", lambda e, dc=dc: e.matmul(
                                psA[0][:, 0:SCN],
                                wg1[:, dc * 128:(dc + 1) * 128],
                                chs(xs, dc), start=False,
                                stop=(dc == ND - 1)))
                    tmm = ph.tok_last("tensor")
                    tt1 = ph.op("scalar", lambda e: e.activation(
                        tnh128[:], psA[0][:, 0:SCN], AF.Tanh), after=(tmm,),
                        token=True)
                    psg["A0"] = tt1
                    ggt = []
                    for do in range(ND):
                        b = do % 2
                        bank = f"B{b}"
                        tmm = MM(bank, lambda e, do=do, b=b: e.matmul(
                            psB[b][:, 0:SCN],
                            wg2[:, do * 128:(do + 1) * 128], tnh128[:],
                            start=True, stop=True), after=(tt1,))
                        ggt.append(ph.op("vector", lambda e, b=b, do=do:
                                         e.tensor_copy(
                            chs(gT, do), psB[b][:, 0:SCN]), after=(tmm,),
                            token=True))
                        psg[bank] = ggt[-1]
                    pd[(5, 0)] = ggt[-2]
                    pd[(5, 1)] = ggt[-1]
                xs_guard = pd[(s6, 1)]

            kknt = []
            for co in range(CPS):
                fw = []
                for dc in range(ND):
                    b = dc % 2
                    bank = f"B{b}"
                    ttr = MM(bank, lambda e, dc=dc, co=co, b=b: e.transpose(
                        psB[b][:], chs(kkn, dc)[:, co * 128:(co + 1) * 128],
                        g["idf"][:]), after=pd[(2, "kk")] + pd[(2, "ks")])
                    tcp = ph.op("vector", lambda e, b=b, dc=dc, co=co:
                                e.tensor_copy(
                        kkn_tok[:, co * D + dc * 128:co * D + (dc + 1) * 128],
                        psB[b][:]), after=(ttr,) + CG, token=True)
                    psg[bank] = tcp
                    fw.append(tcp)
                tq = ph.op("scalar", lambda e, co=co: e.activation(
                    ygb[:], kkn_tok[:, co * D:(co + 1) * D], AF.Square,
                    accum_out=rsg[:]), after=tuple(fw) + CG, token=True)
                tq = ph.op("scalar", lambda e: e.activation(
                    rsg[:], rsg[:], AF.Ln), after=(tq,), force=True)
                tq = ph.op("scalar", lambda e: e.activation(
                    rsg[:], rsg[:], AF.Exp, scale=-0.5), token=True)
                tq2 = ph.op("vector", lambda e: e.tensor_scalar(
                    rsg[:], rsg[:], 1e12, None, op0=ALU.min), after=(tq,))
                tq2 = ph.op("vector", lambda e, co=co: e.tensor_scalar(
                    kkn_tok[:, co * D:(co + 1) * D],
                    kkn_tok[:, co * D:(co + 1) * D], rsg[:], None,
                    op0=ALU.mult), token=True)
                for dc in range(ND):
                    b = dc % 2
                    bank = f"B{b}"
                    ttr = MM(bank, lambda e, b=b, dc=dc, co=co: e.transpose(
                        psB[b][:],
                        kkn_tok[:, co * D + dc * 128:co * D + (dc + 1) * 128],
                        g["idf"][:]), after=(tq2,))
                    tcp = ph.op("vector", lambda e, b=b, dc=dc, co=co:
                                e.tensor_copy(
                        chs(kkn, dc)[:, co * 128:(co + 1) * 128], psB[b][:]),
                        after=(ttr,), token=True)
                    psg[bank] = tcp
                    kknt.append(tcp)
            KKNT = tuple(kknt)

            for co in range(CPS):
                ssb_t = ph.op("scalar", lambda e: e.copy(SSb[:], SS[:]),
                              after=(ssb_prev,) + CG, token=True)
                pcht = []
                for dc in range(ND):
                    pcht.append(ph.op("scalar", lambda e, dc=dc, co=co:
                                      e.activation(
                        p_ch[:, dc * 128:(dc + 1) * 128],
                        chs(wraw, dc)[:, co * 128:(co + 1) * 128],
                        AF.Exp), after=(pd[(1, 0)], pd[(1, 1)]) + CG,
                        token=True))
                ptt = []
                trg2 = [None, None]
                for dc in range(ND):
                    b = dc % 2
                    bank = f"B{b}"
                    ttr = MM(bank, lambda e, b=b, dc=dc: e.transpose(
                        psB[b][:], p_ch[:, dc * 128:(dc + 1) * 128],
                        g["idf"][:]), after=(pcht[dc],))
                    tcp = ph.op("vector", lambda e, b=b, dc=dc:
                                e.tensor_copy(
                        p_tok[:, dc * 128:(dc + 1) * 128], psB[b][:]),
                        after=(ttr,) + CG, token=True)
                    psg[bank] = tcp
                    ptt.append(tcp)
                pchT = []
                for dc in range(ND):
                    b = dc % 2
                    bank = f"B{b}"
                    sl = slice(dc * 128, (dc + 1) * 128)
                    tmm = MM(bank, lambda e, b=b, dc=dc: e.matmul(
                        psB[b][:], p_tok[:, dc * 128:(dc + 1) * 128],
                        g["tri_incl"][:], start=True, stop=True),
                        after=tuple(ptt))
                    ph.op("scalar", lambda e, b=b, sl=sl: e.activation(
                        E0[:, sl], psB[b][:], AF.Exp, scale=-1.0),
                        after=(tmm,) + CG)
                    ph.op("scalar", lambda e, b=b, sl=sl: e.activation(
                        E2[:, sl], psB[b][:], AF.Exp), token=False)
                    t3 = ph.op("scalar", lambda e, b=b, sl=sl: e.activation(
                        E1[:, sl], psB[b][:], AF.Copy, accum_out=None),
                        token=True)
                    tcp = ph.op("vector", lambda e, b=b, sl=sl:
                                e.tensor_tensor(
                        E1[:, sl], E1[:, sl], p_ch[:, sl], op=ALU.subtract),
                        after=(t3, pcht[dc]) + CG, token=True)
                    psg[bank] = tcp
                    pchT.append(tcp)
                t_e0 = ph.op("scalar", lambda e: e.nop(),
                             after=tuple(pchT), token=True)
                t_e2 = t_e0
                t_e1 = ph.op("scalar", lambda e: e.activation(
                    E1[:], E1[:], AF.Exp, scale=-1.0),
                    after=tuple(pchT) + CG, token=True)
                glt = []
                for dc in range(ND):
                    glt.append(ph.op("vector", lambda e, dc=dc:
                                     e.tensor_copy(
                        gl[:, dc:dc + 1],
                        E0[:, dc * 128 + 127:dc * 128 + 128]),
                        after=(t_e0,), token=True))
                t_ab = ph.op("vector", lambda e: e.scalar_tensor_tensor(
                    dview(Ab), dview(kkn), -1.0, dview(E1),
                    op0=ALU.mult, op1=ALU.mult), after=KKNT + (t_e1,) + CG,
                    token=True)
                t_rb = ph.op("vector", lambda e: e.tensor_tensor(
                    dview(Rb), dview(rT), dview(E0), op=ALU.mult),
                    after=(t_e0, pd[(0, 0)], pd[(0, 1)]) + CG, token=True)
                t_kcb = ph.op("vector", lambda e: e.tensor_tensor(
                    dview(Kcb), dview(ks), dview(E2), op=ALU.mult),
                    after=(t_e2,) + pd[(2, "ks")] + CG, token=True)
                t_tk = ph.op("vector", lambda e: e.tensor_tensor(
                    dview(p_tok), dview(kkn), dview(ag), op=ALU.mult),
                    after=KKNT + (t_e1, pd[(4, 0)], pd[(4, 1)]), token=True)
                t_bcb = ph.op("vector", lambda e: e.tensor_tensor(
                    Bcb[:], p_tok[:], E2[:], op=ALU.mult),
                    after=(t_tk, t_e2) + CG, token=True)
                t_vb = ph.op("scalar", lambda e, co=co: e.copy(
                    Vb[:], vtok[:, co * D:(co + 1) * D]),
                    after=(pd[(3, 0)], pd[(3, 1)]) + CG, token=True)
                ktbt = []
                for dc in range(ND):
                    b = dc % 2
                    bank = f"B{b}"
                    ttr = MM(bank, lambda e, b=b, dc=dc: e.matmul(
                        psB[b][:], Kcb[:, dc * 128:(dc + 1) * 128],
                        g["idb"][:], start=True, stop=True), after=(t_kcb,))
                    tcp = ph.op("vector", lambda e, b=b, dc=dc:
                                e.tensor_copy(
                        Ktb[:, dc * 128:(dc + 1) * 128], psB[b][:]),
                        after=(ttr,) + CG, token=True)
                    psg[bank] = tcp
                    ktbt.append(tcp)
                btbt = []
                for dc in range(ND):
                    b = dc % 2
                    bank = f"B{b}"
                    ttr = MM(bank, lambda e, b=b, dc=dc: e.matmul(
                        psB[b][:], Bcb[:, dc * 128:(dc + 1) * 128],
                        g["idb"][:], start=True, stop=True), after=(t_bcb,))
                    btbt.append(ph.op("vector", lambda e, b=b, dc=dc:
                                      e.tensor_copy(
                        Btb[:, dc * 128:(dc + 1) * 128], psB[b][:]),
                        after=(ttr,) + CG, token=True))
                    psg[bank] = btbt[-1]
                ARR = (t_ab, t_rb, t_kcb, t_bcb, t_vb, ssb_t) + tuple(ktbt) \
                    + tuple(btbt) + tuple(glt)

                ssdone = []
                gndone = []

                def head_gen(h):
                    slot = h % NSL
                    off = (h % 2) * 64
                    dct = h // 2
                    aKc = Kcb[off:off + 64, dct * 128:(dct + 1) * 128]
                    aAb = Ab[off:off + 64, dct * 128:(dct + 1) * 128]
                    aBc = Bcb[off:off + 64, dct * 128:(dct + 1) * 128]
                    aRb = Rb[off:off + 64, dct * 128:(dct + 1) * 128]
                    aSSb = SSb[off:off + 64, dct * 64:(dct + 1) * 64]
                    aSS = SS[off:off + 64, dct * 64:(dct + 1) * 64]
                    aVb = Vb[:, h * 64:(h + 1) * 64]
                    aKt = Ktb[:, h * 64:(h + 1) * 64]
                    aBt = Btb[:, h * 64:(h + 1) * 64]
                    agl = gl[off:off + 64, dct:dct + 1]
                    Gt, Wt, W2t, Xb = GtS[slot], WtS[slot], W2tS[slot], \
                        XbS[slot]
                    Ut, Mt = UtS[slot], MtS[slot]

                    kb, bp = nbank("1")
                    tg = MM(kb, lambda e, bp=bp: e.matmul(
                        bp, aKc, aAb, start=True, stop=True), after=ARR)
                    tge = ph.op("vector", lambda e, bp=bp: e.tensor_tensor(
                        Gt[:], bp, g["mask_su"][:], op=ALU.mult),
                        after=(tg,), token=True)
                    psg[kb] = tge
                    yield
                    kb, bp = nbank("1")
                    tu = MM(kb, lambda e, bp=bp: e.matmul(
                        bp, aBc, aAb, start=True, stop=True), after=ARR)
                    tue = ph.op("vector", lambda e, bp=bp: e.tensor_tensor(
                        Ut[0][:], bp, g["mask_su"][:], op=ALU.mult),
                        after=(tu,), token=True)
                    psg[kb] = tue
                    yield
                    kb, bp = nbank("1")
                    tm_ = MM(kb, lambda e, bp=bp: e.matmul(
                        bp, aAb, aBc, start=True, stop=True))
                    tme = ph.op("vector", lambda e, bp=bp: e.tensor_tensor(
                        Mt[0][:], bp, g["mask_sl"][:], op=ALU.mult),
                        after=(tm_,), token=True)
                    psg[kb] = tme
                    yield
                    pe = {0: (tue, tme)}
                    for kp in range(1, 7):
                        kb, bp = nbank("1")
                        tu_ = MM(kb, lambda e, kp=kp, bp=bp: e.matmul(
                            bp, Mt[kp - 1][:], Ut[(kp - 1) % 2][:],
                            start=True, stop=True), after=pe[kp - 1])
                        ue = ph.op("vector", lambda e, kp=kp, bp=bp:
                                   e.tensor_copy(
                            Ut[kp % 2][:], bp), after=(tu_,), token=True)
                        psg[kb] = ue
                        kb2, bp2 = nbank("1")
                        tm2 = MM(kb2, lambda e, kp=kp, bp2=bp2: e.matmul(
                            bp2, Ut[(kp - 1) % 2][:], Mt[kp - 1][:],
                            start=True, stop=True))
                        me = ph.op("scalar", lambda e, kp=kp, bp2=bp2: e.copy(
                            Mt[kp][:], bp2), after=(tm2,), token=True)
                        psg[kb2] = me
                        pe[kp] = (ue, me)
                        yield
                    kb, bp = nbank("2")
                    tx = MM(kb, lambda e, bp=bp: e.matmul(
                        bp, aAb, aSSb, start=True, stop=False),
                        after=(ssb_t, tge))
                    ph.op("tensor", lambda e, bp=bp: e.matmul(
                        bp, Gt[:], aVb, start=False, stop=True),
                        after=(t_vb,))
                    tnx = ph.tok_last("tensor")
                    txe = ph.op("scalar", lambda e, bp=bp: e.copy(Xb[:], bp),
                                after=(tnx,), token=True)
                    psg[kb] = txe
                    yield
                    for kp in range(7):
                        kb, bp = nbank("2")
                        MM(kb, lambda e, kp=kp, bp=bp: e.matmul(
                            bp, Mt[kp][:], Xb[:], start=True,
                            stop=False), after=(txe, pe[kp][1]), token=False)
                        ph.op("tensor", lambda e, bp=bp: e.matmul(
                            bp, g["idb"][:], Xb[:], start=False,
                            stop=True))
                        tnp = ph.tok_last("tensor")
                        txe = ph.op("scalar", lambda e, bp=bp: e.copy(
                            Xb[:], bp), after=(tnp,), token=True)
                        psg[kb] = txe
                        yield
                    kb, bp = nbank("1")
                    tw = MM(kb, lambda e, bp=bp: e.matmul(
                        bp, aKc, aRb, start=True, stop=True),
                        after=(t_rb, pe[6][0]))
                    twe = ph.op("vector", lambda e, bp=bp: e.tensor_tensor(
                        Wt[:], bp, g["mask_iu"][:], op=ALU.mult),
                        after=(tw,), token=True)
                    psg[kb] = twe
                    kb2, bp2 = nbank("1")
                    tw2 = MM(kb2, lambda e, bp2=bp2: e.matmul(
                        bp2, aBc, aRb, start=True, stop=True),
                        after=(pe[6][1],))
                    tw2e = ph.op("vector", lambda e, bp2=bp2: e.tensor_tensor(
                        W2t[:], bp2, g["mask_iu"][:], op=ALU.mult),
                        after=(tw2,), token=True)
                    psg[kb2] = tw2e
                    yield
                    kb, bp = nbank("2")
                    MM(kb, lambda e, bp=bp: e.matmul(
                        bp, aRb, aSSb, start=True, stop=False),
                        after=(txe, twe, tw2e), token=False)
                    ph.op("tensor", lambda e, bp=bp: e.matmul(
                        bp, Wt[:], aVb, start=False, stop=False))
                    ph.op("tensor", lambda e, bp=bp: e.matmul(
                        bp, W2t[:], Xb[:], start=False, stop=True))
                    tny = ph.tok_last("tensor")
                    tye = ph.op("scalar", lambda e, co=co, h=h, bp=bp: e.copy(
                        y_sc[:, co * D + h * 64:co * D + (h + 1) * 64],
                        bp), after=(tny,) + CG, token=True)
                    psg[kb] = tye
                    yield
                    kb, bp = nbank("3")
                    MM(kb, lambda e, bp=bp: e.matmul(
                        bp, aKt, aVb, start=True,
                        stop=False), after=(tye,), token=False)
                    ph.op("tensor", lambda e, bp=bp: e.matmul(
                        bp, aBt, Xb[:], start=False,
                        stop=True))
                    tns = ph.tok_last("tensor")
                    tse = ph.op("vector", lambda e, bp=bp: e.tensor_tensor(
                        aSS, bp, aSS, op=ALU.add),
                        after=(tns,), token=True)
                    tse = ph.op("vector", lambda e: e.tensor_scalar(
                        aSS, aSS, agl, None, op0=ALU.mult), token=True)
                    psg[kb] = tse
                    ssdone.append(tse)
                    yield
                    ys = y_sc[:, co * D + h * 64:co * D + (h + 1) * 64]
                    mv, rs = mvgS[slot], rsgS[slot]
                    t1_ = ph.op("vector", lambda e: e.bn_stats(
                        mv[:, 0:6], ys), after=(tye,), token=True)
                    t1_ = ph.op("vector", lambda e: e.bn_aggr(
                        mv[:, 6:8], mv[:, 0:6]), after=(t1_,), force=True,
                        token=True)
                    t2_ = ph.op("scalar", lambda e: e.activation(
                        rs[:], mv[:, 7:8], AF.Ln, bias=GN_EPS),
                        after=(t1_,))
                    t2_ = ph.op("scalar", lambda e: e.activation(
                        rs[:], rs[:], AF.Exp, scale=-0.5), token=True)
                    gndone.append(ph.op("vector", lambda e: e.tensor_scalar(
                        ys, ys, mv[:, 6:7], rs[:], op0=ALU.subtract,
                        op1=ALU.mult), after=(t2_, t1_), force=True,
                        token=True))

                for h0 in range(0, H, NSL):
                    alive = [head_gen(h) for h in range(h0, h0 + NSL)]
                    while alive:
                        nxt = []
                        for gen_ in alive:
                            try:
                                next(gen_)
                                nxt.append(gen_)
                            except StopIteration:
                                pass
                        alive = nxt
                SSD = tuple(ssdone)
                ssb_prev = SSD[-1]

                ysl = y_sc[:, co * D:(co + 1) * D]
                tgn = ph.op("vector", lambda e, ysl=ysl: e.tensor_tensor(
                    ysl, ysl, lnw[:], op=ALU.mult), after=tuple(gndone))
                tgn = ph.op("vector", lambda e, ysl=ysl: e.tensor_tensor(
                    ysl, ysl, lnb[:], op=ALU.add), token=True)
                t1_ = ph.op("vector", lambda e: e.tensor_tensor(
                    dview(E1), dview(rT), dview(ks), op=ALU.mult),
                    after=(pd[(0, 1)],) + pd[(2, "ks")] + tuple(btbt) +
                    (t_ab,), token=True)
                rft = []
                for dc in range(ND):
                    rft.append(ph.op("vector", lambda e, dc=dc:
                                     e.tensor_scalar(
                        Ab[:, dc * 128:(dc + 1) * 128],
                        E1[:, dc * 128:(dc + 1) * 128],
                        vecs["faaaa"][:, dc:dc + 1], None, op0=ALU.mult),
                        after=(t1_, SSD[-1]), token=True))
                for dc in range(ND):
                    if dc == 0:
                        MM("X", lambda e: e.matmul(
                            pX[:, 0:16], Ab[:, 0:128], hb[:, 0:H],
                            start=True, stop=False), after=tuple(rft),
                            token=False)
                    else:
                        ph.op("tensor", lambda e, dc=dc: e.matmul(
                            pX[:, 0:16], Ab[:, dc * 128:(dc + 1) * 128],
                            hb[:, dc * H:(dc + 1) * H], start=False,
                            stop=(dc == ND - 1)))
                tnb = ph.tok_last("tensor")
                tbs = ph.op("vector", lambda e: e.tensor_copy(
                    bscal[:], pX[:, 0:16]), after=(tnb,), token=True)
                psg["X"] = tbs
                ytk = []
                for hh in range(H):
                    ytk.append(ph.op("vector", lambda e, hh=hh, co=co:
                                     e.scalar_tensor_tensor(
                        y_sc[:, co * D + hh * 64:co * D + (hh + 1) * 64],
                        vtok[:, co * D + hh * 64:co * D + (hh + 1) * 64],
                        bscal[:, hh:hh + 1],
                        y_sc[:, co * D + hh * 64:co * D + (hh + 1) * 64],
                        op0=ALU.mult, op1=ALU.add), after=(tbs, tgn),
                        token=True))
                t_tapy = None
                if tap_y is not None:
                    t_tapy = ph.dma("gpsimd", tap_y[ci * 128:(ci + 1) * 128, :],
                                    ysl, after=tuple(ytk), stream="tap")
                ygt = []
                trg3 = [None, None]
                for dc in range(ND):
                    b = dc % 2
                    bank = f"B{b}"
                    ttr = MM(bank, lambda e, b=b, dc=dc, co=co: e.transpose(
                        psB[b][:],
                        y_sc[:, co * D + dc * 128:co * D + (dc + 1) * 128],
                        g["idf"][:]), after=tuple(ytk))
                    tcp = ph.op("vector", lambda e, b=b, dc=dc, co=co:
                                e.tensor_tensor(
                        ygb[:, dc * 128:(dc + 1) * 128], psB[b][:],
                        chs(gT, dc)[:, co * 128:(co + 1) * 128],
                        op=ALU.mult),
                        after=(ttr, pd[(5, 0)], pd[(5, 1)]), token=True)
                    psg[bank] = tcp
                    ygt.append(tcp)
                YGT = tuple(ygt)
                attk = []
                for fo in range(2):
                    b = fo % 2
                    bank = "A" + str(b)
                    for dc in range(ND):
                        if dc == 0:
                            MM(bank, lambda e, fo=fo, b=b: e.matmul(
                                psA[b][:], ygb[:, 0:128],
                                wo[:, fo * 512:(fo + 1) * 512],
                                start=True, stop=False),
                                after=YGT + (prev_sc,), token=False)
                        else:
                            ph.op("tensor", lambda e, b=b, dc=dc, fo=fo:
                                  e.matmul(
                                psA[b][:], ygb[:, dc * 128:(dc + 1) * 128],
                                wo[:, dc * D + fo * 512:
                                   dc * D + (fo + 1) * 512],
                                start=False, stop=(dc == ND - 1)))
                    tnm = ph.tok_last("tensor")
                    tcp = ph.op("vector", lambda e, b=b, fo=fo, co=co:
                                e.tensor_copy(
                        y_sc[:, co * D + fo * 512:co * D + (fo + 1) * 512],
                        psA[b][:]), after=(tnm, prev_sc, t_tapy), token=True)
                    psg[bank] = tcp
                    attk.append(tcp)
                std = ph.dma("gpsimd",
                             dram["att1"][ci * 128:(ci + 1) * 128, :],
                             ysl, after=tuple(attk), stream="at")
                if tap_att1 is not None:
                    std = ph.dma("gpsimd", tap_att1[ci * 128:(ci + 1) * 128, :],
                                 ysl, after=tuple(attk), stream="at")
                prev_sc = std
                CG = (SSD[-1], std, tbs, tgn, attk[-1])
        ph.emit()


def kernel(**inputs):
    shared, percore = prep_inputs(inputs)
    cn = make_consts()
    base = {("c_" + k): np.asarray(v) for k, v in cn.items()}
    base.update(shared)
    nc = build_program()
    in_maps = []
    for b in range(8):
        m = dict(base)
        m["x"] = percore[b]
        in_maps.append(m)
    res = run_bass_kernel_spmd(nc, in_maps, core_ids=list(range(8)))
    return np.stack([r["out"] for r in res.results], axis=0)
